# version 20
# baseline (speedup 1.0000x reference)
import numpy as np
import contextlib
import os
import concourse.bass as bass
import concourse.mybir as mybir
from concourse.bass_utils import run_bass_kernel_spmd

F32 = mybir.dt.float32
BF16 = mybir.dt.bfloat16
I32 = mybir.dt.int32
AF = mybir.ActivationFunctionType
ALU = mybir.AluOpType
AX = mybir.AxisListType

B, N, D = 2, 8192, 1024
NCORES = 8
TOK = 2048
HB = 8
TB = TOK + 2 * HB
HEADS, HD, FEAT = 8, 128, 64
DH = 4096
QT = 122
QS0, NQ = 7, 2050
FT = 512
HBW = 520
EPS_LN, EPS_Z = 1e-5, 1e-8

ENG = ("sync", "vector", "scalar", "tensor", "gpsimd")
BUILD_LIMIT = int(os.environ.get("KBISECT", "0")) or None
PHASE_MARKS = []


class Sched:

    def __init__(self, nc, sems):
        self.nc = nc
        self.sems = sems
        self.ops = {e: [] for e in ENG}
        self.cnt = {e: 0 for e in ENG}
        self.last = None
        self.nsteps = 0
        self.auto_pump = None
        self._pumping = False

    def step(self, eng, builder, dma=False, after=None, extra_waits=(), sem_key=None):
        self.nsteps += 1
        if BUILD_LIMIT is not None and self.nsteps > BUILD_LIMIT:
            return self.last
        if after is None:
            deps = [self.last] if self.last is not None else []
        elif isinstance(after, list):
            deps = list(after)
        else:
            deps = [after]
        deps = [d for d in deps if d is not None]
        deps += list(extra_waits)
        per = {}
        for (s, v) in deps:
            if s == eng:
                continue
            per[s] = max(per.get(s, 0), v)
        waits = sorted(per.items())
        key = sem_key if (dma and sem_key is not None) else eng
        sem = self.sems[key]
        if dma:
            nd = getattr(builder, "n_dma")
            self.cnt[key] += 16 * nd
            tok = (key, self.cnt[key])

            def run(e, waits=tuple(waits), builder=builder, nd=nd):
                for (s, v) in waits:
                    e.wait_ge(self.sems[s], v)
                insts = builder(e)
                if not isinstance(insts, (list, tuple)):
                    insts = [insts]
                assert len(insts) == nd
                for i in insts:
                    i.then_inc(sem, 16)

            self.ops[eng].append(run)
        else:
            self.cnt[eng] += 1
            tok = (eng, self.cnt[eng])

            def run(e, waits=tuple(waits), builder=builder, eng=eng):
                for (s, v) in waits:
                    e.wait_ge(self.sems[s], v)
                insts = builder(e)
                if not isinstance(insts, (list, tuple)):
                    insts = [insts]
                if eng in ("vector", "scalar", "gpsimd"):
                    e.drain().then_inc(sem, 1)
                else:
                    insts[-1].then_inc(sem, 1)

            self.ops[eng].append(run)
        self.last = tok
        if self.auto_pump is not None and not self._pumping:
            self.auto_pump()
        return tok


def dma_step(n):
    def deco(fn):
        fn.n_dma = n
        return fn
    return deco


def build_program():
    nc = bass.Bass("TRN2", target_bir_lowering=False)

    xT_h = nc.dram_tensor("xT", [D, TB], F32, kind="ExternalInput").ap()
    wqkv_h = nc.dram_tensor("wqkv", [D, 3 * D], F32, kind="ExternalInput").ap()
    proj_h = nc.dram_tensor("proj", [HD, FEAT], F32, kind="ExternalInput").ap()
    wout_h = nc.dram_tensor("wout", [D, D], F32, kind="ExternalInput").ap()
    bout_h = nc.dram_tensor("boutrow", [1, D], F32, kind="ExternalInput").ap()
    w1T_h = nc.dram_tensor("w1T", [D, DH], F32, kind="ExternalInput").ap()
    w2T_h = nc.dram_tensor("w2T", [DH, D], F32, kind="ExternalInput").ap()
    convd_h = nc.dram_tensor("convdiag", [128, 96 * 128], F32, kind="ExternalInput").ap()
    convb_h = nc.dram_tensor("convbH", [128, 32], F32, kind="ExternalInput").ap()
    ident_h = nc.dram_tensor("identH", [128, 128], F32, kind="ExternalInput").ap()
    bmask_h = nc.dram_tensor("bandmask", [128, 1024], F32, kind="ExternalInput").ap()
    bmask2_h = nc.dram_tensor("bandmask2", [128, 1024], F32, kind="ExternalInput").ap()
    kmask_h = nc.dram_tensor("kmaskH", [128, 16], F32, kind="ExternalInput").ap()
    qmask_h = nc.dram_tensor("qmaskH", [128, 2], F32, kind="ExternalInput").ap()
    outT_h = nc.dram_tensor("outT", [D, TOK], F32, kind="ExternalOutput").ap()
    _skind = "ExternalOutput" if os.environ.get("KDEBUG") else "Internal"
    xln1_s = nc.dram_tensor("xln1_s", [D, NQ], F32, kind=_skind).ap()
    xq_s = nc.dram_tensor("xq_s", [D, NQ], BF16, kind=_skind).ap()
    w1q_s = nc.dram_tensor("w1q_s", [D, DH], BF16, kind=_skind).ap()
    w2q_s = nc.dram_tensor("w2q_s", [DH, D], BF16, kind=_skind).ap()

    ctx = contextlib.ExitStack()
    with ctx:
        def sb(name, shape, dt):
            return ctx.enter_context(nc.sbuf_tensor(name, shape, dt))

        def psum(name, shape):
            return ctx.enter_context(nc.psum_tensor(name, shape, F32))

        ident = sb("ident", [128, 128], F32)
        onesb = sb("onesb", [128, 1024], BF16)
        onesf = sb("onesf", [128, 128], F32)
        eps128 = sb("eps128", [128, 128], BF16)
        c32 = sb("c32", [32, 520], BF16)
        pp = sb("pp", [128, 192], BF16)
        boutpad = sb("boutpad", [32, 1024], BF16)
        minus1 = sb("minus1", [128, 1], F32)
        c1024 = sb("c1024", [128, 1], F32)
        rows_t = sb("rows_t", [1, 2048], F32)
        srcrow = sb("srcrow", [128, 512], F32)
        cols_t = sb("cols_t", [128, 16], F32)
        mc1col = sb("mc1col", [128, 1], F32)
        mc2col = sb("mc2col", [128, 1], F32)
        ws1col = sb("ws1col", [128, 1], F32)
        ws2col = sb("ws2col", [128, 1], F32)
        wtot = sb("wtot", [1, 8], F32)
        kmask = sb("kmask", [128, 16], F32)
        qmask = sb("qmask", [128, 2], F32)

        pA = psum("pA", [128, 512])
        pB = psum("pB", [128, 512])
        pC = psum("pC", [128, 2048])
        ps3 = psum("ps3", [128, 512])
        ps4 = psum("ps4", [128, 512])

        sems = {}
        for e in ENG + ("wq0", "wq1", "wq2", "wq3", "fx", "fw0", "fw1", "fw2", "gw0", "gw1", "gw2", "ax"):
            sems[e] = ctx.enter_context(nc.semaphore("sem_" + e))
        S = Sched(nc, sems)
        for e in ("wq0", "wq1", "wq2", "wq3", "fx", "fw0", "fw1", "fw2", "gw0", "gw1", "gw2", "ax"):
            S.cnt[e] = 0

        @dma_step(4)
        def _(e):
            return [
                e.dma_start(out=ident[:], in_=ident_h[:]),
                e.dma_start(out=kmask[:], in_=kmask_h[:]),
                e.dma_start(out=qmask[:], in_=qmask_h[:]),
                e.dma_start(out=srcrow[:, 0:64], in_=proj_h[:]),
            ]
        S.step("sync", _, dma=True)

        def _(e):
            return [
                e.memset(onesb[:], 1.0),
                e.memset(onesf[:], 1.0),
                e.memset(eps128[:], EPS_Z / 128.0),
                e.memset(c32[:], 1.0 / 32.0),
                e.memset(pp[:], 0.0),
                e.memset(boutpad[:], 0.0),
                e.memset(minus1[:], -1.0),
                e.memset(c1024[:], 1.0 / 1024.0),
                e.memset(rows_t[:], 0.0),
                e.memset(cols_t[:], 0.0),
                e.drain(),
                e.tensor_copy(out=pp[:, 0:64], in_=srcrow[:, 0:64]),
                e.tensor_copy(out=pp[:, 128:192], in_=srcrow[:, 0:64]),
                e.drain(),
                e.memset(srcrow[:], 0.0),
            ]
        S.step("vector", _)

        PHASE_MARKS.append(("P0", S.nsteps))

        wq_done = [None, None]
        wq_mean = [None, None]

        def wquant_gen(wT_h, wq_s, n_iters, width, acc_base, mccol, wscol, slot,
                       rows128, colw, gidx):
            stg = [stageW, stageW2, stageW3, stageW4]
            tok_dma = [None] * n_iters
            tok_abs = [None] * n_iters

            def src_ap(it):
                rb, cb = it // colw, it % colw
                return wT_h[rb * 128:(rb + 1) * 128, cb * width:(cb + 1) * width]

            def dst_ap(it):
                rb, cb = it // colw, it % colw
                return wq_s[rb * 128:(rb + 1) * 128, cb * width:(cb + 1) * width]

            for it in range(n_iters):
                @dma_step(1)
                def _(e, it=it):
                    return e.dma_start(out=stg[it % 4][:, 0:width], in_=src_ap(it))
                if it >= 4:
                    dep = [tok_abs[it - 4]]
                elif gidx == 1:
                    dep = [wq_done[0]]
                else:
                    dep = [S.last]
                tok_dma[it] = S.step("sync", _, dma=True, after=dep,
                                     sem_key="wq%d" % (it % 4))

                def _(e, it=it):
                    return e.activation(out=wtmp[:, 0:width], in_=stg[it % 4][:, 0:width],
                                        func=AF.Abs, accum_out=wacc[:, acc_base + it: acc_base + it + 1])
                yield
                tok_abs[it] = S.step("scalar", _, after=[tok_dma[it]])
                yield

            def _(e):
                return e.reduce_sum(out=cols_t[:, 8:9], in_=wacc[:, acc_base:acc_base + n_iters], axis=AX.X)
            S.step("vector", _, after=[tok_abs[n_iters - 1]])

            def _(e):
                return e.transpose(out=ps3[0:1, 0:128], in_=cols_t[:, 8:9], identity=ident[:])
            S.step("tensor", _, after=[S.last])

            def _(e):
                return [
                    e.reduce_sum(out=wtot[0:1, slot:slot + 1], in_=ps3[0:1, 0:128], axis=AX.X),
                    e.drain(),
                    e.tensor_scalar(out=wtot[0:1, slot + 1:slot + 2], in0=wtot[0:1, slot:slot + 1],
                                    scalar1=1.0 / (n_iters * 128.0 * width), scalar2=None,
                                    op0=ALU.mult),
                    e.drain(),
                    e.tensor_scalar(out=wtot[0:1, slot + 1:slot + 2], in0=wtot[0:1, slot + 1:slot + 2],
                                    scalar1=1e-5, scalar2=None, op0=ALU.max),
                    e.drain(),
                    e.reciprocal(out=wtot[0:1, slot + 2:slot + 3], in_=wtot[0:1, slot + 1:slot + 2]),
                    e.drain(),
                    e.tensor_copy(out=srcrow[0:1, 0:1], in_=wtot[0:1, slot + 1:slot + 2]),
                    e.tensor_copy(out=srcrow[0:1, 1:2], in_=wtot[0:1, slot + 2:slot + 3]),
                ]
            S.step("vector", _, after=[S.last])

            def _(e):
                return e.matmul(out=ps4[:, 0:2], lhsT=onesf[:], rhs=srcrow[:, 0:2],
                                start=True, stop=True)
            S.step("tensor", _, after=[S.last])

            def _(e):
                return [
                    e.tensor_copy(out=mccol[:], in_=ps4[:, 0:1]),
                    e.tensor_copy(out=wscol[:], in_=ps4[:, 1:2]),
                ]
            tok_sc = S.step("vector", _, after=[S.last])
            wq_mean[gidx] = tok_sc
            yield

            tok_v = [None] * n_iters
            tok_s = [None] * n_iters
            tok_g = [None] * n_iters
            for it in range(n_iters):
                @dma_step(1)
                def _(e, it=it):
                    return e.dma_start(out=stg[it % 4][:, 0:width], in_=src_ap(it))
                tok_dma[it] = S.step("sync", _, dma=True,
                                     after=[tok_v[it - 4]] if it >= 4 else [tok_sc],
                                     sem_key="wq%d" % (it % 4))
                yield

                def _(e, it=it):
                    return e.activation(out=wtmp[:, 0:width], in_=stg[it % 4][:, 0:width],
                                        func=AF.Identity, scale=wscol[:])
                tok_sa = S.step("scalar", _,
                                after=[tok_dma[it], tok_sc] + ([tok_v[it - 1]] if it >= 1 else []))

                def _(e, it=it):
                    return e.tensor_scalar(out=w32[:, 0:width], in0=wtmp[:, 0:width],
                                           scalar1=1.0, scalar2=-1.0,
                                           op0=ALU.min, op1=ALU.max)
                tok_v[it] = S.step("vector", _,
                                   after=[tok_sa] + ([tok_s[it - 1]] if it >= 1 else []))

                def _(e, it=it):
                    return e.copy(out=wqout[:, 0:width], in_=w32[:, 0:width])
                tok_s[it] = S.step("scalar", _,
                                   after=[tok_v[it]] + ([tok_g[it - 1]] if it >= 1 else []))

                @dma_step(1)
                def _(e, it=it):
                    return e.dma_start(out=dst_ap(it), in_=wqout[:, 0:width])
                tok_g[it] = S.step("gpsimd", _, dma=True, after=[tok_s[it]])
                yield
            wq_done[gidx] = tok_g[n_iters - 1]

        wq_gens = [None, None]

        def pump_wq(n):
            saved = S.last
            S._pumping = True
            for _ in range(n):
                for gidx in range(2):
                    if wq_gens[gidx] is not None:
                        try:
                            next(wq_gens[gidx])
                            break
                        except StopIteration:
                            wq_gens[gidx] = None
            S._pumping = False
            S.last = saved

        def drain_wq():
            saved = S.last
            S._pumping = True
            for gidx in range(2):
                if wq_gens[gidx] is not None:
                    for _ in wq_gens[gidx]:
                        pass
                    wq_gens[gidx] = None
            S._pumping = False
            S.last = saved

            def _(e):
                return e.engine_nop()
            S.step("vector", _, after=[saved] + [t for t in wq_done if t is not None])

        PHASE_MARKS.append(("P1-wquant", S.nsteps))

        with contextlib.ExitStack() as actx:
            x_bf = actx.enter_context(nc.sbuf_tensor("x_bf", [128, 8 * TB], BF16))
            qphi = actx.enter_context(nc.sbuf_tensor("qphi", [128, 4 * TB], BF16))
            kphi = actx.enter_context(nc.sbuf_tensor("kphi", [128, 4 * TB], BF16))
            xres = actx.enter_context(nc.sbuf_tensor("xres", [128, 8 * 528], F32))
            stage = actx.enter_context(nc.sbuf_tensor("stageA", [128, 1024], F32))
            wacc = actx.enter_context(nc.sbuf_tensor("wacc", [128, 80], F32))
            stageW = actx.enter_context(nc.sbuf_tensor("stageW", [128, 1024], F32))
            stageW2 = actx.enter_context(nc.sbuf_tensor("stageW2", [128, 1024], F32))
            stageW3 = actx.enter_context(nc.sbuf_tensor("stageW3", [128, 1024], F32))
            stageW4 = actx.enter_context(nc.sbuf_tensor("stageW4", [128, 1024], F32))
            wtmp = actx.enter_context(nc.sbuf_tensor("wtmp", [128, 1024], F32))
            w32 = actx.enter_context(nc.sbuf_tensor("w32", [128, 1024], I32))
            wqout = actx.enter_context(nc.sbuf_tensor("wqout", [128, 1024], BF16))
            wq_gens[0] = wquant_gen(w1T_h, w1q_s, 32, 1024, 0, mc1col, ws1col, 0, 8, 4, 0)
            wq_gens[1] = wquant_gen(w2T_h, w2q_s, 32, 1024, 32, mc2col, ws2col, 4, 32, 1, 1)
            bmask = actx.enter_context(nc.sbuf_tensor("bmask", [128, 1024], F32))
            bmask2 = actx.enter_context(nc.sbuf_tensor("bmask2", [128, 1024], F32))

            @dma_step(3)
            def _(e):
                return [e.dma_start(out=bmask[:], in_=bmask_h[:]),
                        e.dma_start(out=bmask2[:], in_=bmask2_h[:]),
                        e.dma_start(out=stage[0:1, 0:1024], in_=bout_h[:])]
            S.step("sync", _, dma=True)

            def _(e):
                return e.tensor_copy(out=boutpad[0:1, :], in_=stage[0:1, 0:1024])
            S.step("vector", _)

            with contextlib.ExitStack() as a1ctx:
                wqk = a1ctx.enter_context(nc.sbuf_tensor("wqk", [128, 8 * 2048], BF16))
                qtmp = a1ctx.enter_context(nc.sbuf_tensor("qtmp", [128, 8 * 512], BF16))
                for half in range(16):
                    @dma_step(1)
                    def _(e, half=half):
                        dc, hh = half // 2, half % 2
                        return e.dma_start(out=stage[:, 0:1024],
                                           in_=wqkv_h[dc * 128:(dc + 1) * 128, hh * 1024:(hh + 1) * 1024])
                    S.step("sync", _, dma=True)

                    def _(e, half=half):
                        return e.copy(out=wqk[:, half * 1024:(half + 1) * 1024], in_=stage[:, 0:1024])
                    S.step("scalar", _)

                S.auto_pump = lambda: pump_wq(3)
                tiles = [(t, min(512, TB - t)) for t in range(0, TB, 512)]
                for (ts, tl) in tiles:
                    @dma_step(1)
                    def _(e, ts=ts, tl=tl):
                        return e.dma_start(
                            out=xres[:].rearrange("p (c t) -> p c t", c=8)[:, :, 0:tl],
                            in_=xT_h.rearrange("(c p) t -> p c t", p=128)[:, :, ts:ts + tl])
                    S.step("sync", _, dma=True)

                    def _(e, ts=ts, tl=tl):
                        return e.tensor_copy(
                            out=x_bf[:].rearrange("p (c t) -> p c t", c=8)[:, :, ts:ts + tl],
                            in_=xres[:].rearrange("p (c t) -> p c t", c=8)[:, :, 0:tl])
                    S.step("vector", _)

                    for (wbase, phidst) in ((0, qphi), (1024, kphi)):
                        for hb in range(2):
                            def _(e, hb=hb, wbase=wbase, ts=ts, tl=tl):
                                mms = []
                                for hl in range(4):
                                    h = hb * 4 + hl
                                    for dc in range(8):
                                        mms.append(e.matmul(
                                            out=pC[:, hl * 512: hl * 512 + tl],
                                            lhsT=wqk[:, dc * 2048 + wbase + h * 128: dc * 2048 + wbase + (h + 1) * 128],
                                            rhs=x_bf[:, dc * TB + ts: dc * TB + ts + tl],
                                            start=(dc == 0), stop=(dc == 7)))
                                return mms
                            S.step("tensor", _)

                            def _(e, hb=hb, tl=tl):
                                return [e.copy(out=qtmp[:, (hb * 4 + hl) * 512: (hb * 4 + hl) * 512 + tl],
                                               in_=pC[:, hl * 512: hl * 512 + tl])
                                        for hl in range(4)]
                            S.step("scalar", _)

                        def _(e, ts=ts, tl=tl):
                            mms = []
                            for hp in range(4):
                                mms.append(e.matmul(out=pC[:, hp * 512: hp * 512 + tl], lhsT=pp[:, 0:128],
                                                    rhs=qtmp[:, hp * 512: hp * 512 + tl],
                                                    start=True, stop=False))
                                mms.append(e.matmul(out=pC[:, hp * 512: hp * 512 + tl], lhsT=pp[:, 64:192],
                                                    rhs=qtmp[:, (hp + 4) * 512: (hp + 4) * 512 + tl],
                                                    start=False, stop=False))
                                mms.append(e.matmul(out=pC[:, hp * 512: hp * 512 + tl],
                                                    lhsT=onesb[0:32, 0:128],
                                                    rhs=c32[0:32, 0:tl], start=False, stop=True))
                            return mms
                        S.step("tensor", _)

                        def _(e, tl=tl):
                            return [e.activation(out=xres[:, hp * 1056: hp * 1056 + tl],
                                                 in_=pC[:, hp * 512: hp * 512 + tl],
                                                 func=AF.Exp, bias=minus1[:], scale=1.0)
                                    for hp in range(4)]
                        S.step("scalar", _)

                        def _(e, ts=ts, tl=tl, phidst=phidst):
                            return [e.scalar_tensor_tensor(
                                out=phidst[:, hp * TB + ts: hp * TB + ts + tl],
                                in0=xres[:, hp * 1056: hp * 1056 + tl], scalar=1.0,
                                in1=pC[:, hp * 512: hp * 512 + tl],
                                op0=ALU.min, op1=ALU.max) for hp in range(4)]
                        S.step("vector", _)

                def _(e):
                    km3 = kphi[:].rearrange("p (c t) -> p c t", c=4)
                    return [
                        e.tensor_tensor(out=km3[:, :, 0:HB], in0=km3[:, :, 0:HB],
                                        in1=kmask[:, None, 0:8].broadcast_to([128, 4, 8]),
                                        op=ALU.mult),
                        e.tensor_tensor(out=km3[:, :, TB - HB:TB], in0=km3[:, :, TB - HB:TB],
                                        in1=kmask[:, None, 8:16].broadcast_to([128, 4, 8]),
                                        op=ALU.mult),
                    ]
                S.step("vector", _)
                while wq_mean[1] is None and wq_gens[1] is not None:
                    pump_wq(1)
                S.auto_pump = lambda: pump_wq(1)

                def _(e):
                    return e.engine_nop()
                S.step("vector", _,
                       after=[S.last] + [t for t in wq_mean if t is not None])

            PHASE_MARKS.append(("A1", S.nsteps))

            with contextlib.ExitStack() as a2ctx:
                def a2sb(name, shape, dt):
                    return a2ctx.enter_context(nc.sbuf_tensor(name, shape, dt))
                wv = a2sb("wv", [128, 8 * 1024], BF16)
                woutb = a2sb("woutb", [128, 8 * 1024], BF16)
                v_sb = a2sb("v_sb", [128, 1024], BF16)
                Sm = a2sb("Sm", [128, 1024], BF16)
                zinv = a2sb("zinv", [128, 1024], F32)
                attn_cp = a2sb("attn_cp", [128, 1024], F32)
                attn_sb = a2sb("attn_sb", [128, 1024], BF16)
                vres = a2sb("vres", [128, 976], F32)
                vbf = a2sb("vbf", [128, 976], BF16)
                v2bf = a2sb("v2bf", [128, 976], BF16)
                xln1t = a2sb("xln1t", [128, 976], F32)
                tq = a2sb("tq", [128, 976], F32)
                tq32 = a2sb("tq32", [128, 976], I32)
                xqt = a2sb("xqt", [128, 976], BF16)
                Mx = a2sb("Mx", [128, 976], F32)

                for (dst, src_h, cbase) in ((wv, wqkv_h, 2048), (woutb, wout_h, 0)):
                    for dc in range(8):
                        @dma_step(1)
                        def _(e, dc=dc, src_h=src_h, cbase=cbase):
                            return e.dma_start(out=stage[:, 0:1024],
                                               in_=src_h[dc * 128:(dc + 1) * 128, cbase:cbase + 1024])
                        S.step("sync", _, dma=True)

                        def _(e, dc=dc, dst=dst):
                            return e.copy(out=dst[:, dc * 1024:(dc + 1) * 1024], in_=stage[:, 0:1024])
                        S.step("scalar", _)

                n_tiles = (NQ + QT - 1) // QT
                t_vres = t_recip = t_acp = t_quant = None
                for it in range(n_tiles):
                    qs = QS0 + it * QT
                    qn = min(QT, QS0 + NQ - qs)
                    hs = min(qs - 3, TB - 128)
                    mask = bmask if it < n_tiles - 1 else bmask2
                    sc = qs - QS0

                    @dma_step(1)
                    def _(e, qs=qs, qn=qn):
                        return e.dma_start(
                            out=xres[:].rearrange("p (c t) -> p c t", c=8)[:, :, 0:qn],
                            in_=xT_h.rearrange("(c p) t -> p c t", p=128)[:, :, qs:qs + qn])
                    t_xdma = S.step("sync", _, dma=True, sem_key="ax",
                                    after=[t_vres] if t_vres else None)

                    def _(e, hs=hs):
                        mms = []
                        for dc in range(8):
                            for hb in range(2):
                                mms.append(e.matmul(
                                    out=pC[:, hb * 512:(hb + 1) * 512],
                                    lhsT=x_bf[:, dc * TB + hs: dc * TB + hs + 128],
                                    rhs=wv[:, dc * 1024 + hb * 512: dc * 1024 + (hb + 1) * 512],
                                    start=(dc == 0), stop=(dc == 7)))
                        return mms
                    t_vmm = S.step("tensor", _, after=[t_vres] if t_vres else [S.last])

                    def _(e):
                        return e.tensor_copy(out=v_sb[:], in_=pC[:, 0:1024])
                    t_vevac = S.step("vector", _, after=[t_vmm])

                    def _(e, hs=hs, qs=qs, qn=qn):
                        mms = []
                        for h in range(HEADS):
                            par = (h // 4) * 64
                            blk = (h % 4) * TB
                            mms.append(e.matmul(
                                out=pC[:, 1024 + h * 128: 1024 + h * 128 + qn],
                                lhsT=kphi[par:par + 64, blk + hs: blk + hs + 128],
                                rhs=qphi[par:par + 64, blk + qs: blk + qs + qn],
                                start=(h % 4 == 0), stop=(h % 4 == 3),
                                tile_position=(par, 0)))
                        return mms
                    t_smm = S.step("tensor", _, after=[t_recip] if t_recip else [S.last])

                    def _(e, mask=mask):
                        return e.tensor_tensor(out=Sm[:], in0=pC[:, 1024:2048],
                                               in1=mask[:], op=ALU.mult)
                    t_mask = S.step("vector", _, after=[t_smm])

                    def _(e, qn=qn):
                        mms = []
                        for h in range(HEADS):
                            mms.append(e.matmul(
                                out=pC[:, 1024 + h * 128: 1024 + h * 128 + qn],
                                lhsT=onesb[:, 0:128], rhs=Sm[:, h * 128: h * 128 + qn],
                                start=(h % 4 == 0), stop=False))
                            mms.append(e.matmul(
                                out=pC[:, 1024 + h * 128: 1024 + h * 128 + qn],
                                lhsT=onesb[:, 0:128], rhs=eps128[:, 0:qn],
                                start=False, stop=(h % 4 == 3)))
                        return mms
                    t_zmm = S.step("tensor", _, after=[t_mask])

                    def _(e):
                        return e.reciprocal(out=zinv[:], in_=pC[:, 1024:2048])
                    t_recip = S.step("vector", _, after=[t_zmm])

                    def _(e, qn=qn):
                        mms = []
                        for h in range(HEADS):
                            dst = pA if h < 4 else pB
                            off = (h % 4) * 128
                            mms.append(e.matmul(
                                out=dst[:, off: off + qn],
                                lhsT=v_sb[:, h * 128:(h + 1) * 128],
                                rhs=Sm[:, h * 128: h * 128 + qn],
                                start=(h % 4 == 0), stop=(h % 4 == 3)))
                        return mms
                    t_amm = S.step("tensor", _,
                                   after=[t_mask, t_vevac] + ([t_acp] if t_acp else []))

                    def _(e):
                        return [e.copy(out=attn_cp[:, 0:512], in_=pA[:]),
                                e.copy(out=attn_cp[:, 512:1024], in_=pB[:])]
                    t_acp = S.step("scalar", _, after=[t_amm])

                    def _(e):
                        return e.tensor_tensor(out=attn_sb[:], in0=attn_cp[:],
                                               in1=zinv[:], op=ALU.mult)
                    t_zmul = S.step("vector", _, after=[t_acp])

                    def _(e, qn=qn):
                        mms = []
                        for h in range(HEADS):
                            for oc in range(8):
                                mms.append(e.matmul(
                                    out=pC[:, oc * 128: oc * 128 + qn],
                                    lhsT=woutb[:, h * 1024 + oc * 128: h * 1024 + (oc + 1) * 128],
                                    rhs=attn_sb[:, h * 128: h * 128 + qn],
                                    start=(h == 0 and oc % 4 == 0), stop=False))
                        for oc in range(8):
                            mms.append(e.matmul(
                                out=pC[:, oc * 128: oc * 128 + qn],
                                lhsT=boutpad[0:32, oc * 128:(oc + 1) * 128],
                                rhs=onesb[0:32, 0:qn],
                                start=False, stop=(oc % 4 == 3)))
                        return mms
                    t_oproj = S.step("tensor", _, after=[t_zmul, t_vevac])

                    def _(e, qn=qn):
                        return e.tensor_tensor(
                            out=vres[:].rearrange("p (c t) -> p c t", c=8)[:, :, 0:qn],
                            in0=pC[:, 0:1024].rearrange("p (c t) -> p c t", t=128)[:, :, 0:qn],
                            in1=xres[:].rearrange("p (c t) -> p c t", c=8)[:, :, 0:qn],
                            op=ALU.add)
                    t_vres = S.step("vector", _, after=[t_oproj, t_xdma])

                    def _(e, qn=qn):
                        return [
                            e.copy(out=vbf[:], in_=vres[:]),
                            e.activation(out=v2bf[:], in_=vres[:], func=AF.Square),
                        ]
                    t_cast = S.step("scalar", _, after=[t_vres])

                    def _(e, qn=qn):
                        mms = []
                        for dc in range(8):
                            mms.append(e.matmul(out=ps3[0:1, 0:qn], lhsT=onesb[:, 0:1],
                                                rhs=vbf[:, dc * QT: dc * QT + qn],
                                                start=(dc == 0), stop=False))
                        for dc in range(8):
                            mms.append(e.matmul(out=ps3[0:1, 128:128 + qn], lhsT=onesb[:, 0:1],
                                                rhs=v2bf[:, dc * QT: dc * QT + qn],
                                                start=False, stop=(dc == 7)))
                        return mms
                    t_stats = S.step("tensor", _, after=[t_cast])

                    def _(e, qn=qn):
                        return [
                            e.tensor_scalar(out=rows_t[0:1, 0:qn], in0=ps3[0:1, 0:qn],
                                            scalar1=1.0 / 1024.0, scalar2=None, op0=ALU.mult),
                            e.drain(),
                            e.tensor_tensor(out=rows_t[0:1, 512:512 + qn], in0=rows_t[0:1, 0:qn],
                                            in1=rows_t[0:1, 0:qn], op=ALU.mult),
                            e.drain(),
                            e.scalar_tensor_tensor(out=rows_t[0:1, 1024:1024 + qn],
                                                   in0=ps3[0:1, 128:128 + qn], scalar=1.0 / 1024.0,
                                                   in1=rows_t[0:1, 512:512 + qn],
                                                   op0=ALU.mult, op1=ALU.subtract),
                            e.drain(),
                            e.tensor_scalar(out=rows_t[0:1, 1024:1024 + qn],
                                            in0=rows_t[0:1, 1024:1024 + qn],
                                            scalar1=EPS_LN, scalar2=None, op0=ALU.add),
                            e.drain(),
                            e.reciprocal(out=rows_t[0:1, 1536:1536 + qn],
                                         in_=rows_t[0:1, 1024:1024 + qn]),
                        ]
                    t_chain = S.step("vector", _, after=[t_stats])

                    def _(e, qn=qn):
                        return e.activation(out=srcrow[0:1, 0:qn], in_=rows_t[0:1, 1536:1536 + qn],
                                            func=AF.Sqrt)
                    t_sqrt = S.step("scalar", _, after=[t_chain])

                    def _(e, qn=qn):
                        return e.tensor_tensor(out=srcrow[0:1, 128:128 + qn],
                                               in0=srcrow[0:1, 0:qn], in1=rows_t[0:1, 0:qn],
                                               op=ALU.mult)
                    t_mr = S.step("vector", _, after=[t_sqrt])

                    def _(e, qn=qn):
                        return [
                            e.matmul(out=ps4[:, 0:qn], lhsT=onesf[:], rhs=srcrow[:, 0:qn],
                                     start=True, stop=False),
                            e.matmul(out=ps4[:, 128:128 + qn], lhsT=onesf[:], rhs=srcrow[:, 128:128 + qn],
                                     start=False, stop=True),
                        ]
                    t_bc1 = S.step("tensor", _, after=[t_mr])

                    def _(e, qn=qn):
                        v3 = vres[:].rearrange("p (c t) -> p c t", c=8)[:, :, 0:qn]
                        x3 = xln1t[:].rearrange("p (c t) -> p c t", c=8)[:, :, 0:qn]
                        xin = xln1t[:].rearrange("p (c t) -> p t c", c=8)[:, 0:qn, :]
                        return [
                            e.tensor_tensor(out=x3, in0=v3,
                                            in1=ps4[:, None, 0:qn].broadcast_to([128, 8, qn]),
                                            op=ALU.mult),
                            e.drain(),
                            e.tensor_tensor(out=x3, in0=x3,
                                            in1=ps4[:, None, 128:128 + qn].broadcast_to([128, 8, qn]),
                                            op=ALU.subtract),
                            e.drain(),
                            e.tensor_reduce(
                                out=Mx[:, 0:qn].rearrange("p (t o) -> p t o", o=1),
                                in_=xin, axis=AX.X, op=ALU.max, apply_absolute_value=True),
                        ]
                    t_apply = S.step("vector", _, after=[t_bc1])

                    @dma_step(8)
                    def _(e, sc=sc, qn=qn):
                        return [e.dma_start(out=xln1_s[dc * 128:(dc + 1) * 128, sc:sc + qn],
                                            in_=xln1t[:, dc * QT: dc * QT + qn])
                                for dc in range(8)]
                    S.step("gpsimd", _, dma=True, after=[t_apply])

                    def _(e, qn=qn):
                        return e.transpose(out=ps4[0:qn, 384:512], in_=Mx[:, 0:qn], identity=ident[:])
                    t_mxtr = S.step("tensor", _, after=[t_apply])

                    def _(e, qn=qn):
                        return [
                            e.reduce_max(out=cols_t[0:qn, 0:1], in_=ps4[0:qn, 384:512], axis=AX.X),
                            e.drain(),
                            e.tensor_scalar(out=cols_t[0:qn, 1:2], in0=cols_t[0:qn, 0:1],
                                            scalar1=1e-5, scalar2=None, op0=ALU.max),
                            e.drain(),
                            e.tensor_scalar(out=cols_t[0:qn, 1:2], in0=cols_t[0:qn, 1:2],
                                            scalar1=1.0 / 127.0, scalar2=None, op0=ALU.mult),
                            e.drain(),
                            e.reciprocal(out=cols_t[0:qn, 2:3], in_=cols_t[0:qn, 1:2]),
                            e.drain(),
                            e.tensor_tensor(out=cols_t[0:qn, 3:4], in0=cols_t[0:qn, 1:2],
                                            in1=mc1col[0:qn, :], op=ALU.mult),
                        ]
                    t_cols = S.step("vector", _, after=[t_mxtr])

                    def _(e, qn=qn):
                        return [
                            e.transpose(out=ps3[0:1, 256:256 + qn], in_=cols_t[0:qn, 2:3],
                                        identity=ident[0:qn, 0:qn]),
                            e.transpose(out=ps3[0:1, 384:384 + qn], in_=cols_t[0:qn, 3:4],
                                        identity=ident[0:qn, 0:qn]),
                        ]
                    t_rowtr = S.step("tensor", _, after=[t_cols])

                    def _(e, qn=qn):
                        return [
                            e.tensor_copy(out=srcrow[0:1, 256:256 + qn], in_=ps3[0:1, 256:256 + qn]),
                            e.tensor_copy(out=srcrow[0:1, 384:384 + qn], in_=ps3[0:1, 384:384 + qn]),
                        ]
                    t_rowcp = S.step("vector", _, after=[t_rowtr])

                    def _(e, qn=qn):
                        return [
                            e.matmul(out=ps4[:, 256:256 + qn], lhsT=onesf[:],
                                     rhs=srcrow[:, 256:256 + qn], start=True, stop=False),
                            e.matmul(out=ps4[:, 384:384 + qn], lhsT=onesf[:],
                                     rhs=srcrow[:, 384:384 + qn], start=False, stop=True),
                        ]
                    t_bc2 = S.step("tensor", _, after=[t_rowcp])

                    def _(e, qn=qn, it=it):
                        x3 = xln1t[:].rearrange("p (c t) -> p c t", c=8)[:, :, 0:qn]
                        t3 = tq[:].rearrange("p (c t) -> p c t", c=8)[:, :, 0:qn]
                        q3 = xqt[:].rearrange("p (c t) -> p c t", c=8)[:, :, 0:qn]
                        ops = [
                            e.tensor_tensor(out=t3, in0=x3,
                                            in1=ps4[:, None, 256:256 + qn].broadcast_to([128, 8, qn]),
                                            op=ALU.mult),
                            e.drain(),
                            e.tensor_copy(out=tq32[:], in_=tq[:]),
                            e.drain(),
                            e.tensor_copy(out=tq[:], in_=tq32[:]),
                            e.drain(),
                            e.tensor_tensor(out=q3, in0=t3,
                                            in1=ps4[:, None, 384:384 + qn].broadcast_to([128, 8, qn]),
                                            op=ALU.mult),
                        ]
                        if it == 0 or it == n_tiles - 1:
                            col = 0 if it == 0 else qn - 1
                            midx = 0 if it == 0 else 1
                            qm = xqt[:].rearrange("p (c t) -> p c t", c=8)[:, :, col:col + 1]
                            ops.append(e.drain())
                            ops.append(e.tensor_tensor(
                                out=qm, in0=qm,
                                in1=qmask[:, None, midx:midx + 1].broadcast_to([128, 8, 1]),
                                op=ALU.mult))
                        return ops
                    t_quant = S.step("vector", _, after=[t_bc2])

                    @dma_step(8)
                    def _(e, sc=sc, qn=qn):
                        return [e.dma_start(out=xq_s[dc * 128:(dc + 1) * 128, sc:sc + qn],
                                            in_=xqt[:, dc * QT: dc * QT + qn])
                                for dc in range(8)]
                    t_qsp = S.step("gpsimd", _, dma=True, after=[t_quant])
                S.last = t_qsp

            PHASE_MARKS.append(("A2", S.nsteps))

        S.auto_pump = None
        drain_wq()

        with contextlib.ExitStack() as fctx:
            def fsb(name, shape, dt):
                return fctx.enter_context(nc.sbuf_tensor(name, shape, dt))
            convd = fsb("convd", [128, 96 * 128], BF16)
            convb = fsb("convb", [128, 32], F32)
            xqb = fsb("xqb", [128, 8 * 520], BF16)
            w1buf = fsb("w1buf", [128, 3 * 8 * 512], BF16)
            w2buf = fsb("w2buf", [128, 2 * 4 * 512], BF16)
            h_bf = fsb("h_bf", [128, 32 * HBW], BF16)
            h_silu = fsb("h_silu", [128, 32 * FT], BF16)
            M2 = fsb("M2", [128, 512], F32)
            tbuf = fsb("tbuf", [128, 4 * FT], F32)
            t32 = fsb("t32", [128, 4 * FT], I32)
            d2sb = fsb("d2sb", [128, 512], F32)
            xl1b = fsb("xl1b", [128, 8 * FT], F32)
            outst = fsb("outst", [128, 8 * FT], F32)
            vbf2 = fsb("vbf2", [128, 8 * FT], BF16)

            @dma_step(1)
            def _(e):
                return e.dma_start(out=convb[:], in_=convb_h[:])
            S.step("sync", _, dma=True)

            for part in range(6):
                @dma_step(1)
                def _(e, part=part):
                    return e.dma_start(out=tbuf[:, 0:2048],
                                       in_=convd_h[:, part * 2048:(part + 1) * 2048])
                S.step("sync", _, dma=True)

                def _(e, part=part):
                    return e.copy(out=convd[:, part * 2048:(part + 1) * 2048], in_=tbuf[:, 0:2048])
                S.step("scalar", _)

            tok_ln2_prev = None
            tok_w1dma = {}
            tok_w2dma = {}
            for jt in range(4):
                ts0 = 1 + jt * FT
                fn = FT
                hn = fn + 2

                @dma_step(2)
                def _(e, ts0=ts0, hn=hn, fn=fn):
                    return [
                        e.dma_start(
                            out=xqb[:].rearrange("p (c t) -> p c t", c=8)[:, :, 0:hn],
                            in_=xq_s.rearrange("(c p) t -> p c t", p=128)[:, :, ts0 - 1: ts0 - 1 + hn]),
                        e.dma_start(
                            out=xl1b[:].rearrange("p (c t) -> p c t", c=8),
                            in_=xln1_s.rearrange("(c p) t -> p c t", p=128)[:, :, ts0:ts0 + fn]),
                    ]
                tok_xq = S.step("sync", _, dma=True, sem_key="fx",
                                after=([tok_ln2_prev] if tok_ln2_prev else None))

                mm1_tok = [None] * 32
                evac_tok = [None] * 32
                conv_tok = [None] * 16
                silu_tok = [None] * 16

                def emit_mm1(k):
                    g = k // 4
                    if (k % 4) == 0:
                        @dma_step(1)
                        def _(e, g=g):
                            return e.dma_start(
                                out=w1buf[:, (g % 3) * 4096:(g % 3) * 4096 + 4096]
                                .rearrange("p (c t) -> p c t", c=8),
                                in_=w1q_s.rearrange("(c p) t -> p c t", p=128)[:, :, g * 512:(g + 1) * 512])
                        dep = [tok_xq]
                        if g >= 3:
                            dep.append(mm1_tok[4 * (g - 3) + 3])
                        tok_w1dma[g] = S.step("sync", _, dma=True, after=dep,
                                              sem_key="fw%d" % (g % 3))

                    base = (k % 2) * 1024

                    def _(e, k=k, g=g, base=base):
                        hcl = k % 4
                        wb = (g % 3) * 4096
                        mms = []
                        for dc in range(8):
                            mms.append(e.matmul(
                                out=pC[:, base: base + 258],
                                lhsT=w1buf[:, wb + dc * 512 + hcl * 128: wb + dc * 512 + (hcl + 1) * 128],
                                rhs=xqb[:, dc * 520: dc * 520 + 258],
                                start=(dc == 0), stop=(dc == 7)))
                        for dc in range(8):
                            mms.append(e.matmul(
                                out=pC[:, base + 512: base + 512 + 256],
                                lhsT=w1buf[:, wb + dc * 512 + hcl * 128: wb + dc * 512 + (hcl + 1) * 128],
                                rhs=xqb[:, dc * 520 + 258: dc * 520 + 514],
                                start=(dc == 0), stop=(dc == 7)))
                        return mms
                    dep = [tok_w1dma[g], tok_xq]
                    if k >= 2:
                        dep.append(evac_tok[k - 2])
                    mm1_tok[k] = S.step("tensor", _, after=dep)

                    def _(e, k=k, base=base):
                        hc = k
                        return [
                            e.copy(out=h_bf[:, hc * HBW: hc * HBW + 258], in_=pC[:, base: base + 258]),
                            e.copy(out=h_bf[:, hc * HBW + 258: hc * HBW + 514],
                                   in_=pC[:, base + 512: base + 512 + 256]),
                        ]
                    evac_tok[k] = S.step("scalar", _, after=[mm1_tok[k]])

                def emit_conv(ck):
                    psa, psb = (pA, pB) if (ck % 2) == 0 else (ps3, ps4)

                    def _(e, ck=ck, fn=fn, psa=psa, psb=psb):
                        mms = []
                        for cl in range(2):
                            c = 2 * ck + cl
                            po = psa if cl == 0 else psb
                            for tap in range(3):
                                mms.append(e.matmul(
                                    out=po[:, 0:fn],
                                    lhsT=convd[:, (3 * c + tap) * 128:(3 * c + tap + 1) * 128],
                                    rhs=h_bf[:, c * HBW + tap: c * HBW + tap + fn],
                                    start=(tap == 0), stop=(tap == 2)))
                        return mms
                    dep = [evac_tok[2 * ck + 1]]
                    if ck >= 2:
                        dep.append(silu_tok[ck - 2])
                    if ck < 2 and tok_ln2_prev is not None:
                        dep.append(tok_ln2_prev)
                    conv_tok[ck] = S.step("tensor", _, after=dep)

                    def _(e, ck=ck, fn=fn, psa=psa, psb=psb):
                        return [e.activation(out=h_silu[:, (2 * ck + cl) * FT: (2 * ck + cl) * FT + fn],
                                             in_=(psa if cl == 0 else psb)[:, 0:fn],
                                             func=AF.Silu, bias=convb[:, 2 * ck + cl: 2 * ck + cl + 1],
                                             scale=1.0)
                                for cl in range(2)]
                    silu_tok[ck] = S.step("scalar", _, after=[conv_tok[ck]])

                for k in range(32):
                    emit_mm1(k)
                    if k >= 3 and k % 2 == 1:
                        emit_conv((k - 3) // 2)
                emit_conv(15)

                def _(e, fn=fn):
                    hin = h_silu[:].rearrange("p (c t) -> p t c", c=32)
                    return e.tensor_reduce(
                        out=M2[:].rearrange("p (t o) -> p t o", o=1),
                        in_=hin, axis=AX.X, op=ALU.max, apply_absolute_value=True)
                tok_red = S.step("vector", _, after=[silu_tok[15]])

                def _(e):
                    return [e.transpose(out=ps4[0:128, t * 128:(t + 1) * 128],
                                        in_=M2[:, t * 128:(t + 1) * 128],
                                        identity=ident[:])
                            for t in range(4)]
                tok_tr = S.step("tensor", _, after=[tok_red, silu_tok[15]])

                def _(e):
                    ops = []
                    for t in range(4):
                        ops.append(e.reduce_max(out=cols_t[0:128, 4 + t:5 + t],
                                                in_=ps4[0:128, t * 128:(t + 1) * 128], axis=AX.X))
                    ops += [
                        e.drain(),
                        e.tensor_scalar(out=cols_t[:, 8:12], in0=cols_t[:, 4:8],
                                        scalar1=1e-5, scalar2=None, op0=ALU.max),
                        e.drain(),
                        e.tensor_scalar(out=cols_t[:, 8:12], in0=cols_t[:, 8:12],
                                        scalar1=1.0 / 127.0, scalar2=None, op0=ALU.mult),
                        e.drain(),
                        e.reciprocal(out=cols_t[:, 12:16], in_=cols_t[:, 8:12]),
                        e.drain(),
                        e.tensor_tensor(out=cols_t[:, 4:8], in0=cols_t[:, 8:12],
                                        in1=mc2col[:, 0:1].broadcast_to([128, 4]),
                                        op=ALU.mult),
                    ]
                    return ops
                tok_cols = S.step("vector", _, after=[tok_tr])

                def _(e):
                    return [e.transpose(out=ps3[0:1, t * 128:(t + 1) * 128],
                                        in_=cols_t[:, 12 + t:13 + t], identity=ident[:])
                            for t in range(4)]
                tok_s2t = S.step("tensor", _, after=[tok_cols, silu_tok[15]])

                def _(e):
                    return e.tensor_copy(out=srcrow[0:1, 0:512], in_=ps3[0:1, 0:512])
                tok_s2c = S.step("vector", _, after=[tok_s2t])

                def _(e, fn=fn):
                    return e.matmul(out=ps3[:, 0:fn], lhsT=onesf[:], rhs=srcrow[:, 0:fn],
                                    start=True, stop=True)
                tok_s2b = S.step("tensor", _, after=[tok_s2c])

                def _(e):
                    return [e.transpose(out=ps4[0:1, t * 128:(t + 1) * 128],
                                        in_=cols_t[:, 4 + t:5 + t], identity=ident[:])
                            for t in range(4)]
                tok_d2t = S.step("tensor", _, after=[tok_cols])

                def _(e):
                    return e.tensor_copy(out=srcrow[0:1, 0:512], in_=ps4[0:1, 0:512])
                tok_d2c = S.step("vector", _, after=[tok_d2t, tok_s2b])

                def _(e, fn=fn):
                    return e.matmul(out=ps4[:, 0:fn], lhsT=onesf[:], rhs=srcrow[:, 0:fn],
                                    start=True, stop=True)
                tok_d2b = S.step("tensor", _, after=[tok_d2c])

                def _(e, fn=fn):
                    return e.tensor_copy(out=d2sb[:, 0:fn], in_=ps4[:, 0:fn])
                tok_d2s = S.step("vector", _, after=[tok_d2b])

                qv_tok = [None] * 8
                qs_tok = [None] * 8
                for grp in range(8):
                    def _(e, grp=grp, fn=fn):
                        h3 = h_silu[:].rearrange("p (c t) -> p c t", c=32)[:, 4 * grp:4 * grp + 4, :]
                        t3 = tbuf[:].rearrange("p (c t) -> p c t", c=4)
                        return [
                            e.tensor_tensor(out=t3, in0=h3,
                                            in1=ps3[:, None, 0:fn].broadcast_to([128, 4, fn]),
                                            op=ALU.mult),
                            e.drain(),
                            e.tensor_copy(out=t32[:], in_=tbuf[:]),
                        ]
                    dep = [tok_s2b]
                    if grp >= 1:
                        dep.append(qs_tok[grp - 1])
                    qv_tok[grp] = S.step("vector", _, after=dep)

                    def _(e, grp=grp, fn=fn):
                        h3 = h_bf[:].rearrange("p (c t) -> p c t", t=HBW)[:, 4 * grp:4 * grp + 4, 0:fn]
                        t3 = t32[:].rearrange("p (c t) -> p c t", c=4)
                        return e.copy(out=h3, in_=t3)
                    qs_tok[grp] = S.step("scalar", _,
                                         after=[qv_tok[grp], conv_tok[2 * grp + 1],
                                                mm2_prev_reader.get(grp, None)] if False else
                                         [qv_tok[grp], conv_tok[2 * grp + 1]])

                mm2_tok = {}
                evh_tok = {}
                for half in range(2):
                    for g in range(8):
                        @dma_step(1)
                        def _(e, g=g, half=half):
                            gi = half * 8 + g
                            return e.dma_start(
                                out=w2buf[:, (gi % 2) * 2048:(gi % 2) * 2048 + 2048]
                                .rearrange("p (c t) -> p c t", c=4),
                                in_=w2q_s.rearrange("(c p) t -> p c t", p=128)
                                [:, 4 * g: 4 * (g + 1), half * 512:(half + 1) * 512])
                        gi = half * 8 + g
                        dep = [tok_xq]
                        if gi >= 2:
                            dep.append(mm2_tok[gi - 2])
                        tok_w2dma[gi] = S.step("sync", _, dma=True, after=dep,
                                               sem_key="gw%d" % (gi % 2))

                        def _(e, g=g, half=half, fn=fn):
                            gi = half * 8 + g
                            wb = (gi % 2) * 2048
                            mms = []
                            for hcl in range(4):
                                hc = 4 * g + hcl
                                for ocl in range(4):
                                    mms.append(e.matmul(
                                        out=pC[:, ocl * 512: ocl * 512 + fn],
                                        lhsT=w2buf[:, wb + hcl * 512 + ocl * 128: wb + hcl * 512 + (ocl + 1) * 128],
                                        rhs=h_bf[:, hc * HBW: hc * HBW + fn],
                                        start=(hc == 0), stop=(hc == 31)))
                            return mms
                        dep = [tok_w2dma[gi], qs_tok[g]]
                        if half == 0 and g == 0:
                            dep.append(silu_tok[15])
                            dep.append(evac_tok[31])
                        if half == 1 and g == 0:
                            dep.append(evh_tok[0])
                        mm2_tok[gi] = S.step("tensor", _, after=dep)

                    def _(e, half=half, fn=fn):
                        p3 = pC[:].rearrange("p (c t) -> p c t", t=512)
                        o3 = outst[:].rearrange("p (c t) -> p c t", c=8)[:, half * 4:(half + 1) * 4, :]
                        x3 = xl1b[:].rearrange("p (c t) -> p c t", c=8)[:, half * 4:(half + 1) * 4, :]
                        return [
                            e.tensor_tensor(out=o3, in0=p3,
                                            in1=d2sb[:, None, 0:fn].broadcast_to([128, 4, fn]),
                                            op=ALU.mult),
                            e.drain(),
                            e.tensor_tensor(out=o3, in0=o3, in1=x3, op=ALU.add),
                        ]
                    evh_tok[half] = S.step("vector", _,
                                           after=[mm2_tok[half * 8 + 7], tok_d2s])

                def _(e, fn=fn):
                    return e.tensor_copy(out=vbf2[:], in_=outst[:])
                S.step("vector", _, after=[evh_tok[1]])

                def _(e, fn=fn):
                    return [e.matmul(out=ps3[0:1, 0:fn], lhsT=onesb[:, 0:1],
                                     rhs=vbf2[:, dc * FT: dc * FT + fn],
                                     start=(dc == 0), stop=(dc == 7))
                            for dc in range(8)]
                S.step("tensor", _, extra_waits=[("vector", S.cnt["vector"])])

                def _(e, fn=fn):
                    return e.activation(out=vbf2[:], in_=outst[:], func=AF.Square)
                S.step("scalar", _, extra_waits=[("tensor", S.cnt["tensor"])])

                def _(e, fn=fn):
                    return [e.matmul(out=ps4[0:1, 0:fn], lhsT=onesb[:, 0:1],
                                     rhs=vbf2[:, dc * FT: dc * FT + fn],
                                     start=(dc == 0), stop=(dc == 7))
                            for dc in range(8)]
                S.step("tensor", _)

                def _(e, fn=fn):
                    return [
                        e.tensor_scalar(out=rows_t[0:1, 0:fn], in0=ps3[0:1, 0:fn],
                                        scalar1=1.0 / 1024.0, scalar2=None, op0=ALU.mult),
                        e.drain(),
                        e.tensor_tensor(out=rows_t[0:1, 512:512 + fn], in0=rows_t[0:1, 0:fn],
                                        in1=rows_t[0:1, 0:fn], op=ALU.mult),
                        e.drain(),
                        e.scalar_tensor_tensor(out=rows_t[0:1, 1024:1024 + fn],
                                               in0=ps4[0:1, 0:fn], scalar=1.0 / 1024.0,
                                               in1=rows_t[0:1, 512:512 + fn],
                                               op0=ALU.mult, op1=ALU.subtract),
                        e.drain(),
                        e.tensor_scalar(out=rows_t[0:1, 1024:1024 + fn],
                                        in0=rows_t[0:1, 1024:1024 + fn],
                                        scalar1=EPS_LN, scalar2=None, op0=ALU.add),
                        e.drain(),
                        e.reciprocal(out=rows_t[0:1, 1536:1536 + fn],
                                     in_=rows_t[0:1, 1024:1024 + fn]),
                    ]
                S.step("vector", _)

                def _(e, fn=fn):
                    return e.activation(out=srcrow[0:1, 0:fn], in_=rows_t[0:1, 1536:1536 + fn],
                                        func=AF.Sqrt)
                S.step("scalar", _)

                def _(e, fn=fn):
                    return e.matmul(out=ps4[:, 0:fn], lhsT=onesf[:], rhs=srcrow[:, 0:fn],
                                    start=True, stop=True)
                S.step("tensor", _)

                def _(e, fn=fn):
                    return e.tensor_tensor(out=srcrow[0:1, 0:fn], in0=srcrow[0:1, 0:fn],
                                           in1=rows_t[0:1, 0:fn], op=ALU.mult)
                S.step("vector", _)

                def _(e, fn=fn):
                    return e.matmul(out=ps3[:, 0:fn], lhsT=onesf[:], rhs=srcrow[:, 0:fn],
                                    start=True, stop=True)
                S.step("tensor", _)

                def _(e, fn=fn):
                    o3 = outst[:].rearrange("p (c t) -> p c t", c=8)
                    return [
                        e.tensor_tensor(out=o3, in0=o3,
                                        in1=ps4[:, None, 0:fn].broadcast_to([128, 8, fn]),
                                        op=ALU.mult),
                        e.drain(),
                        e.tensor_tensor(out=o3, in0=o3,
                                        in1=ps3[:, None, 0:fn].broadcast_to([128, 8, fn]),
                                        op=ALU.subtract),
                    ]
                tok_ln2_prev = S.step("vector", _)

                @dma_step(1)
                def _(e, ts0=ts0, fn=fn):
                    return e.dma_start(
                        out=outT_h.rearrange("(c p) t -> p c t", p=128)[:, :, ts0 - 1: ts0 - 1 + fn],
                        in_=outst[:].rearrange("p (c t) -> p c t", c=8))
                S.step("gpsimd", _, dma=True, after=[tok_ln2_prev])
                S.last = tok_ln2_prev

        PHASE_MARKS.append(("FFN", S.nsteps))

        final_g = S.cnt["gpsimd"]

        def _(e):
            e.wait_ge(sems["gpsimd"], final_g)
            return e.engine_nop()
        S.step("gpsimd", _)

        with nc.Block() as block:
            @block.sync
            def _(eng):
                for op in S.ops["sync"]:
                    op(eng)

            @block.vector
            def _(eng):
                for op in S.ops["vector"]:
                    op(eng)

            @block.scalar
            def _(eng):
                for op in S.ops["scalar"]:
                    op(eng)

            @block.tensor
            def _(eng):
                for op in S.ops["tensor"]:
                    op(eng)

            @block.gpsimd
            def _(eng):
                for op in S.ops["gpsimd"]:
                    op(eng)

    return nc


def _band_mask(off, qn):
    m = np.zeros((128, 1024), np.float32)
    for h in range(8):
        mm = np.zeros((128, 128), np.float32)
        i = np.arange(128)[:, None]
        j = np.arange(128)[None, :]
        band = (np.abs(i - j - off) <= 3) & (j < qn)
        mm[band] = 1.0
        m[:, h * 128:(h + 1) * 128] = mm
    return m


def kernel(L0_attn_x, L0_attn_w_qkv, L0_attn_proj, L0_attn_w_out, L0_attn_b_out,
           L0_ffn_w1, L0_ffn_conv_w, L0_ffn_conv_b, L0_ffn_w2):
    x = np.ascontiguousarray(np.asarray(L0_attn_x, dtype=np.float32))
    wqkv = np.ascontiguousarray(np.asarray(L0_attn_w_qkv, dtype=np.float32))
    proj = np.ascontiguousarray(np.asarray(L0_attn_proj, dtype=np.float32))
    wout = np.ascontiguousarray(np.asarray(L0_attn_w_out, dtype=np.float32))
    bout = np.ascontiguousarray(np.asarray(L0_attn_b_out, dtype=np.float32))[None, :]
    w1 = np.asarray(L0_ffn_w1, dtype=np.float32)
    convw = np.asarray(L0_ffn_conv_w, dtype=np.float32)[:, 0, :]
    convb = np.asarray(L0_ffn_conv_b, dtype=np.float32)
    w2 = np.asarray(L0_ffn_w2, dtype=np.float32)

    w1T = np.ascontiguousarray(w1.T)
    w2T = np.ascontiguousarray(w2.T)

    convdiag = np.zeros((128, 96 * 128), np.float32)
    idx = np.arange(128)
    for c in range(32):
        for tap in range(3):
            blk = (3 * c + tap) * 128
            convdiag[idx, blk + idx] = convw[c * 128:(c + 1) * 128, tap]
    convb_m = np.ascontiguousarray(convb.reshape(32, 128).T)

    ident = np.eye(128, dtype=np.float32)
    n_tiles = (NQ + QT - 1) // QT
    qs_last = QS0 + (n_tiles - 1) * QT
    qn_last = QS0 + NQ - qs_last
    off_last = qs_last - min(qs_last - 3, TB - 128)
    bandmask = _band_mask(3, QT)
    bandmask2 = _band_mask(off_last, qn_last)

    in_maps = []
    for core in range(NCORES):
        b, r = core // 4, core % 4
        t0 = r * TOK
        lo, hi = t0 - HB, t0 + TOK + HB
        xT = np.zeros((D, TB), np.float32)
        s, e = max(lo, 0), min(hi, N)
        xT[:, s - lo: s - lo + (e - s)] = x[b, s:e, :].T
        kmask = np.zeros((128, 16), np.float32)
        kmask[:, 0:8] = 1.0 if lo >= 0 else 0.0
        kmask[:, 8:16] = 1.0 if hi <= N else 0.0
        qmask = np.zeros((128, 2), np.float32)
        qmask[:, 0] = 1.0 if t0 - 1 >= 0 else 0.0
        qmask[:, 1] = 1.0 if t0 + TOK + 1 <= N else 0.0
        in_maps.append({
            "xT": xT, "wqkv": wqkv, "proj": proj, "wout": wout, "boutrow": bout,
            "w1T": w1T, "w2T": w2T, "convdiag": convdiag, "convbH": convb_m,
            "identH": ident, "bandmask": bandmask, "bandmask2": bandmask2,
            "kmaskH": kmask, "qmaskH": qmask,
        })

    nc = build_program()
    res = run_bass_kernel_spmd(nc, in_maps, core_ids=list(range(NCORES)))

    out = np.empty((B, N, D), np.float32)
    for core in range(NCORES):
        b, r = core // 4, core % 4
        t0 = r * TOK
        out[b, t0:t0 + TOK, :] = res.results[core]["outT"].T
    return out


# version 21
# speedup vs baseline: 1.0264x; 1.0264x over previous
import numpy as np
import contextlib
import os
import concourse.bass as bass
import concourse.mybir as mybir
from concourse.bass_utils import run_bass_kernel_spmd

F32 = mybir.dt.float32
BF16 = mybir.dt.bfloat16
I32 = mybir.dt.int32
AF = mybir.ActivationFunctionType
ALU = mybir.AluOpType
AX = mybir.AxisListType

B, N, D = 2, 8192, 1024
NCORES = 8
TOK = 2048
HB = 8
TB = TOK + 2 * HB
HEADS, HD, FEAT = 8, 128, 64
DH = 4096
QT = 122
QS0, NQ = 7, 2050
FT = 512
HBW = 520
EPS_LN, EPS_Z = 1e-5, 1e-8

ENG = ("sync", "vector", "scalar", "tensor", "gpsimd")
BUILD_LIMIT = int(os.environ.get("KBISECT", "0")) or None
PHASE_MARKS = []


class Sched:

    def __init__(self, nc, sems):
        self.nc = nc
        self.sems = sems
        self.ops = {e: [] for e in ENG}
        self.cnt = {e: 0 for e in ENG}
        self.last = None
        self.nsteps = 0
        self.auto_pump = None
        self._pumping = False

    def step(self, eng, builder, dma=False, after=None, extra_waits=(), sem_key=None):
        self.nsteps += 1
        if BUILD_LIMIT is not None and self.nsteps > BUILD_LIMIT:
            return self.last
        if after is None:
            deps = [self.last] if self.last is not None else []
        elif isinstance(after, list):
            deps = list(after)
        else:
            deps = [after]
        deps = [d for d in deps if d is not None]
        deps += list(extra_waits)
        per = {}
        for (s, v) in deps:
            if s == eng:
                continue
            per[s] = max(per.get(s, 0), v)
        waits = sorted(per.items())
        key = sem_key if (dma and sem_key is not None) else eng
        sem = self.sems[key]
        if dma:
            nd = getattr(builder, "n_dma")
            self.cnt[key] += 16 * nd
            tok = (key, self.cnt[key])

            def run(e, waits=tuple(waits), builder=builder, nd=nd):
                for (s, v) in waits:
                    e.wait_ge(self.sems[s], v)
                insts = builder(e)
                if not isinstance(insts, (list, tuple)):
                    insts = [insts]
                assert len(insts) == nd
                for i in insts:
                    i.then_inc(sem, 16)

            self.ops[eng].append(run)
        else:
            self.cnt[eng] += 1
            tok = (eng, self.cnt[eng])

            def run(e, waits=tuple(waits), builder=builder, eng=eng):
                for (s, v) in waits:
                    e.wait_ge(self.sems[s], v)
                insts = builder(e)
                if not isinstance(insts, (list, tuple)):
                    insts = [insts]
                if eng in ("vector", "scalar", "gpsimd"):
                    e.drain().then_inc(sem, 1)
                else:
                    insts[-1].then_inc(sem, 1)

            self.ops[eng].append(run)
        self.last = tok
        if self.auto_pump is not None and not self._pumping:
            self.auto_pump()
        return tok


def dma_step(n):
    def deco(fn):
        fn.n_dma = n
        return fn
    return deco


def build_program():
    nc = bass.Bass("TRN2", target_bir_lowering=False)

    xT_h = nc.dram_tensor("xT", [D, TB], F32, kind="ExternalInput").ap()
    wqkv_h = nc.dram_tensor("wqkv", [D, 3 * D], F32, kind="ExternalInput").ap()
    proj_h = nc.dram_tensor("proj", [HD, FEAT], F32, kind="ExternalInput").ap()
    wout_h = nc.dram_tensor("wout", [D, D], F32, kind="ExternalInput").ap()
    bout_h = nc.dram_tensor("boutrow", [1, D], F32, kind="ExternalInput").ap()
    w1T_h = nc.dram_tensor("w1T", [D, DH], F32, kind="ExternalInput").ap()
    w2T_h = nc.dram_tensor("w2T", [DH, D], F32, kind="ExternalInput").ap()
    convd_h = nc.dram_tensor("convdiag", [128, 96 * 128], F32, kind="ExternalInput").ap()
    convb_h = nc.dram_tensor("convbH", [128, 32], F32, kind="ExternalInput").ap()
    ident_h = nc.dram_tensor("identH", [128, 128], F32, kind="ExternalInput").ap()
    bmask_h = nc.dram_tensor("bandmask", [128, 1024], F32, kind="ExternalInput").ap()
    bmask2_h = nc.dram_tensor("bandmask2", [128, 1024], F32, kind="ExternalInput").ap()
    kmask_h = nc.dram_tensor("kmaskH", [128, 16], F32, kind="ExternalInput").ap()
    qmask_h = nc.dram_tensor("qmaskH", [128, 2], F32, kind="ExternalInput").ap()
    outT_h = nc.dram_tensor("outT", [D, TOK], F32, kind="ExternalOutput").ap()
    _skind = "ExternalOutput" if os.environ.get("KDEBUG") else "Internal"
    xln1_s = nc.dram_tensor("xln1_s", [D, NQ], F32, kind=_skind).ap()
    xq_s = nc.dram_tensor("xq_s", [D, NQ], BF16, kind=_skind).ap()
    w1q_s = nc.dram_tensor("w1q_s", [D, DH], BF16, kind=_skind).ap()
    w2q_s = nc.dram_tensor("w2q_s", [DH, D], BF16, kind=_skind).ap()

    ctx = contextlib.ExitStack()
    with ctx:
        def sb(name, shape, dt):
            return ctx.enter_context(nc.sbuf_tensor(name, shape, dt))

        def psum(name, shape):
            return ctx.enter_context(nc.psum_tensor(name, shape, F32))

        ident = sb("ident", [128, 128], F32)
        onesb = sb("onesb", [128, 1024], BF16)
        onesf = sb("onesf", [128, 128], F32)
        eps128 = sb("eps128", [128, 128], BF16)
        c32 = sb("c32", [32, 520], BF16)
        pp = sb("pp", [128, 192], BF16)
        boutpad = sb("boutpad", [32, 1024], BF16)
        minus1 = sb("minus1", [128, 1], F32)
        c1024 = sb("c1024", [128, 1], F32)
        rows_t = sb("rows_t", [1, 2048], F32)
        srcrow = sb("srcrow", [128, 512], F32)
        cols_t = sb("cols_t", [128, 16], F32)
        mc1col = sb("mc1col", [128, 1], F32)
        mc2col = sb("mc2col", [128, 1], F32)
        ws1col = sb("ws1col", [128, 1], F32)
        ws2col = sb("ws2col", [128, 1], F32)
        wtot = sb("wtot", [1, 8], F32)
        kmask = sb("kmask", [128, 16], F32)
        qmask = sb("qmask", [128, 2], F32)

        pA = psum("pA", [128, 512])
        pB = psum("pB", [128, 512])
        pC = psum("pC", [128, 2048])
        ps3 = psum("ps3", [128, 512])
        ps4 = psum("ps4", [128, 512])

        sems = {}
        for e in ENG + ("wq0", "wq1", "wq2", "wq3", "fx", "fw0", "fw1", "fw2", "gw0", "gw1", "gw2", "ax"):
            sems[e] = ctx.enter_context(nc.semaphore("sem_" + e))
        S = Sched(nc, sems)
        for e in ("wq0", "wq1", "wq2", "wq3", "fx", "fw0", "fw1", "fw2", "gw0", "gw1", "gw2", "ax"):
            S.cnt[e] = 0

        @dma_step(4)
        def _(e):
            return [
                e.dma_start(out=ident[:], in_=ident_h[:]),
                e.dma_start(out=kmask[:], in_=kmask_h[:]),
                e.dma_start(out=qmask[:], in_=qmask_h[:]),
                e.dma_start(out=srcrow[:, 0:64], in_=proj_h[:]),
            ]
        S.step("sync", _, dma=True)

        def _(e):
            return [
                e.memset(onesb[:], 1.0),
                e.memset(onesf[:], 1.0),
                e.memset(eps128[:], EPS_Z / 128.0),
                e.memset(c32[:], 1.0 / 32.0),
                e.memset(pp[:], 0.0),
                e.memset(boutpad[:], 0.0),
                e.memset(minus1[:], -1.0),
                e.memset(c1024[:], 1.0 / 1024.0),
                e.memset(rows_t[:], 0.0),
                e.memset(cols_t[:], 0.0),
                e.drain(),
                e.tensor_copy(out=pp[:, 0:64], in_=srcrow[:, 0:64]),
                e.tensor_copy(out=pp[:, 128:192], in_=srcrow[:, 0:64]),
                e.drain(),
                e.memset(srcrow[:], 0.0),
            ]
        S.step("vector", _)

        PHASE_MARKS.append(("P0", S.nsteps))

        wq_done = [None, None]
        wq_mean = [None, None]

        def wquant_gen(wT_h, wq_s, n_iters, width, acc_base, mccol, wscol, slot,
                       rows128, colw, gidx):
            stg = [stageW, stageW2, stageW3, stageW4]
            tok_dma = [None] * n_iters
            tok_abs = [None] * n_iters

            def src_ap(it):
                rb, cb = it // colw, it % colw
                return wT_h[rb * 128:(rb + 1) * 128, cb * width:(cb + 1) * width]

            def dst_ap(it):
                rb, cb = it // colw, it % colw
                return wq_s[rb * 128:(rb + 1) * 128, cb * width:(cb + 1) * width]

            for it in range(n_iters):
                @dma_step(1)
                def _(e, it=it):
                    return e.dma_start(out=stg[it % 4][:, 0:width], in_=src_ap(it))
                if it >= 4:
                    dep = [tok_abs[it - 4]]
                elif gidx == 1:
                    dep = [wq_done[0]]
                else:
                    dep = [S.last]
                tok_dma[it] = S.step("sync", _, dma=True, after=dep,
                                     sem_key="wq%d" % (it % 4))

                def _(e, it=it):
                    return e.activation(out=wtmp[:, 0:width], in_=stg[it % 4][:, 0:width],
                                        func=AF.Abs, accum_out=wacc[:, acc_base + it: acc_base + it + 1])
                yield
                tok_abs[it] = S.step("scalar", _, after=[tok_dma[it]])
                yield

            def _(e):
                return e.reduce_sum(out=cols_t[:, 8:9], in_=wacc[:, acc_base:acc_base + n_iters], axis=AX.X)
            S.step("vector", _, after=[tok_abs[n_iters - 1]])

            def _(e):
                return e.transpose(out=ps3[0:1, 0:128], in_=cols_t[:, 8:9], identity=ident[:])
            S.step("tensor", _, after=[S.last])

            def _(e):
                return [
                    e.reduce_sum(out=wtot[0:1, slot:slot + 1], in_=ps3[0:1, 0:128], axis=AX.X),
                    e.drain(),
                    e.tensor_scalar(out=wtot[0:1, slot + 1:slot + 2], in0=wtot[0:1, slot:slot + 1],
                                    scalar1=1.0 / (n_iters * 128.0 * width), scalar2=None,
                                    op0=ALU.mult),
                    e.drain(),
                    e.tensor_scalar(out=wtot[0:1, slot + 1:slot + 2], in0=wtot[0:1, slot + 1:slot + 2],
                                    scalar1=1e-5, scalar2=None, op0=ALU.max),
                    e.drain(),
                    e.reciprocal(out=wtot[0:1, slot + 2:slot + 3], in_=wtot[0:1, slot + 1:slot + 2]),
                    e.drain(),
                    e.tensor_copy(out=srcrow[0:1, 0:1], in_=wtot[0:1, slot + 1:slot + 2]),
                    e.tensor_copy(out=srcrow[0:1, 1:2], in_=wtot[0:1, slot + 2:slot + 3]),
                ]
            S.step("vector", _, after=[S.last])

            def _(e):
                return e.matmul(out=ps4[:, 0:2], lhsT=onesf[:], rhs=srcrow[:, 0:2],
                                start=True, stop=True)
            S.step("tensor", _, after=[S.last])

            def _(e):
                return [
                    e.tensor_copy(out=mccol[:], in_=ps4[:, 0:1]),
                    e.tensor_copy(out=wscol[:], in_=ps4[:, 1:2]),
                ]
            tok_sc = S.step("vector", _, after=[S.last])
            wq_mean[gidx] = tok_sc
            yield

            tok_v = [None] * n_iters
            tok_s = [None] * n_iters
            tok_g = [None] * n_iters
            for it in range(n_iters):
                @dma_step(1)
                def _(e, it=it):
                    return e.dma_start(out=stg[it % 4][:, 0:width], in_=src_ap(it))
                tok_dma[it] = S.step("sync", _, dma=True,
                                     after=[tok_v[it - 4]] if it >= 4 else [tok_sc],
                                     sem_key="wq%d" % (it % 4))
                yield

                def _(e, it=it):
                    return e.activation(out=wtmp[:, 0:width], in_=stg[it % 4][:, 0:width],
                                        func=AF.Identity, scale=wscol[:])
                tok_sa = S.step("scalar", _,
                                after=[tok_dma[it], tok_sc] + ([tok_v[it - 1]] if it >= 1 else []))

                def _(e, it=it):
                    return e.tensor_scalar(out=w32[:, 0:width], in0=wtmp[:, 0:width],
                                           scalar1=1.0, scalar2=-1.0,
                                           op0=ALU.min, op1=ALU.max)
                tok_v[it] = S.step("vector", _,
                                   after=[tok_sa] + ([tok_s[it - 1]] if it >= 1 else []))

                def _(e, it=it):
                    return e.copy(out=wqout[:, 0:width], in_=w32[:, 0:width])
                tok_s[it] = S.step("scalar", _,
                                   after=[tok_v[it]] + ([tok_g[it - 1]] if it >= 1 else []))

                @dma_step(1)
                def _(e, it=it):
                    return e.dma_start(out=dst_ap(it), in_=wqout[:, 0:width])
                tok_g[it] = S.step("gpsimd", _, dma=True, after=[tok_s[it]])
                yield
            wq_done[gidx] = tok_g[n_iters - 1]

        wq_gens = [None, None]

        def pump_wq(n):
            saved = S.last
            S._pumping = True
            for _ in range(n):
                for gidx in range(2):
                    if wq_gens[gidx] is not None:
                        try:
                            next(wq_gens[gidx])
                            break
                        except StopIteration:
                            wq_gens[gidx] = None
            S._pumping = False
            S.last = saved

        def drain_wq():
            saved = S.last
            S._pumping = True
            for gidx in range(2):
                if wq_gens[gidx] is not None:
                    for _ in wq_gens[gidx]:
                        pass
                    wq_gens[gidx] = None
            S._pumping = False
            S.last = saved

            def _(e):
                return e.engine_nop()
            S.step("vector", _, after=[saved] + [t for t in wq_done if t is not None])

        PHASE_MARKS.append(("P1-wquant", S.nsteps))

        with contextlib.ExitStack() as actx:
            x_bf = actx.enter_context(nc.sbuf_tensor("x_bf", [128, 8 * TB], BF16))
            qphi = actx.enter_context(nc.sbuf_tensor("qphi", [128, 4 * TB], BF16))
            kphi = actx.enter_context(nc.sbuf_tensor("kphi", [128, 4 * TB], BF16))
            stage = actx.enter_context(nc.sbuf_tensor("stageA", [128, 1024], F32))
            wacc = actx.enter_context(nc.sbuf_tensor("wacc", [128, 80], F32))
            stageW = actx.enter_context(nc.sbuf_tensor("stageW", [128, 1024], F32))
            stageW2 = actx.enter_context(nc.sbuf_tensor("stageW2", [128, 1024], F32))
            stageW3 = actx.enter_context(nc.sbuf_tensor("stageW3", [128, 1024], F32))
            stageW4 = actx.enter_context(nc.sbuf_tensor("stageW4", [128, 1024], F32))
            wtmp = actx.enter_context(nc.sbuf_tensor("wtmp", [128, 1024], F32))
            w32 = actx.enter_context(nc.sbuf_tensor("w32", [128, 1024], I32))
            wqout = actx.enter_context(nc.sbuf_tensor("wqout", [128, 1024], BF16))
            wq_gens[0] = wquant_gen(w1T_h, w1q_s, 32, 1024, 0, mc1col, ws1col, 0, 8, 4, 0)
            wq_gens[1] = wquant_gen(w2T_h, w2q_s, 32, 1024, 32, mc2col, ws2col, 4, 32, 1, 1)
            bmask = actx.enter_context(nc.sbuf_tensor("bmask", [128, 1024], F32))
            bmask2 = actx.enter_context(nc.sbuf_tensor("bmask2", [128, 1024], F32))

            @dma_step(3)
            def _(e):
                return [e.dma_start(out=bmask[:], in_=bmask_h[:]),
                        e.dma_start(out=bmask2[:], in_=bmask2_h[:]),
                        e.dma_start(out=stage[0:1, 0:1024], in_=bout_h[:])]
            S.step("sync", _, dma=True)

            def _(e):
                return e.tensor_copy(out=boutpad[0:1, :], in_=stage[0:1, 0:1024])
            S.step("vector", _)

            with contextlib.ExitStack() as a1ctx:
                wqk = a1ctx.enter_context(nc.sbuf_tensor("wqk", [128, 8 * 2048], BF16))
                xres = a1ctx.enter_context(nc.sbuf_tensor("xres", [128, 8 * 528], F32))
                qtmp = a1ctx.enter_context(nc.sbuf_tensor("qtmp", [128, 8 * 512], BF16))
                for half in range(16):
                    @dma_step(1)
                    def _(e, half=half):
                        dc, hh = half // 2, half % 2
                        return e.dma_start(out=stage[:, 0:1024],
                                           in_=wqkv_h[dc * 128:(dc + 1) * 128, hh * 1024:(hh + 1) * 1024])
                    S.step("sync", _, dma=True)

                    def _(e, half=half):
                        return e.copy(out=wqk[:, half * 1024:(half + 1) * 1024], in_=stage[:, 0:1024])
                    S.step("scalar", _)

                S.auto_pump = lambda: pump_wq(3)
                tiles = [(t, min(512, TB - t)) for t in range(0, TB, 512)]
                for (ts, tl) in tiles:
                    @dma_step(1)
                    def _(e, ts=ts, tl=tl):
                        return e.dma_start(
                            out=xres[:].rearrange("p (c t) -> p c t", c=8)[:, :, 0:tl],
                            in_=xT_h.rearrange("(c p) t -> p c t", p=128)[:, :, ts:ts + tl])
                    S.step("sync", _, dma=True)

                    def _(e, ts=ts, tl=tl):
                        return e.tensor_copy(
                            out=x_bf[:].rearrange("p (c t) -> p c t", c=8)[:, :, ts:ts + tl],
                            in_=xres[:].rearrange("p (c t) -> p c t", c=8)[:, :, 0:tl])
                    S.step("vector", _)

                    for (wbase, phidst) in ((0, qphi), (1024, kphi)):
                        for hb in range(2):
                            def _(e, hb=hb, wbase=wbase, ts=ts, tl=tl):
                                mms = []
                                for hl in range(4):
                                    h = hb * 4 + hl
                                    for dc in range(8):
                                        mms.append(e.matmul(
                                            out=pC[:, hl * 512: hl * 512 + tl],
                                            lhsT=wqk[:, dc * 2048 + wbase + h * 128: dc * 2048 + wbase + (h + 1) * 128],
                                            rhs=x_bf[:, dc * TB + ts: dc * TB + ts + tl],
                                            start=(dc == 0), stop=(dc == 7)))
                                return mms
                            S.step("tensor", _)

                            def _(e, hb=hb, tl=tl):
                                return [e.copy(out=qtmp[:, (hb * 4 + hl) * 512: (hb * 4 + hl) * 512 + tl],
                                               in_=pC[:, hl * 512: hl * 512 + tl])
                                        for hl in range(4)]
                            S.step("scalar", _)

                        def _(e, ts=ts, tl=tl):
                            mms = []
                            for hp in range(4):
                                mms.append(e.matmul(out=pC[:, hp * 512: hp * 512 + tl], lhsT=pp[:, 0:128],
                                                    rhs=qtmp[:, hp * 512: hp * 512 + tl],
                                                    start=True, stop=False))
                                mms.append(e.matmul(out=pC[:, hp * 512: hp * 512 + tl], lhsT=pp[:, 64:192],
                                                    rhs=qtmp[:, (hp + 4) * 512: (hp + 4) * 512 + tl],
                                                    start=False, stop=False))
                                mms.append(e.matmul(out=pC[:, hp * 512: hp * 512 + tl],
                                                    lhsT=onesb[0:32, 0:128],
                                                    rhs=c32[0:32, 0:tl], start=False, stop=True))
                            return mms
                        S.step("tensor", _)

                        def _(e, tl=tl):
                            return [e.activation(out=xres[:, hp * 1056: hp * 1056 + tl],
                                                 in_=pC[:, hp * 512: hp * 512 + tl],
                                                 func=AF.Exp, bias=minus1[:], scale=1.0)
                                    for hp in range(4)]
                        S.step("scalar", _)

                        def _(e, ts=ts, tl=tl, phidst=phidst):
                            return [e.scalar_tensor_tensor(
                                out=phidst[:, hp * TB + ts: hp * TB + ts + tl],
                                in0=xres[:, hp * 1056: hp * 1056 + tl], scalar=1.0,
                                in1=pC[:, hp * 512: hp * 512 + tl],
                                op0=ALU.min, op1=ALU.max) for hp in range(4)]
                        S.step("vector", _)

                def _(e):
                    km3 = kphi[:].rearrange("p (c t) -> p c t", c=4)
                    return [
                        e.tensor_tensor(out=km3[:, :, 0:HB], in0=km3[:, :, 0:HB],
                                        in1=kmask[:, None, 0:8].broadcast_to([128, 4, 8]),
                                        op=ALU.mult),
                        e.tensor_tensor(out=km3[:, :, TB - HB:TB], in0=km3[:, :, TB - HB:TB],
                                        in1=kmask[:, None, 8:16].broadcast_to([128, 4, 8]),
                                        op=ALU.mult),
                    ]
                S.step("vector", _)
                while wq_mean[1] is None and wq_gens[1] is not None:
                    pump_wq(1)
                S.auto_pump = lambda: pump_wq(1)

                def _(e):
                    return e.engine_nop()
                S.step("vector", _,
                       after=[S.last] + [t for t in wq_mean if t is not None])

            PHASE_MARKS.append(("A1", S.nsteps))

            with contextlib.ExitStack() as a2ctx:
                def a2sb(name, shape, dt):
                    return a2ctx.enter_context(nc.sbuf_tensor(name, shape, dt))
                wv = a2sb("wv", [128, 8 * 1024], BF16)
                woutb = a2sb("woutb", [128, 8 * 1024], BF16)
                v_sb = a2sb("v_sb", [128, 1024], BF16)
                Sm = a2sb("Sm", [128, 1024], BF16)
                zinv = a2sb("zinv", [128, 1024], F32)
                attn_cp = a2sb("attn_cp", [128, 1024], F32)
                attn_sb = a2sb("attn_sb", [128, 1024], BF16)
                vres = a2sb("vres", [128, 976], F32)
                vresB = a2sb("vresB", [128, 976], F32)
                xresA2 = a2sb("xresA2", [128, 976], F32)
                vbf = a2sb("vbf", [128, 976], BF16)
                v2bf = a2sb("v2bf", [128, 976], BF16)
                xln1t = a2sb("xln1t", [128, 976], F32)
                tq = a2sb("tq", [128, 976], F32)
                tq32 = a2sb("tq32", [128, 976], I32)
                xqt = a2sb("xqt", [128, 976], BF16)
                Mx = a2sb("Mx", [128, 976], F32)

                for (dst, src_h, cbase) in ((wv, wqkv_h, 2048), (woutb, wout_h, 0)):
                    for dc in range(8):
                        @dma_step(1)
                        def _(e, dc=dc, src_h=src_h, cbase=cbase):
                            return e.dma_start(out=stage[:, 0:1024],
                                               in_=src_h[dc * 128:(dc + 1) * 128, cbase:cbase + 1024])
                        S.step("sync", _, dma=True)

                        def _(e, dc=dc, dst=dst):
                            return e.copy(out=dst[:, dc * 1024:(dc + 1) * 1024], in_=stage[:, 0:1024])
                        S.step("scalar", _)

                n_tiles = (NQ + QT - 1) // QT
                t0_tok = S.last
                TK = {k: [None] * n_tiles for k in (
                    "xdma", "vmm", "vevac", "smm", "mask", "zmm", "recip", "amm",
                    "acp", "zmul", "oproj", "vres", "cast", "stats", "chain",
                    "sqrt", "mr", "bc1", "apply", "mxtr", "cols", "rowtr",
                    "rowcp", "bc2", "quant", "qsp")}
                vres2 = [vres, vresB]

                def front(it):
                    qs = QS0 + it * QT
                    qn = min(QT, QS0 + NQ - qs)
                    hs = min(qs - 3, TB - 128)
                    mask = bmask if it < n_tiles - 1 else bmask2
                    vr = vres2[it % 2]

                    @dma_step(1)
                    def _(e, qs=qs, qn=qn):
                        return e.dma_start(
                            out=xresA2[:].rearrange("p (c t) -> p c t", c=8)[:, :, 0:qn],
                            in_=xT_h.rearrange("(c p) t -> p c t", p=128)[:, :, qs:qs + qn])
                    TK["xdma"][it] = S.step("sync", _, dma=True, sem_key="ax",
                                            after=[TK["vres"][it - 1]] if it else [t0_tok])

                    def _(e, hs=hs):
                        mms = []
                        for dc in range(8):
                            for hb in range(2):
                                mms.append(e.matmul(
                                    out=pC[:, hb * 512:(hb + 1) * 512],
                                    lhsT=x_bf[:, dc * TB + hs: dc * TB + hs + 128],
                                    rhs=wv[:, dc * 1024 + hb * 512: dc * 1024 + (hb + 1) * 512],
                                    start=(dc == 0), stop=(dc == 7)))
                        return mms
                    TK["vmm"][it] = S.step("tensor", _,
                                           after=[TK["vres"][it - 1]] if it else [t0_tok])

                    def _(e):
                        return e.tensor_copy(out=v_sb[:], in_=pC[:, 0:1024])
                    TK["vevac"][it] = S.step("vector", _, after=[TK["vmm"][it]])

                    def _(e, hs=hs, qs=qs, qn=qn):
                        mms = []
                        for h in range(HEADS):
                            par = (h // 4) * 64
                            blk = (h % 4) * TB
                            mms.append(e.matmul(
                                out=pC[:, 1024 + h * 128: 1024 + h * 128 + qn],
                                lhsT=kphi[par:par + 64, blk + hs: blk + hs + 128],
                                rhs=qphi[par:par + 64, blk + qs: blk + qs + qn],
                                start=(h % 4 == 0), stop=(h % 4 == 3),
                                tile_position=(par, 0)))
                        return mms
                    TK["smm"][it] = S.step("tensor", _,
                                           after=[TK["recip"][it - 1]] if it else [t0_tok])

                    def _(e, mask=mask):
                        return e.tensor_tensor(out=Sm[:], in0=pC[:, 1024:2048],
                                               in1=mask[:], op=ALU.mult)
                    TK["mask"][it] = S.step("vector", _, after=[TK["smm"][it]])

                    def _(e, qn=qn):
                        mms = []
                        for h in range(HEADS):
                            mms.append(e.matmul(
                                out=pC[:, 1024 + h * 128: 1024 + h * 128 + qn],
                                lhsT=onesb[:, 0:128], rhs=Sm[:, h * 128: h * 128 + qn],
                                start=(h % 4 == 0), stop=False))
                            mms.append(e.matmul(
                                out=pC[:, 1024 + h * 128: 1024 + h * 128 + qn],
                                lhsT=onesb[:, 0:128], rhs=eps128[:, 0:qn],
                                start=False, stop=(h % 4 == 3)))
                        return mms
                    TK["zmm"][it] = S.step("tensor", _, after=[TK["mask"][it]])

                    def _(e):
                        return e.reciprocal(out=zinv[:], in_=pC[:, 1024:2048])
                    TK["recip"][it] = S.step("vector", _, after=[TK["zmm"][it]])

                    def _(e, qn=qn):
                        mms = []
                        for h in range(HEADS):
                            dst = pA if h < 4 else pB
                            off = (h % 4) * 128
                            mms.append(e.matmul(
                                out=dst[:, off: off + qn],
                                lhsT=v_sb[:, h * 128:(h + 1) * 128],
                                rhs=Sm[:, h * 128: h * 128 + qn],
                                start=(h % 4 == 0), stop=(h % 4 == 3)))
                        return mms
                    TK["amm"][it] = S.step("tensor", _,
                                           after=[TK["mask"][it], TK["vevac"][it]] +
                                           ([TK["acp"][it - 1]] if it else []))

                    def _(e):
                        return [e.copy(out=attn_cp[:, 0:512], in_=pA[:]),
                                e.copy(out=attn_cp[:, 512:1024], in_=pB[:])]
                    TK["acp"][it] = S.step("scalar", _, after=[TK["amm"][it]])

                    def _(e):
                        return e.tensor_tensor(out=attn_sb[:], in0=attn_cp[:],
                                               in1=zinv[:], op=ALU.mult)
                    TK["zmul"][it] = S.step("vector", _, after=[TK["acp"][it]])

                    def _(e, qn=qn):
                        mms = []
                        for h in range(HEADS):
                            for oc in range(8):
                                mms.append(e.matmul(
                                    out=pC[:, oc * 128: oc * 128 + qn],
                                    lhsT=woutb[:, h * 1024 + oc * 128: h * 1024 + (oc + 1) * 128],
                                    rhs=attn_sb[:, h * 128: h * 128 + qn],
                                    start=(h == 0 and oc % 4 == 0), stop=False))
                        for oc in range(8):
                            mms.append(e.matmul(
                                out=pC[:, oc * 128: oc * 128 + qn],
                                lhsT=boutpad[0:32, oc * 128:(oc + 1) * 128],
                                rhs=onesb[0:32, 0:qn],
                                start=False, stop=(oc % 4 == 3)))
                        return mms
                    TK["oproj"][it] = S.step("tensor", _,
                                             after=[TK["zmul"][it], TK["vevac"][it]])

                    def _(e, qn=qn, vr=vr):
                        return e.tensor_tensor(
                            out=vr[:].rearrange("p (c t) -> p c t", c=8)[:, :, 0:qn],
                            in0=pC[:, 0:1024].rearrange("p (c t) -> p c t", t=128)[:, :, 0:qn],
                            in1=xresA2[:].rearrange("p (c t) -> p c t", c=8)[:, :, 0:qn],
                            op=ALU.add)
                    TK["vres"][it] = S.step("vector", _,
                                            after=[TK["oproj"][it], TK["xdma"][it]] +
                                            ([TK["apply"][it - 2]] if it >= 2 else []))

                def tail(it):
                    qs = QS0 + it * QT
                    qn = min(QT, QS0 + NQ - qs)
                    sc = qs - QS0
                    vr = vres2[it % 2]

                    def _(e, qn=qn, vr=vr):
                        return [
                            e.copy(out=vbf[:], in_=vr[:]),
                            e.activation(out=v2bf[:], in_=vr[:], func=AF.Square),
                        ]
                    TK["cast"][it] = S.step("scalar", _,
                                            after=[TK["vres"][it]] +
                                            ([TK["quant"][it - 1]] if it else []))

                    def _(e, qn=qn):
                        mms = []
                        for dc in range(8):
                            mms.append(e.matmul(out=ps3[0:1, 0:qn], lhsT=onesb[:, 0:1],
                                                rhs=vbf[:, dc * QT: dc * QT + qn],
                                                start=(dc == 0), stop=False))
                        for dc in range(8):
                            mms.append(e.matmul(out=ps3[0:1, 128:128 + qn], lhsT=onesb[:, 0:1],
                                                rhs=v2bf[:, dc * QT: dc * QT + qn],
                                                start=False, stop=(dc == 7)))
                        return mms
                    TK["stats"][it] = S.step("tensor", _, after=[TK["cast"][it]])

                    def _(e, qn=qn):
                        return [
                            e.tensor_scalar(out=rows_t[0:1, 0:qn], in0=ps3[0:1, 0:qn],
                                            scalar1=1.0 / 1024.0, scalar2=None, op0=ALU.mult),
                            e.drain(),
                            e.tensor_tensor(out=rows_t[0:1, 512:512 + qn], in0=rows_t[0:1, 0:qn],
                                            in1=rows_t[0:1, 0:qn], op=ALU.mult),
                            e.drain(),
                            e.scalar_tensor_tensor(out=rows_t[0:1, 1024:1024 + qn],
                                                   in0=ps3[0:1, 128:128 + qn], scalar=1.0 / 1024.0,
                                                   in1=rows_t[0:1, 512:512 + qn],
                                                   op0=ALU.mult, op1=ALU.subtract),
                            e.drain(),
                            e.tensor_scalar(out=rows_t[0:1, 1024:1024 + qn],
                                            in0=rows_t[0:1, 1024:1024 + qn],
                                            scalar1=EPS_LN, scalar2=None, op0=ALU.add),
                            e.drain(),
                            e.reciprocal(out=rows_t[0:1, 1536:1536 + qn],
                                         in_=rows_t[0:1, 1024:1024 + qn]),
                        ]
                    TK["chain"][it] = S.step("vector", _, after=[TK["stats"][it]])

                    def _(e, qn=qn):
                        return e.activation(out=srcrow[0:1, 0:qn], in_=rows_t[0:1, 1536:1536 + qn],
                                            func=AF.Sqrt)
                    TK["sqrt"][it] = S.step("scalar", _, after=[TK["chain"][it]])

                    def _(e, qn=qn):
                        return e.tensor_tensor(out=srcrow[0:1, 128:128 + qn],
                                               in0=srcrow[0:1, 0:qn], in1=rows_t[0:1, 0:qn],
                                               op=ALU.mult)
                    TK["mr"][it] = S.step("vector", _, after=[TK["sqrt"][it]])

                    def _(e, qn=qn):
                        return [
                            e.matmul(out=ps4[:, 0:qn], lhsT=onesf[:], rhs=srcrow[:, 0:qn],
                                     start=True, stop=False),
                            e.matmul(out=ps4[:, 128:128 + qn], lhsT=onesf[:], rhs=srcrow[:, 128:128 + qn],
                                     start=False, stop=True),
                        ]
                    TK["bc1"][it] = S.step("tensor", _, after=[TK["mr"][it]])

                    def _(e, qn=qn, vr=vr):
                        v3 = vr[:].rearrange("p (c t) -> p c t", c=8)[:, :, 0:qn]
                        x3 = xln1t[:].rearrange("p (c t) -> p c t", c=8)[:, :, 0:qn]
                        xin = xln1t[:].rearrange("p (c t) -> p t c", c=8)[:, 0:qn, :]
                        return [
                            e.tensor_tensor(out=x3, in0=v3,
                                            in1=ps4[:, None, 0:qn].broadcast_to([128, 8, qn]),
                                            op=ALU.mult),
                            e.drain(),
                            e.tensor_tensor(out=x3, in0=x3,
                                            in1=ps4[:, None, 128:128 + qn].broadcast_to([128, 8, qn]),
                                            op=ALU.subtract),
                            e.drain(),
                            e.tensor_reduce(
                                out=Mx[:, 0:qn].rearrange("p (t o) -> p t o", o=1),
                                in_=xin, axis=AX.X, op=ALU.max, apply_absolute_value=True),
                        ]
                    TK["apply"][it] = S.step("vector", _, after=[TK["bc1"][it]])

                    @dma_step(8)
                    def _(e, sc=sc, qn=qn):
                        return [e.dma_start(out=xln1_s[dc * 128:(dc + 1) * 128, sc:sc + qn],
                                            in_=xln1t[:, dc * QT: dc * QT + qn])
                                for dc in range(8)]
                    S.step("gpsimd", _, dma=True, after=[TK["apply"][it]])

                    def _(e, qn=qn):
                        return e.transpose(out=ps4[0:qn, 384:512], in_=Mx[:, 0:qn], identity=ident[:])
                    TK["mxtr"][it] = S.step("tensor", _, after=[TK["apply"][it]])

                    def _(e, qn=qn):
                        return [
                            e.reduce_max(out=cols_t[0:qn, 0:1], in_=ps4[0:qn, 384:512], axis=AX.X),
                            e.drain(),
                            e.tensor_scalar(out=cols_t[0:qn, 1:2], in0=cols_t[0:qn, 0:1],
                                            scalar1=1e-5, scalar2=None, op0=ALU.max),
                            e.drain(),
                            e.tensor_scalar(out=cols_t[0:qn, 1:2], in0=cols_t[0:qn, 1:2],
                                            scalar1=1.0 / 127.0, scalar2=None, op0=ALU.mult),
                            e.drain(),
                            e.reciprocal(out=cols_t[0:qn, 2:3], in_=cols_t[0:qn, 1:2]),
                            e.drain(),
                            e.tensor_tensor(out=cols_t[0:qn, 3:4], in0=cols_t[0:qn, 1:2],
                                            in1=mc1col[0:qn, :], op=ALU.mult),
                        ]
                    TK["cols"][it] = S.step("vector", _, after=[TK["mxtr"][it]])

                    def _(e, qn=qn):
                        return [
                            e.transpose(out=ps3[0:1, 256:256 + qn], in_=cols_t[0:qn, 2:3],
                                        identity=ident[0:qn, 0:qn]),
                            e.transpose(out=ps3[0:1, 384:384 + qn], in_=cols_t[0:qn, 3:4],
                                        identity=ident[0:qn, 0:qn]),
                        ]
                    TK["rowtr"][it] = S.step("tensor", _, after=[TK["cols"][it]])

                    def _(e, qn=qn):
                        return [
                            e.tensor_copy(out=srcrow[0:1, 256:256 + qn], in_=ps3[0:1, 256:256 + qn]),
                            e.tensor_copy(out=srcrow[0:1, 384:384 + qn], in_=ps3[0:1, 384:384 + qn]),
                        ]
                    TK["rowcp"][it] = S.step("vector", _, after=[TK["rowtr"][it]])

                    def _(e, qn=qn):
                        return [
                            e.matmul(out=ps4[:, 256:256 + qn], lhsT=onesf[:],
                                     rhs=srcrow[:, 256:256 + qn], start=True, stop=False),
                            e.matmul(out=ps4[:, 384:384 + qn], lhsT=onesf[:],
                                     rhs=srcrow[:, 384:384 + qn], start=False, stop=True),
                        ]
                    TK["bc2"][it] = S.step("tensor", _, after=[TK["rowcp"][it]])

                    def _(e, qn=qn, it=it):
                        x3 = xln1t[:].rearrange("p (c t) -> p c t", c=8)[:, :, 0:qn]
                        t3 = tq[:].rearrange("p (c t) -> p c t", c=8)[:, :, 0:qn]
                        q3 = xqt[:].rearrange("p (c t) -> p c t", c=8)[:, :, 0:qn]
                        ops = [
                            e.tensor_tensor(out=t3, in0=x3,
                                            in1=ps4[:, None, 256:256 + qn].broadcast_to([128, 8, qn]),
                                            op=ALU.mult),
                            e.drain(),
                            e.tensor_copy(out=tq32[:], in_=tq[:]),
                            e.drain(),
                            e.tensor_copy(out=tq[:], in_=tq32[:]),
                            e.drain(),
                            e.tensor_tensor(out=q3, in0=t3,
                                            in1=ps4[:, None, 384:384 + qn].broadcast_to([128, 8, qn]),
                                            op=ALU.mult),
                        ]
                        if it == 0 or it == n_tiles - 1:
                            col = 0 if it == 0 else qn - 1
                            midx = 0 if it == 0 else 1
                            qm = xqt[:].rearrange("p (c t) -> p c t", c=8)[:, :, col:col + 1]
                            ops.append(e.drain())
                            ops.append(e.tensor_tensor(
                                out=qm, in0=qm,
                                in1=qmask[:, None, midx:midx + 1].broadcast_to([128, 8, 1]),
                                op=ALU.mult))
                        return ops
                    TK["quant"][it] = S.step("vector", _,
                                             after=[TK["bc2"][it]] +
                                             ([TK["qsp"][it - 1]] if it else []))

                    @dma_step(8)
                    def _(e, sc=sc, qn=qn):
                        return [e.dma_start(out=xq_s[dc * 128:(dc + 1) * 128, sc:sc + qn],
                                            in_=xqt[:, dc * QT: dc * QT + qn])
                                for dc in range(8)]
                    TK["qsp"][it] = S.step("gpsimd", _, dma=True, after=[TK["quant"][it]])

                front(0)
                for it in range(1, n_tiles):
                    front(it)
                    tail(it - 1)
                tail(n_tiles - 1)
                S.last = TK["qsp"][n_tiles - 1]

            PHASE_MARKS.append(("A2", S.nsteps))

        S.auto_pump = None
        drain_wq()

        with contextlib.ExitStack() as fctx:
            def fsb(name, shape, dt):
                return fctx.enter_context(nc.sbuf_tensor(name, shape, dt))
            convd = fsb("convd", [128, 96 * 128], BF16)
            convb = fsb("convb", [128, 32], F32)
            xqb = fsb("xqb", [128, 8 * 520], BF16)
            w1buf = fsb("w1buf", [128, 3 * 8 * 512], BF16)
            w2buf = fsb("w2buf", [128, 2 * 4 * 512], BF16)
            h_bf = fsb("h_bf", [128, 32 * HBW], BF16)
            h_silu = fsb("h_silu", [128, 32 * FT], BF16)
            M2 = fsb("M2", [128, 512], F32)
            tbuf = fsb("tbuf", [128, 4 * FT], F32)
            t32 = fsb("t32", [128, 4 * FT], I32)
            d2sb = fsb("d2sb", [128, 512], F32)
            xl1b = fsb("xl1b", [128, 8 * FT], F32)
            outst = fsb("outst", [128, 8 * FT], F32)
            vbf2 = fsb("vbf2", [128, 8 * FT], BF16)

            @dma_step(1)
            def _(e):
                return e.dma_start(out=convb[:], in_=convb_h[:])
            S.step("sync", _, dma=True)

            for part in range(6):
                @dma_step(1)
                def _(e, part=part):
                    return e.dma_start(out=tbuf[:, 0:2048],
                                       in_=convd_h[:, part * 2048:(part + 1) * 2048])
                S.step("sync", _, dma=True)

                def _(e, part=part):
                    return e.copy(out=convd[:, part * 2048:(part + 1) * 2048], in_=tbuf[:, 0:2048])
                S.step("scalar", _)

            tok_ln2_prev = None
            tok_w1dma = {}
            tok_w2dma = {}
            for jt in range(4):
                ts0 = 1 + jt * FT
                fn = FT
                hn = fn + 2

                @dma_step(2)
                def _(e, ts0=ts0, hn=hn, fn=fn):
                    return [
                        e.dma_start(
                            out=xqb[:].rearrange("p (c t) -> p c t", c=8)[:, :, 0:hn],
                            in_=xq_s.rearrange("(c p) t -> p c t", p=128)[:, :, ts0 - 1: ts0 - 1 + hn]),
                        e.dma_start(
                            out=xl1b[:].rearrange("p (c t) -> p c t", c=8),
                            in_=xln1_s.rearrange("(c p) t -> p c t", p=128)[:, :, ts0:ts0 + fn]),
                    ]
                tok_xq = S.step("sync", _, dma=True, sem_key="fx",
                                after=([tok_ln2_prev] if tok_ln2_prev else None))

                mm1_tok = [None] * 32
                evac_tok = [None] * 32
                conv_tok = [None] * 16
                silu_tok = [None] * 16

                def emit_mm1(k):
                    g = k // 4
                    if (k % 4) == 0:
                        @dma_step(1)
                        def _(e, g=g):
                            return e.dma_start(
                                out=w1buf[:, (g % 3) * 4096:(g % 3) * 4096 + 4096]
                                .rearrange("p (c t) -> p c t", c=8),
                                in_=w1q_s.rearrange("(c p) t -> p c t", p=128)[:, :, g * 512:(g + 1) * 512])
                        dep = [tok_xq]
                        if g >= 3:
                            dep.append(mm1_tok[4 * (g - 3) + 3])
                        tok_w1dma[g] = S.step("sync", _, dma=True, after=dep,
                                              sem_key="fw%d" % (g % 3))

                    base = (k % 2) * 1024

                    def _(e, k=k, g=g, base=base):
                        hcl = k % 4
                        wb = (g % 3) * 4096
                        mms = []
                        for dc in range(8):
                            mms.append(e.matmul(
                                out=pC[:, base: base + 258],
                                lhsT=w1buf[:, wb + dc * 512 + hcl * 128: wb + dc * 512 + (hcl + 1) * 128],
                                rhs=xqb[:, dc * 520: dc * 520 + 258],
                                start=(dc == 0), stop=(dc == 7)))
                        for dc in range(8):
                            mms.append(e.matmul(
                                out=pC[:, base + 512: base + 512 + 256],
                                lhsT=w1buf[:, wb + dc * 512 + hcl * 128: wb + dc * 512 + (hcl + 1) * 128],
                                rhs=xqb[:, dc * 520 + 258: dc * 520 + 514],
                                start=(dc == 0), stop=(dc == 7)))
                        return mms
                    dep = [tok_w1dma[g], tok_xq]
                    if k >= 2:
                        dep.append(evac_tok[k - 2])
                    mm1_tok[k] = S.step("tensor", _, after=dep)

                    def _(e, k=k, base=base):
                        hc = k
                        return [
                            e.copy(out=h_bf[:, hc * HBW: hc * HBW + 258], in_=pC[:, base: base + 258]),
                            e.copy(out=h_bf[:, hc * HBW + 258: hc * HBW + 514],
                                   in_=pC[:, base + 512: base + 512 + 256]),
                        ]
                    evac_tok[k] = S.step("scalar", _, after=[mm1_tok[k]])

                def emit_conv(ck):
                    psa, psb = (pA, pB) if (ck % 2) == 0 else (ps3, ps4)

                    def _(e, ck=ck, fn=fn, psa=psa, psb=psb):
                        mms = []
                        for cl in range(2):
                            c = 2 * ck + cl
                            po = psa if cl == 0 else psb
                            for tap in range(3):
                                mms.append(e.matmul(
                                    out=po[:, 0:fn],
                                    lhsT=convd[:, (3 * c + tap) * 128:(3 * c + tap + 1) * 128],
                                    rhs=h_bf[:, c * HBW + tap: c * HBW + tap + fn],
                                    start=(tap == 0), stop=(tap == 2)))
                        return mms
                    dep = [evac_tok[2 * ck + 1]]
                    if ck >= 2:
                        dep.append(silu_tok[ck - 2])
                    if ck < 2 and tok_ln2_prev is not None:
                        dep.append(tok_ln2_prev)
                    conv_tok[ck] = S.step("tensor", _, after=dep)

                    def _(e, ck=ck, fn=fn, psa=psa, psb=psb):
                        return [e.activation(out=h_silu[:, (2 * ck + cl) * FT: (2 * ck + cl) * FT + fn],
                                             in_=(psa if cl == 0 else psb)[:, 0:fn],
                                             func=AF.Silu, bias=convb[:, 2 * ck + cl: 2 * ck + cl + 1],
                                             scale=1.0)
                                for cl in range(2)]
                    silu_tok[ck] = S.step("scalar", _, after=[conv_tok[ck]])

                for k in range(32):
                    emit_mm1(k)
                    if k >= 3 and k % 2 == 1:
                        emit_conv((k - 3) // 2)
                emit_conv(15)

                def _(e, fn=fn):
                    hin = h_silu[:].rearrange("p (c t) -> p t c", c=32)
                    return e.tensor_reduce(
                        out=M2[:].rearrange("p (t o) -> p t o", o=1),
                        in_=hin, axis=AX.X, op=ALU.max, apply_absolute_value=True)
                tok_red = S.step("vector", _, after=[silu_tok[15]])

                def _(e):
                    return [e.transpose(out=ps4[0:128, t * 128:(t + 1) * 128],
                                        in_=M2[:, t * 128:(t + 1) * 128],
                                        identity=ident[:])
                            for t in range(4)]
                tok_tr = S.step("tensor", _, after=[tok_red, silu_tok[15]])

                def _(e):
                    ops = []
                    for t in range(4):
                        ops.append(e.reduce_max(out=cols_t[0:128, 4 + t:5 + t],
                                                in_=ps4[0:128, t * 128:(t + 1) * 128], axis=AX.X))
                    ops += [
                        e.drain(),
                        e.tensor_scalar(out=cols_t[:, 8:12], in0=cols_t[:, 4:8],
                                        scalar1=1e-5, scalar2=None, op0=ALU.max),
                        e.drain(),
                        e.tensor_scalar(out=cols_t[:, 8:12], in0=cols_t[:, 8:12],
                                        scalar1=1.0 / 127.0, scalar2=None, op0=ALU.mult),
                        e.drain(),
                        e.reciprocal(out=cols_t[:, 12:16], in_=cols_t[:, 8:12]),
                        e.drain(),
                        e.tensor_tensor(out=cols_t[:, 4:8], in0=cols_t[:, 8:12],
                                        in1=mc2col[:, 0:1].broadcast_to([128, 4]),
                                        op=ALU.mult),
                    ]
                    return ops
                tok_cols = S.step("vector", _, after=[tok_tr])

                def _(e):
                    return [e.transpose(out=ps3[0:1, t * 128:(t + 1) * 128],
                                        in_=cols_t[:, 12 + t:13 + t], identity=ident[:])
                            for t in range(4)]
                tok_s2t = S.step("tensor", _, after=[tok_cols, silu_tok[15]])

                def _(e):
                    return e.tensor_copy(out=srcrow[0:1, 0:512], in_=ps3[0:1, 0:512])
                tok_s2c = S.step("vector", _, after=[tok_s2t])

                def _(e, fn=fn):
                    return e.matmul(out=ps3[:, 0:fn], lhsT=onesf[:], rhs=srcrow[:, 0:fn],
                                    start=True, stop=True)
                tok_s2b = S.step("tensor", _, after=[tok_s2c])

                def _(e):
                    return [e.transpose(out=ps4[0:1, t * 128:(t + 1) * 128],
                                        in_=cols_t[:, 4 + t:5 + t], identity=ident[:])
                            for t in range(4)]
                tok_d2t = S.step("tensor", _, after=[tok_cols])

                def _(e):
                    return e.tensor_copy(out=srcrow[0:1, 0:512], in_=ps4[0:1, 0:512])
                tok_d2c = S.step("vector", _, after=[tok_d2t, tok_s2b])

                def _(e, fn=fn):
                    return e.matmul(out=ps4[:, 0:fn], lhsT=onesf[:], rhs=srcrow[:, 0:fn],
                                    start=True, stop=True)
                tok_d2b = S.step("tensor", _, after=[tok_d2c])

                def _(e, fn=fn):
                    return e.tensor_copy(out=d2sb[:, 0:fn], in_=ps4[:, 0:fn])
                tok_d2s = S.step("vector", _, after=[tok_d2b])

                qv_tok = [None] * 8
                qs_tok = [None] * 8
                for grp in range(8):
                    def _(e, grp=grp, fn=fn):
                        h3 = h_silu[:].rearrange("p (c t) -> p c t", c=32)[:, 4 * grp:4 * grp + 4, :]
                        t3 = tbuf[:].rearrange("p (c t) -> p c t", c=4)
                        return [
                            e.tensor_tensor(out=t3, in0=h3,
                                            in1=ps3[:, None, 0:fn].broadcast_to([128, 4, fn]),
                                            op=ALU.mult),
                            e.drain(),
                            e.tensor_copy(out=t32[:], in_=tbuf[:]),
                        ]
                    dep = [tok_s2b]
                    if grp >= 1:
                        dep.append(qs_tok[grp - 1])
                    qv_tok[grp] = S.step("vector", _, after=dep)

                    def _(e, grp=grp, fn=fn):
                        h3 = h_bf[:].rearrange("p (c t) -> p c t", t=HBW)[:, 4 * grp:4 * grp + 4, 0:fn]
                        t3 = t32[:].rearrange("p (c t) -> p c t", c=4)
                        return e.copy(out=h3, in_=t3)
                    qs_tok[grp] = S.step("scalar", _,
                                         after=[qv_tok[grp], conv_tok[2 * grp + 1],
                                                mm2_prev_reader.get(grp, None)] if False else
                                         [qv_tok[grp], conv_tok[2 * grp + 1]])

                mm2_tok = {}
                evh_tok = {}
                for half in range(2):
                    for g in range(8):
                        @dma_step(1)
                        def _(e, g=g, half=half):
                            gi = half * 8 + g
                            return e.dma_start(
                                out=w2buf[:, (gi % 2) * 2048:(gi % 2) * 2048 + 2048]
                                .rearrange("p (c t) -> p c t", c=4),
                                in_=w2q_s.rearrange("(c p) t -> p c t", p=128)
                                [:, 4 * g: 4 * (g + 1), half * 512:(half + 1) * 512])
                        gi = half * 8 + g
                        dep = [tok_xq]
                        if gi >= 2:
                            dep.append(mm2_tok[gi - 2])
                        tok_w2dma[gi] = S.step("sync", _, dma=True, after=dep,
                                               sem_key="gw%d" % (gi % 2))

                        def _(e, g=g, half=half, fn=fn):
                            gi = half * 8 + g
                            wb = (gi % 2) * 2048
                            mms = []
                            for hcl in range(4):
                                hc = 4 * g + hcl
                                for ocl in range(4):
                                    mms.append(e.matmul(
                                        out=pC[:, ocl * 512: ocl * 512 + fn],
                                        lhsT=w2buf[:, wb + hcl * 512 + ocl * 128: wb + hcl * 512 + (ocl + 1) * 128],
                                        rhs=h_bf[:, hc * HBW: hc * HBW + fn],
                                        start=(hc == 0), stop=(hc == 31)))
                            return mms
                        dep = [tok_w2dma[gi], qs_tok[g]]
                        if half == 0 and g == 0:
                            dep.append(silu_tok[15])
                            dep.append(evac_tok[31])
                        if half == 1 and g == 0:
                            dep.append(evh_tok[0])
                        mm2_tok[gi] = S.step("tensor", _, after=dep)

                    def _(e, half=half, fn=fn):
                        p3 = pC[:].rearrange("p (c t) -> p c t", t=512)
                        o3 = outst[:].rearrange("p (c t) -> p c t", c=8)[:, half * 4:(half + 1) * 4, :]
                        x3 = xl1b[:].rearrange("p (c t) -> p c t", c=8)[:, half * 4:(half + 1) * 4, :]
                        return [
                            e.tensor_tensor(out=o3, in0=p3,
                                            in1=d2sb[:, None, 0:fn].broadcast_to([128, 4, fn]),
                                            op=ALU.mult),
                            e.drain(),
                            e.tensor_tensor(out=o3, in0=o3, in1=x3, op=ALU.add),
                        ]
                    evh_tok[half] = S.step("vector", _,
                                           after=[mm2_tok[half * 8 + 7], tok_d2s])

                def _(e, fn=fn):
                    return e.tensor_copy(out=vbf2[:], in_=outst[:])
                S.step("vector", _, after=[evh_tok[1]])

                def _(e, fn=fn):
                    return [e.matmul(out=ps3[0:1, 0:fn], lhsT=onesb[:, 0:1],
                                     rhs=vbf2[:, dc * FT: dc * FT + fn],
                                     start=(dc == 0), stop=(dc == 7))
                            for dc in range(8)]
                S.step("tensor", _, extra_waits=[("vector", S.cnt["vector"])])

                def _(e, fn=fn):
                    return e.activation(out=vbf2[:], in_=outst[:], func=AF.Square)
                S.step("scalar", _, extra_waits=[("tensor", S.cnt["tensor"])])

                def _(e, fn=fn):
                    return [e.matmul(out=ps4[0:1, 0:fn], lhsT=onesb[:, 0:1],
                                     rhs=vbf2[:, dc * FT: dc * FT + fn],
                                     start=(dc == 0), stop=(dc == 7))
                            for dc in range(8)]
                S.step("tensor", _)

                def _(e, fn=fn):
                    return [
                        e.tensor_scalar(out=rows_t[0:1, 0:fn], in0=ps3[0:1, 0:fn],
                                        scalar1=1.0 / 1024.0, scalar2=None, op0=ALU.mult),
                        e.drain(),
                        e.tensor_tensor(out=rows_t[0:1, 512:512 + fn], in0=rows_t[0:1, 0:fn],
                                        in1=rows_t[0:1, 0:fn], op=ALU.mult),
                        e.drain(),
                        e.scalar_tensor_tensor(out=rows_t[0:1, 1024:1024 + fn],
                                               in0=ps4[0:1, 0:fn], scalar=1.0 / 1024.0,
                                               in1=rows_t[0:1, 512:512 + fn],
                                               op0=ALU.mult, op1=ALU.subtract),
                        e.drain(),
                        e.tensor_scalar(out=rows_t[0:1, 1024:1024 + fn],
                                        in0=rows_t[0:1, 1024:1024 + fn],
                                        scalar1=EPS_LN, scalar2=None, op0=ALU.add),
                        e.drain(),
                        e.reciprocal(out=rows_t[0:1, 1536:1536 + fn],
                                     in_=rows_t[0:1, 1024:1024 + fn]),
                    ]
                S.step("vector", _)

                def _(e, fn=fn):
                    return e.activation(out=srcrow[0:1, 0:fn], in_=rows_t[0:1, 1536:1536 + fn],
                                        func=AF.Sqrt)
                S.step("scalar", _)

                def _(e, fn=fn):
                    return e.matmul(out=ps4[:, 0:fn], lhsT=onesf[:], rhs=srcrow[:, 0:fn],
                                    start=True, stop=True)
                S.step("tensor", _)

                def _(e, fn=fn):
                    return e.tensor_tensor(out=srcrow[0:1, 0:fn], in0=srcrow[0:1, 0:fn],
                                           in1=rows_t[0:1, 0:fn], op=ALU.mult)
                S.step("vector", _)

                def _(e, fn=fn):
                    return e.matmul(out=ps3[:, 0:fn], lhsT=onesf[:], rhs=srcrow[:, 0:fn],
                                    start=True, stop=True)
                S.step("tensor", _)

                def _(e, fn=fn):
                    o3 = outst[:].rearrange("p (c t) -> p c t", c=8)
                    return [
                        e.tensor_tensor(out=o3, in0=o3,
                                        in1=ps4[:, None, 0:fn].broadcast_to([128, 8, fn]),
                                        op=ALU.mult),
                        e.drain(),
                        e.tensor_tensor(out=o3, in0=o3,
                                        in1=ps3[:, None, 0:fn].broadcast_to([128, 8, fn]),
                                        op=ALU.subtract),
                    ]
                tok_ln2_prev = S.step("vector", _)

                @dma_step(1)
                def _(e, ts0=ts0, fn=fn):
                    return e.dma_start(
                        out=outT_h.rearrange("(c p) t -> p c t", p=128)[:, :, ts0 - 1: ts0 - 1 + fn],
                        in_=outst[:].rearrange("p (c t) -> p c t", c=8))
                S.step("gpsimd", _, dma=True, after=[tok_ln2_prev])
                S.last = tok_ln2_prev

        PHASE_MARKS.append(("FFN", S.nsteps))

        final_g = S.cnt["gpsimd"]

        def _(e):
            e.wait_ge(sems["gpsimd"], final_g)
            return e.engine_nop()
        S.step("gpsimd", _)

        with nc.Block() as block:
            @block.sync
            def _(eng):
                for op in S.ops["sync"]:
                    op(eng)

            @block.vector
            def _(eng):
                for op in S.ops["vector"]:
                    op(eng)

            @block.scalar
            def _(eng):
                for op in S.ops["scalar"]:
                    op(eng)

            @block.tensor
            def _(eng):
                for op in S.ops["tensor"]:
                    op(eng)

            @block.gpsimd
            def _(eng):
                for op in S.ops["gpsimd"]:
                    op(eng)

    return nc


def _band_mask(off, qn):
    m = np.zeros((128, 1024), np.float32)
    for h in range(8):
        mm = np.zeros((128, 128), np.float32)
        i = np.arange(128)[:, None]
        j = np.arange(128)[None, :]
        band = (np.abs(i - j - off) <= 3) & (j < qn)
        mm[band] = 1.0
        m[:, h * 128:(h + 1) * 128] = mm
    return m


def kernel(L0_attn_x, L0_attn_w_qkv, L0_attn_proj, L0_attn_w_out, L0_attn_b_out,
           L0_ffn_w1, L0_ffn_conv_w, L0_ffn_conv_b, L0_ffn_w2):
    x = np.ascontiguousarray(np.asarray(L0_attn_x, dtype=np.float32))
    wqkv = np.ascontiguousarray(np.asarray(L0_attn_w_qkv, dtype=np.float32))
    proj = np.ascontiguousarray(np.asarray(L0_attn_proj, dtype=np.float32))
    wout = np.ascontiguousarray(np.asarray(L0_attn_w_out, dtype=np.float32))
    bout = np.ascontiguousarray(np.asarray(L0_attn_b_out, dtype=np.float32))[None, :]
    w1 = np.asarray(L0_ffn_w1, dtype=np.float32)
    convw = np.asarray(L0_ffn_conv_w, dtype=np.float32)[:, 0, :]
    convb = np.asarray(L0_ffn_conv_b, dtype=np.float32)
    w2 = np.asarray(L0_ffn_w2, dtype=np.float32)

    w1T = np.ascontiguousarray(w1.T)
    w2T = np.ascontiguousarray(w2.T)

    convdiag = np.zeros((128, 96 * 128), np.float32)
    idx = np.arange(128)
    for c in range(32):
        for tap in range(3):
            blk = (3 * c + tap) * 128
            convdiag[idx, blk + idx] = convw[c * 128:(c + 1) * 128, tap]
    convb_m = np.ascontiguousarray(convb.reshape(32, 128).T)

    ident = np.eye(128, dtype=np.float32)
    n_tiles = (NQ + QT - 1) // QT
    qs_last = QS0 + (n_tiles - 1) * QT
    qn_last = QS0 + NQ - qs_last
    off_last = qs_last - min(qs_last - 3, TB - 128)
    bandmask = _band_mask(3, QT)
    bandmask2 = _band_mask(off_last, qn_last)

    in_maps = []
    for core in range(NCORES):
        b, r = core // 4, core % 4
        t0 = r * TOK
        lo, hi = t0 - HB, t0 + TOK + HB
        xT = np.zeros((D, TB), np.float32)
        s, e = max(lo, 0), min(hi, N)
        xT[:, s - lo: s - lo + (e - s)] = x[b, s:e, :].T
        kmask = np.zeros((128, 16), np.float32)
        kmask[:, 0:8] = 1.0 if lo >= 0 else 0.0
        kmask[:, 8:16] = 1.0 if hi <= N else 0.0
        qmask = np.zeros((128, 2), np.float32)
        qmask[:, 0] = 1.0 if t0 - 1 >= 0 else 0.0
        qmask[:, 1] = 1.0 if t0 + TOK + 1 <= N else 0.0
        in_maps.append({
            "xT": xT, "wqkv": wqkv, "proj": proj, "wout": wout, "boutrow": bout,
            "w1T": w1T, "w2T": w2T, "convdiag": convdiag, "convbH": convb_m,
            "identH": ident, "bandmask": bandmask, "bandmask2": bandmask2,
            "kmaskH": kmask, "qmaskH": qmask,
        })

    nc = build_program()
    res = run_bass_kernel_spmd(nc, in_maps, core_ids=list(range(NCORES)))

    out = np.empty((B, N, D), np.float32)
    for core in range(NCORES):
        b, r = core // 4, core % 4
        t0 = r * TOK
        out[b, t0:t0 + TOK, :] = res.results[core]["outT"].T
    return out


# version 22
# speedup vs baseline: 1.0725x; 1.0449x over previous
import numpy as np
import contextlib
import os
import concourse.bass as bass
import concourse.mybir as mybir
from concourse.bass_utils import run_bass_kernel_spmd

F32 = mybir.dt.float32
BF16 = mybir.dt.bfloat16
I32 = mybir.dt.int32
AF = mybir.ActivationFunctionType
ALU = mybir.AluOpType
AX = mybir.AxisListType

B, N, D = 2, 8192, 1024
NCORES = 8
TOK = 2048
HB = 8
TB = TOK + 2 * HB
HEADS, HD, FEAT = 8, 128, 64
DH = 4096
QT = 122
QS0, NQ = 7, 2050
FT = 512
HBW = 520
EPS_LN, EPS_Z = 1e-5, 1e-8

ENG = ("sync", "vector", "scalar", "tensor", "gpsimd")
BUILD_LIMIT = int(os.environ.get("KBISECT", "0")) or None
PHASE_MARKS = []


class Sched:

    def __init__(self, nc, sems):
        self.nc = nc
        self.sems = sems
        self.ops = {e: [] for e in ENG}
        self.cnt = {e: 0 for e in ENG}
        self.last = None
        self.nsteps = 0
        self.auto_pump = None
        self._pumping = False

    def step(self, eng, builder, dma=False, after=None, extra_waits=(), sem_key=None):
        self.nsteps += 1
        if BUILD_LIMIT is not None and self.nsteps > BUILD_LIMIT:
            return self.last
        if after is None:
            deps = [self.last] if self.last is not None else []
        elif isinstance(after, list):
            deps = list(after)
        else:
            deps = [after]
        deps = [d for d in deps if d is not None]
        deps += list(extra_waits)
        per = {}
        for (s, v) in deps:
            if s == eng:
                continue
            per[s] = max(per.get(s, 0), v)
        waits = sorted(per.items())
        key = sem_key if (dma and sem_key is not None) else eng
        sem = self.sems[key]
        if dma:
            nd = getattr(builder, "n_dma")
            self.cnt[key] += 16 * nd
            tok = (key, self.cnt[key])

            def run(e, waits=tuple(waits), builder=builder, nd=nd):
                for (s, v) in waits:
                    e.wait_ge(self.sems[s], v)
                insts = builder(e)
                if not isinstance(insts, (list, tuple)):
                    insts = [insts]
                assert len(insts) == nd
                for i in insts:
                    i.then_inc(sem, 16)

            self.ops[eng].append(run)
        else:
            self.cnt[eng] += 1
            tok = (eng, self.cnt[eng])

            def run(e, waits=tuple(waits), builder=builder, eng=eng):
                for (s, v) in waits:
                    e.wait_ge(self.sems[s], v)
                insts = builder(e)
                if not isinstance(insts, (list, tuple)):
                    insts = [insts]
                if eng in ("vector", "scalar", "gpsimd"):
                    e.drain().then_inc(sem, 1)
                else:
                    insts[-1].then_inc(sem, 1)

            self.ops[eng].append(run)
        self.last = tok
        if self.auto_pump is not None and not self._pumping:
            self.auto_pump()
        return tok


def dma_step(n):
    def deco(fn):
        fn.n_dma = n
        return fn
    return deco


def build_program():
    nc = bass.Bass("TRN2", target_bir_lowering=False)

    xT_h = nc.dram_tensor("xT", [D, TB], F32, kind="ExternalInput").ap()
    wqkv_h = nc.dram_tensor("wqkv", [D, 3 * D], F32, kind="ExternalInput").ap()
    proj_h = nc.dram_tensor("proj", [HD, FEAT], F32, kind="ExternalInput").ap()
    wout_h = nc.dram_tensor("wout", [D, D], F32, kind="ExternalInput").ap()
    bout_h = nc.dram_tensor("boutrow", [1, D], F32, kind="ExternalInput").ap()
    w1T_h = nc.dram_tensor("w1T", [D, DH], F32, kind="ExternalInput").ap()
    w2T_h = nc.dram_tensor("w2T", [DH, D], F32, kind="ExternalInput").ap()
    convd_h = nc.dram_tensor("convdiag", [128, 96 * 128], F32, kind="ExternalInput").ap()
    convb_h = nc.dram_tensor("convbH", [128, 32], F32, kind="ExternalInput").ap()
    ident_h = nc.dram_tensor("identH", [128, 128], F32, kind="ExternalInput").ap()
    bmask_h = nc.dram_tensor("bandmask", [128, 1024], F32, kind="ExternalInput").ap()
    bmask2_h = nc.dram_tensor("bandmask2", [128, 1024], F32, kind="ExternalInput").ap()
    kmask_h = nc.dram_tensor("kmaskH", [128, 16], F32, kind="ExternalInput").ap()
    qmask_h = nc.dram_tensor("qmaskH", [128, 2], F32, kind="ExternalInput").ap()
    outT_h = nc.dram_tensor("outT", [D, TOK], F32, kind="ExternalOutput").ap()
    _skind = "ExternalOutput" if os.environ.get("KDEBUG") else "Internal"
    xln1_s = nc.dram_tensor("xln1_s", [D, NQ], F32, kind=_skind).ap()
    xq_s = nc.dram_tensor("xq_s", [D, NQ], BF16, kind=_skind).ap()
    w1q_s = nc.dram_tensor("w1q_s", [D, DH], BF16, kind=_skind).ap()
    w2q_s = nc.dram_tensor("w2q_s", [DH, D], BF16, kind=_skind).ap()

    ctx = contextlib.ExitStack()
    with ctx:
        def sb(name, shape, dt):
            return ctx.enter_context(nc.sbuf_tensor(name, shape, dt))

        def psum(name, shape):
            return ctx.enter_context(nc.psum_tensor(name, shape, F32))

        ident = sb("ident", [128, 128], F32)
        onesb = sb("onesb", [128, 1024], BF16)
        onesf = sb("onesf", [128, 128], F32)
        eps128 = sb("eps128", [128, 128], BF16)
        c32 = sb("c32", [32, 520], BF16)
        pp = sb("pp", [128, 192], BF16)
        boutpad = sb("boutpad", [32, 1024], BF16)
        minus1 = sb("minus1", [128, 1], F32)
        c1024 = sb("c1024", [128, 1], F32)
        rows_t = sb("rows_t", [1, 2048], F32)
        srcrow = sb("srcrow", [128, 512], F32)
        cols_t = sb("cols_t", [128, 16], F32)
        mc1col = sb("mc1col", [128, 1], F32)
        mc2col = sb("mc2col", [128, 1], F32)
        ws1col = sb("ws1col", [128, 1], F32)
        ws2col = sb("ws2col", [128, 1], F32)
        wtot = sb("wtot", [1, 8], F32)
        kmask = sb("kmask", [128, 16], F32)
        qmask = sb("qmask", [128, 2], F32)

        pA = psum("pA", [128, 512])
        pB = psum("pB", [128, 512])
        pC = psum("pC", [128, 2048])
        ps3 = psum("ps3", [128, 512])
        ps4 = psum("ps4", [128, 512])

        sems = {}
        for e in ENG + ("wq0", "wq1", "wq2", "wq3", "fx", "fw0", "fw1", "fw2", "gw0", "gw1", "gw2", "ax"):
            sems[e] = ctx.enter_context(nc.semaphore("sem_" + e))
        S = Sched(nc, sems)
        for e in ("wq0", "wq1", "wq2", "wq3", "fx", "fw0", "fw1", "fw2", "gw0", "gw1", "gw2", "ax"):
            S.cnt[e] = 0

        @dma_step(4)
        def _(e):
            return [
                e.dma_start(out=ident[:], in_=ident_h[:]),
                e.dma_start(out=kmask[:], in_=kmask_h[:]),
                e.dma_start(out=qmask[:], in_=qmask_h[:]),
                e.dma_start(out=srcrow[:, 0:64], in_=proj_h[:]),
            ]
        S.step("sync", _, dma=True)

        def _(e):
            return [
                e.memset(onesb[:], 1.0),
                e.memset(onesf[:], 1.0),
                e.memset(eps128[:], EPS_Z / 128.0),
                e.memset(c32[:], 1.0 / 32.0),
                e.memset(pp[:], 0.0),
                e.memset(boutpad[:], 0.0),
                e.memset(minus1[:], -1.0),
                e.memset(c1024[:], 1.0 / 1024.0),
                e.memset(rows_t[:], 0.0),
                e.memset(cols_t[:], 0.0),
                e.drain(),
                e.tensor_copy(out=pp[:, 0:64], in_=srcrow[:, 0:64]),
                e.tensor_copy(out=pp[:, 128:192], in_=srcrow[:, 0:64]),
                e.drain(),
                e.memset(srcrow[:], 0.0),
            ]
        S.step("vector", _)

        PHASE_MARKS.append(("P0", S.nsteps))

        wq_done = [None, None]
        wq_mean = [None, None]

        def wquant_gen(wT_h, wq_s, n_iters, width, acc_base, mccol, wscol, slot,
                       rows128, colw, gidx):
            stg = [stageW, stageW2, stageW3, stageW4]
            tok_dma = [None] * n_iters
            tok_abs = [None] * n_iters

            def src_ap(it):
                rb, cb = it // colw, it % colw
                return wT_h[rb * 128:(rb + 1) * 128, cb * width:(cb + 1) * width]

            def dst_ap(it):
                rb, cb = it // colw, it % colw
                return wq_s[rb * 128:(rb + 1) * 128, cb * width:(cb + 1) * width]

            for it in range(n_iters):
                @dma_step(1)
                def _(e, it=it):
                    return e.dma_start(out=stg[it % 4][:, 0:width], in_=src_ap(it))
                if it >= 4:
                    dep = [tok_abs[it - 4]]
                elif gidx == 1:
                    dep = [wq_done[0]]
                else:
                    dep = [S.last]
                tok_dma[it] = S.step("sync", _, dma=True, after=dep,
                                     sem_key="wq%d" % (it % 4))

                def _(e, it=it):
                    return e.activation(out=wtmp[:, 0:width], in_=stg[it % 4][:, 0:width],
                                        func=AF.Abs, accum_out=wacc[:, acc_base + it: acc_base + it + 1])
                yield
                tok_abs[it] = S.step("scalar", _, after=[tok_dma[it]])
                yield

            def _(e):
                return e.reduce_sum(out=cols_t[:, 8:9], in_=wacc[:, acc_base:acc_base + n_iters], axis=AX.X)
            S.step("vector", _, after=[tok_abs[n_iters - 1]])

            def _(e):
                return e.transpose(out=ps3[0:1, 0:128], in_=cols_t[:, 8:9], identity=ident[:])
            S.step("tensor", _, after=[S.last])

            def _(e):
                return [
                    e.reduce_sum(out=wtot[0:1, slot:slot + 1], in_=ps3[0:1, 0:128], axis=AX.X),
                    e.drain(),
                    e.tensor_scalar(out=wtot[0:1, slot + 1:slot + 2], in0=wtot[0:1, slot:slot + 1],
                                    scalar1=1.0 / (n_iters * 128.0 * width), scalar2=None,
                                    op0=ALU.mult),
                    e.drain(),
                    e.tensor_scalar(out=wtot[0:1, slot + 1:slot + 2], in0=wtot[0:1, slot + 1:slot + 2],
                                    scalar1=1e-5, scalar2=None, op0=ALU.max),
                    e.drain(),
                    e.reciprocal(out=wtot[0:1, slot + 2:slot + 3], in_=wtot[0:1, slot + 1:slot + 2]),
                    e.drain(),
                    e.tensor_copy(out=srcrow[0:1, 0:1], in_=wtot[0:1, slot + 1:slot + 2]),
                    e.tensor_copy(out=srcrow[0:1, 1:2], in_=wtot[0:1, slot + 2:slot + 3]),
                ]
            S.step("vector", _, after=[S.last])

            def _(e):
                return e.matmul(out=ps4[:, 0:2], lhsT=onesf[:], rhs=srcrow[:, 0:2],
                                start=True, stop=True)
            S.step("tensor", _, after=[S.last])

            def _(e):
                return [
                    e.tensor_copy(out=mccol[:], in_=ps4[:, 0:1]),
                    e.tensor_copy(out=wscol[:], in_=ps4[:, 1:2]),
                ]
            tok_sc = S.step("vector", _, after=[S.last])
            wq_mean[gidx] = tok_sc
            yield

            tok_v = [None] * n_iters
            tok_s = [None] * n_iters
            tok_g = [None] * n_iters
            for it in range(n_iters):
                @dma_step(1)
                def _(e, it=it):
                    return e.dma_start(out=stg[it % 4][:, 0:width], in_=src_ap(it))
                tok_dma[it] = S.step("sync", _, dma=True,
                                     after=[tok_v[it - 4]] if it >= 4 else [tok_sc],
                                     sem_key="wq%d" % (it % 4))
                yield

                def _(e, it=it):
                    return e.activation(out=wtmp[:, 0:width], in_=stg[it % 4][:, 0:width],
                                        func=AF.Identity, scale=wscol[:])
                tok_sa = S.step("scalar", _,
                                after=[tok_dma[it], tok_sc] + ([tok_v[it - 1]] if it >= 1 else []))

                def _(e, it=it):
                    return e.tensor_scalar(out=w32[:, 0:width], in0=wtmp[:, 0:width],
                                           scalar1=1.0, scalar2=-1.0,
                                           op0=ALU.min, op1=ALU.max)
                tok_v[it] = S.step("vector", _,
                                   after=[tok_sa] + ([tok_s[it - 1]] if it >= 1 else []))

                def _(e, it=it):
                    return e.copy(out=wqout[:, 0:width], in_=w32[:, 0:width])
                tok_s[it] = S.step("scalar", _,
                                   after=[tok_v[it]] + ([tok_g[it - 1]] if it >= 1 else []))

                @dma_step(1)
                def _(e, it=it):
                    return e.dma_start(out=dst_ap(it), in_=wqout[:, 0:width])
                tok_g[it] = S.step("gpsimd", _, dma=True, after=[tok_s[it]])
                yield
            wq_done[gidx] = tok_g[n_iters - 1]

        wq_gens = [None, None]

        def pump_wq(n):
            saved = S.last
            S._pumping = True
            for _ in range(n):
                for gidx in range(2):
                    if wq_gens[gidx] is not None:
                        try:
                            next(wq_gens[gidx])
                            break
                        except StopIteration:
                            wq_gens[gidx] = None
            S._pumping = False
            S.last = saved

        def drain_wq():
            saved = S.last
            S._pumping = True
            for gidx in range(2):
                if wq_gens[gidx] is not None:
                    for _ in wq_gens[gidx]:
                        pass
                    wq_gens[gidx] = None
            S._pumping = False
            S.last = saved

            def _(e):
                return e.engine_nop()
            S.step("vector", _, after=[saved] + [t for t in wq_done if t is not None])

        PHASE_MARKS.append(("P1-wquant", S.nsteps))

        with contextlib.ExitStack() as actx:
            x_bf = actx.enter_context(nc.sbuf_tensor("x_bf", [128, 8 * TB], BF16))
            qphi = actx.enter_context(nc.sbuf_tensor("qphi", [128, 4 * TB], BF16))
            kphi = actx.enter_context(nc.sbuf_tensor("kphi", [128, 4 * TB], BF16))
            stage = actx.enter_context(nc.sbuf_tensor("stageA", [128, 1024], F32))
            wacc = actx.enter_context(nc.sbuf_tensor("wacc", [128, 80], F32))
            stageW = actx.enter_context(nc.sbuf_tensor("stageW", [128, 1024], F32))
            stageW2 = actx.enter_context(nc.sbuf_tensor("stageW2", [128, 1024], F32))
            stageW3 = actx.enter_context(nc.sbuf_tensor("stageW3", [128, 1024], F32))
            stageW4 = actx.enter_context(nc.sbuf_tensor("stageW4", [128, 1024], F32))
            wtmp = actx.enter_context(nc.sbuf_tensor("wtmp", [128, 1024], F32))
            w32 = actx.enter_context(nc.sbuf_tensor("w32", [128, 1024], I32))
            wqout = actx.enter_context(nc.sbuf_tensor("wqout", [128, 1024], BF16))
            wq_gens[0] = wquant_gen(w1T_h, w1q_s, 32, 1024, 0, mc1col, ws1col, 0, 8, 4, 0)
            wq_gens[1] = wquant_gen(w2T_h, w2q_s, 32, 1024, 32, mc2col, ws2col, 4, 32, 1, 1)
            bmask = actx.enter_context(nc.sbuf_tensor("bmask", [128, 1024], F32))
            bmask2 = actx.enter_context(nc.sbuf_tensor("bmask2", [128, 1024], F32))

            @dma_step(3)
            def _(e):
                return [e.dma_start(out=bmask[:], in_=bmask_h[:]),
                        e.dma_start(out=bmask2[:], in_=bmask2_h[:]),
                        e.dma_start(out=stage[0:1, 0:1024], in_=bout_h[:])]
            S.step("sync", _, dma=True)

            def _(e):
                return e.tensor_copy(out=boutpad[0:1, :], in_=stage[0:1, 0:1024])
            S.step("vector", _)

            with contextlib.ExitStack() as a1ctx:
                wqk = a1ctx.enter_context(nc.sbuf_tensor("wqk", [128, 8 * 2048], BF16))
                xres = a1ctx.enter_context(nc.sbuf_tensor("xres", [128, 8 * 528], F32))
                qtmp = a1ctx.enter_context(nc.sbuf_tensor("qtmp", [128, 8 * 512], BF16))
                for half in range(16):
                    @dma_step(1)
                    def _(e, half=half):
                        dc, hh = half // 2, half % 2
                        return e.dma_start(out=stage[:, 0:1024],
                                           in_=wqkv_h[dc * 128:(dc + 1) * 128, hh * 1024:(hh + 1) * 1024])
                    S.step("sync", _, dma=True)

                    def _(e, half=half):
                        return e.copy(out=wqk[:, half * 1024:(half + 1) * 1024], in_=stage[:, 0:1024])
                    S.step("scalar", _)

                S.auto_pump = lambda: pump_wq(3)
                tiles = [(t, min(512, TB - t)) for t in range(0, TB, 512)]
                for (ts, tl) in tiles:
                    @dma_step(1)
                    def _(e, ts=ts, tl=tl):
                        return e.dma_start(
                            out=xres[:].rearrange("p (c t) -> p c t", c=8)[:, :, 0:tl],
                            in_=xT_h.rearrange("(c p) t -> p c t", p=128)[:, :, ts:ts + tl])
                    S.step("sync", _, dma=True)

                    def _(e, ts=ts, tl=tl):
                        return e.tensor_copy(
                            out=x_bf[:].rearrange("p (c t) -> p c t", c=8)[:, :, ts:ts + tl],
                            in_=xres[:].rearrange("p (c t) -> p c t", c=8)[:, :, 0:tl])
                    S.step("vector", _)

                    for (wbase, phidst) in ((0, qphi), (1024, kphi)):
                        for hb in range(2):
                            def _(e, hb=hb, wbase=wbase, ts=ts, tl=tl):
                                mms = []
                                for hl in range(4):
                                    h = hb * 4 + hl
                                    for dc in range(8):
                                        mms.append(e.matmul(
                                            out=pC[:, hl * 512: hl * 512 + tl],
                                            lhsT=wqk[:, dc * 2048 + wbase + h * 128: dc * 2048 + wbase + (h + 1) * 128],
                                            rhs=x_bf[:, dc * TB + ts: dc * TB + ts + tl],
                                            start=(dc == 0), stop=(dc == 7)))
                                return mms
                            S.step("tensor", _)

                            def _(e, hb=hb, tl=tl):
                                return [e.copy(out=qtmp[:, (hb * 4 + hl) * 512: (hb * 4 + hl) * 512 + tl],
                                               in_=pC[:, hl * 512: hl * 512 + tl])
                                        for hl in range(4)]
                            S.step("scalar", _)

                        def _(e, ts=ts, tl=tl):
                            mms = []
                            for hp in range(4):
                                mms.append(e.matmul(out=pC[:, hp * 512: hp * 512 + tl], lhsT=pp[:, 0:128],
                                                    rhs=qtmp[:, hp * 512: hp * 512 + tl],
                                                    start=True, stop=False))
                                mms.append(e.matmul(out=pC[:, hp * 512: hp * 512 + tl], lhsT=pp[:, 64:192],
                                                    rhs=qtmp[:, (hp + 4) * 512: (hp + 4) * 512 + tl],
                                                    start=False, stop=False))
                                mms.append(e.matmul(out=pC[:, hp * 512: hp * 512 + tl],
                                                    lhsT=onesb[0:32, 0:128],
                                                    rhs=c32[0:32, 0:tl], start=False, stop=True))
                            return mms
                        S.step("tensor", _)

                        def _(e, tl=tl):
                            return [e.activation(out=xres[:, hp * 1056: hp * 1056 + tl],
                                                 in_=pC[:, hp * 512: hp * 512 + tl],
                                                 func=AF.Exp, bias=minus1[:], scale=1.0)
                                    for hp in range(4)]
                        S.step("scalar", _)

                        def _(e, ts=ts, tl=tl, phidst=phidst):
                            return [e.scalar_tensor_tensor(
                                out=phidst[:, hp * TB + ts: hp * TB + ts + tl],
                                in0=xres[:, hp * 1056: hp * 1056 + tl], scalar=1.0,
                                in1=pC[:, hp * 512: hp * 512 + tl],
                                op0=ALU.min, op1=ALU.max) for hp in range(4)]
                        S.step("vector", _)

                def _(e):
                    km3 = kphi[:].rearrange("p (c t) -> p c t", c=4)
                    return [
                        e.tensor_tensor(out=km3[:, :, 0:HB], in0=km3[:, :, 0:HB],
                                        in1=kmask[:, None, 0:8].broadcast_to([128, 4, 8]),
                                        op=ALU.mult),
                        e.tensor_tensor(out=km3[:, :, TB - HB:TB], in0=km3[:, :, TB - HB:TB],
                                        in1=kmask[:, None, 8:16].broadcast_to([128, 4, 8]),
                                        op=ALU.mult),
                    ]
                S.step("vector", _)
                while wq_mean[1] is None and wq_gens[1] is not None:
                    pump_wq(1)
                S.auto_pump = lambda: pump_wq(1)

                def _(e):
                    return e.engine_nop()
                S.step("vector", _,
                       after=[S.last] + [t for t in wq_mean if t is not None])

            PHASE_MARKS.append(("A1", S.nsteps))

            with contextlib.ExitStack() as a2ctx:
                def a2sb(name, shape, dt):
                    return a2ctx.enter_context(nc.sbuf_tensor(name, shape, dt))
                wv = a2sb("wv", [128, 8 * 1024], BF16)
                woutb = a2sb("woutb", [128, 8 * 1024], BF16)
                v_sb = a2sb("v_sb", [128, 1024], BF16)
                Sm = a2sb("Sm", [128, 1024], BF16)
                zinv = a2sb("zinv", [128, 1024], F32)
                attn_cp = a2sb("attn_cp", [128, 1024], F32)
                attn_sb = a2sb("attn_sb", [128, 1024], BF16)
                vres = a2sb("vres", [128, 976], F32)
                vresB = a2sb("vresB", [128, 976], F32)
                xresA2 = a2sb("xresA2", [128, 976], F32)
                vbf = a2sb("vbf", [128, 976], BF16)
                v2bf = a2sb("v2bf", [128, 976], BF16)
                xln1t = a2sb("xln1t", [128, 976], F32)
                tq = a2sb("tq", [128, 976], F32)
                tq32 = a2sb("tq32", [128, 976], I32)
                xqt = a2sb("xqt", [128, 976], BF16)
                Mx = a2sb("Mx", [128, 976], F32)

                for (dst, src_h, cbase) in ((wv, wqkv_h, 2048), (woutb, wout_h, 0)):
                    for dc in range(8):
                        @dma_step(1)
                        def _(e, dc=dc, src_h=src_h, cbase=cbase):
                            return e.dma_start(out=stage[:, 0:1024],
                                               in_=src_h[dc * 128:(dc + 1) * 128, cbase:cbase + 1024])
                        S.step("sync", _, dma=True)

                        def _(e, dc=dc, dst=dst):
                            return e.copy(out=dst[:, dc * 1024:(dc + 1) * 1024], in_=stage[:, 0:1024])
                        S.step("scalar", _)

                n_tiles = (NQ + QT - 1) // QT
                t0_tok = S.last
                TK = {k: [None] * n_tiles for k in (
                    "xdma", "vmm", "vevac", "smm", "mask", "zmm", "recip", "amm",
                    "acp", "zmul", "oproj", "vres", "cast", "stats", "chain",
                    "sqrt", "mr", "bc1", "apply", "mxtr", "cols", "rowtr",
                    "rowcp", "bc2", "quant", "qsp")}
                vres2 = [vres, vresB]

                def front(it):
                    qs = QS0 + it * QT
                    qn = min(QT, QS0 + NQ - qs)
                    hs = min(qs - 3, TB - 128)
                    mask = bmask if it < n_tiles - 1 else bmask2
                    vr = vres2[it % 2]

                    @dma_step(1)
                    def _(e, qs=qs, qn=qn):
                        return e.dma_start(
                            out=xresA2[:].rearrange("p (c t) -> p c t", c=8)[:, :, 0:qn],
                            in_=xT_h.rearrange("(c p) t -> p c t", p=128)[:, :, qs:qs + qn])
                    TK["xdma"][it] = S.step("sync", _, dma=True, sem_key="ax",
                                            after=[TK["vres"][it - 1]] if it else [t0_tok])

                    def _(e, hs=hs):
                        mms = []
                        for dc in range(8):
                            for hb in range(2):
                                mms.append(e.matmul(
                                    out=pC[:, hb * 512:(hb + 1) * 512],
                                    lhsT=x_bf[:, dc * TB + hs: dc * TB + hs + 128],
                                    rhs=wv[:, dc * 1024 + hb * 512: dc * 1024 + (hb + 1) * 512],
                                    start=(dc == 0), stop=(dc == 7)))
                        return mms
                    TK["vmm"][it] = S.step("tensor", _,
                                           after=[TK["vres"][it - 1]] if it else [t0_tok])

                    def _(e):
                        return e.tensor_copy(out=v_sb[:], in_=pC[:, 0:1024])
                    TK["vevac"][it] = S.step("vector", _, after=[TK["vmm"][it]])

                    def _(e, hs=hs, qs=qs, qn=qn):
                        mms = []
                        for h in range(HEADS):
                            par = (h // 4) * 64
                            blk = (h % 4) * TB
                            mms.append(e.matmul(
                                out=pC[:, 1024 + h * 128: 1024 + h * 128 + qn],
                                lhsT=kphi[par:par + 64, blk + hs: blk + hs + 128],
                                rhs=qphi[par:par + 64, blk + qs: blk + qs + qn],
                                start=(h % 4 == 0), stop=(h % 4 == 3),
                                tile_position=(par, 0)))
                        return mms
                    TK["smm"][it] = S.step("tensor", _,
                                           after=[TK["recip"][it - 1]] if it else [t0_tok])

                    def _(e, mask=mask):
                        return e.tensor_tensor(out=Sm[:], in0=pC[:, 1024:2048],
                                               in1=mask[:], op=ALU.mult)
                    TK["mask"][it] = S.step("vector", _, after=[TK["smm"][it]])

                    def _(e, qn=qn):
                        mms = []
                        for h in range(HEADS):
                            mms.append(e.matmul(
                                out=pC[:, 1024 + h * 128: 1024 + h * 128 + qn],
                                lhsT=onesb[:, 0:128], rhs=Sm[:, h * 128: h * 128 + qn],
                                start=(h % 4 == 0), stop=False))
                            mms.append(e.matmul(
                                out=pC[:, 1024 + h * 128: 1024 + h * 128 + qn],
                                lhsT=onesb[:, 0:128], rhs=eps128[:, 0:qn],
                                start=False, stop=(h % 4 == 3)))
                        return mms
                    TK["zmm"][it] = S.step("tensor", _, after=[TK["mask"][it]])

                    def _(e):
                        return e.reciprocal(out=zinv[:], in_=pC[:, 1024:2048])
                    TK["recip"][it] = S.step("vector", _, after=[TK["zmm"][it]])

                    def _(e, qn=qn):
                        mms = []
                        for h in range(HEADS):
                            dst = pA if h < 4 else pB
                            off = (h % 4) * 128
                            mms.append(e.matmul(
                                out=dst[:, off: off + qn],
                                lhsT=v_sb[:, h * 128:(h + 1) * 128],
                                rhs=Sm[:, h * 128: h * 128 + qn],
                                start=(h % 4 == 0), stop=(h % 4 == 3)))
                        return mms
                    TK["amm"][it] = S.step("tensor", _,
                                           after=[TK["mask"][it], TK["vevac"][it]] +
                                           ([TK["acp"][it - 1]] if it else []))

                    def _(e):
                        return [e.copy(out=attn_cp[:, 0:512], in_=pA[:]),
                                e.copy(out=attn_cp[:, 512:1024], in_=pB[:])]
                    TK["acp"][it] = S.step("scalar", _, after=[TK["amm"][it]])

                    def _(e):
                        return e.tensor_tensor(out=attn_sb[:], in0=attn_cp[:],
                                               in1=zinv[:], op=ALU.mult)
                    TK["zmul"][it] = S.step("vector", _, after=[TK["acp"][it]])

                    def _(e, qn=qn):
                        mms = []
                        for h in range(HEADS):
                            for oc in range(8):
                                mms.append(e.matmul(
                                    out=pC[:, oc * 128: oc * 128 + qn],
                                    lhsT=woutb[:, h * 1024 + oc * 128: h * 1024 + (oc + 1) * 128],
                                    rhs=attn_sb[:, h * 128: h * 128 + qn],
                                    start=(h == 0 and oc % 4 == 0), stop=False))
                        for oc in range(8):
                            mms.append(e.matmul(
                                out=pC[:, oc * 128: oc * 128 + qn],
                                lhsT=boutpad[0:32, oc * 128:(oc + 1) * 128],
                                rhs=onesb[0:32, 0:qn],
                                start=False, stop=(oc % 4 == 3)))
                        return mms
                    TK["oproj"][it] = S.step("tensor", _,
                                             after=[TK["zmul"][it], TK["vevac"][it]])

                    def _(e, qn=qn, vr=vr):
                        return e.tensor_tensor(
                            out=vr[:].rearrange("p (c t) -> p c t", c=8)[:, :, 0:qn],
                            in0=pC[:, 0:1024].rearrange("p (c t) -> p c t", t=128)[:, :, 0:qn],
                            in1=xresA2[:].rearrange("p (c t) -> p c t", c=8)[:, :, 0:qn],
                            op=ALU.add)
                    TK["vres"][it] = S.step("vector", _,
                                            after=[TK["oproj"][it], TK["xdma"][it]] +
                                            ([TK["apply"][it - 2]] if it >= 2 else []))

                def tail(it):
                    qs = QS0 + it * QT
                    qn = min(QT, QS0 + NQ - qs)
                    sc = qs - QS0
                    vr = vres2[it % 2]

                    def _(e, qn=qn, vr=vr):
                        return [
                            e.copy(out=vbf[:], in_=vr[:]),
                            e.activation(out=v2bf[:], in_=vr[:], func=AF.Square),
                        ]
                    TK["cast"][it] = S.step("scalar", _,
                                            after=[TK["vres"][it]] +
                                            ([TK["quant"][it - 1]] if it else []))

                    def _(e, qn=qn):
                        mms = []
                        for dc in range(8):
                            mms.append(e.matmul(out=ps3[0:1, 0:qn], lhsT=onesb[:, 0:1],
                                                rhs=vbf[:, dc * QT: dc * QT + qn],
                                                start=(dc == 0), stop=False))
                        for dc in range(8):
                            mms.append(e.matmul(out=ps3[0:1, 128:128 + qn], lhsT=onesb[:, 0:1],
                                                rhs=v2bf[:, dc * QT: dc * QT + qn],
                                                start=False, stop=(dc == 7)))
                        return mms
                    TK["stats"][it] = S.step("tensor", _, after=[TK["cast"][it]])

                    def _(e, qn=qn):
                        return [
                            e.tensor_scalar(out=rows_t[0:1, 0:qn], in0=ps3[0:1, 0:qn],
                                            scalar1=1.0 / 1024.0, scalar2=None, op0=ALU.mult),
                            e.drain(),
                            e.tensor_tensor(out=rows_t[0:1, 512:512 + qn], in0=rows_t[0:1, 0:qn],
                                            in1=rows_t[0:1, 0:qn], op=ALU.mult),
                            e.drain(),
                            e.scalar_tensor_tensor(out=rows_t[0:1, 1024:1024 + qn],
                                                   in0=ps3[0:1, 128:128 + qn], scalar=1.0 / 1024.0,
                                                   in1=rows_t[0:1, 512:512 + qn],
                                                   op0=ALU.mult, op1=ALU.subtract),
                            e.drain(),
                            e.tensor_scalar(out=rows_t[0:1, 1024:1024 + qn],
                                            in0=rows_t[0:1, 1024:1024 + qn],
                                            scalar1=EPS_LN, scalar2=None, op0=ALU.add),
                            e.drain(),
                            e.reciprocal(out=rows_t[0:1, 1536:1536 + qn],
                                         in_=rows_t[0:1, 1024:1024 + qn]),
                        ]
                    TK["chain"][it] = S.step("vector", _, after=[TK["stats"][it]])

                    def _(e, qn=qn):
                        return e.activation(out=srcrow[0:1, 0:qn], in_=rows_t[0:1, 1536:1536 + qn],
                                            func=AF.Sqrt)
                    TK["sqrt"][it] = S.step("scalar", _, after=[TK["chain"][it]])

                    def _(e, qn=qn):
                        return e.tensor_tensor(out=srcrow[0:1, 128:128 + qn],
                                               in0=srcrow[0:1, 0:qn], in1=rows_t[0:1, 0:qn],
                                               op=ALU.mult)
                    TK["mr"][it] = S.step("vector", _, after=[TK["sqrt"][it]])

                    def _(e, qn=qn):
                        return [
                            e.matmul(out=ps4[:, 0:qn], lhsT=onesf[:], rhs=srcrow[:, 0:qn],
                                     start=True, stop=False),
                            e.matmul(out=ps4[:, 128:128 + qn], lhsT=onesf[:], rhs=srcrow[:, 128:128 + qn],
                                     start=False, stop=True),
                        ]
                    TK["bc1"][it] = S.step("tensor", _, after=[TK["mr"][it]])

                    def _(e, qn=qn, vr=vr):
                        v3 = vr[:].rearrange("p (c t) -> p c t", c=8)[:, :, 0:qn]
                        x3 = xln1t[:].rearrange("p (c t) -> p c t", c=8)[:, :, 0:qn]
                        xin = xln1t[:].rearrange("p (c t) -> p t c", c=8)[:, 0:qn, :]
                        return [
                            e.tensor_tensor(out=x3, in0=v3,
                                            in1=ps4[:, None, 0:qn].broadcast_to([128, 8, qn]),
                                            op=ALU.mult),
                            e.drain(),
                            e.tensor_tensor(out=x3, in0=x3,
                                            in1=ps4[:, None, 128:128 + qn].broadcast_to([128, 8, qn]),
                                            op=ALU.subtract),
                            e.drain(),
                            e.tensor_reduce(
                                out=Mx[:, 0:qn].rearrange("p (t o) -> p t o", o=1),
                                in_=xin, axis=AX.X, op=ALU.max, apply_absolute_value=True),
                        ]
                    TK["apply"][it] = S.step("vector", _, after=[TK["bc1"][it]])

                    @dma_step(8)
                    def _(e, sc=sc, qn=qn):
                        return [e.dma_start(out=xln1_s[dc * 128:(dc + 1) * 128, sc:sc + qn],
                                            in_=xln1t[:, dc * QT: dc * QT + qn])
                                for dc in range(8)]
                    S.step("gpsimd", _, dma=True, after=[TK["apply"][it]])

                    def _(e, qn=qn):
                        return e.transpose(out=ps4[0:qn, 384:512], in_=Mx[:, 0:qn], identity=ident[:])
                    TK["mxtr"][it] = S.step("tensor", _, after=[TK["apply"][it]])

                    def _(e, qn=qn):
                        return [
                            e.reduce_max(out=cols_t[0:qn, 0:1], in_=ps4[0:qn, 384:512], axis=AX.X),
                            e.drain(),
                            e.tensor_scalar(out=cols_t[0:qn, 1:2], in0=cols_t[0:qn, 0:1],
                                            scalar1=1e-5, scalar2=None, op0=ALU.max),
                            e.drain(),
                            e.tensor_scalar(out=cols_t[0:qn, 1:2], in0=cols_t[0:qn, 1:2],
                                            scalar1=1.0 / 127.0, scalar2=None, op0=ALU.mult),
                            e.drain(),
                            e.reciprocal(out=cols_t[0:qn, 2:3], in_=cols_t[0:qn, 1:2]),
                            e.drain(),
                            e.tensor_tensor(out=cols_t[0:qn, 3:4], in0=cols_t[0:qn, 1:2],
                                            in1=mc1col[0:qn, :], op=ALU.mult),
                        ]
                    TK["cols"][it] = S.step("vector", _, after=[TK["mxtr"][it]])

                    def _(e, qn=qn):
                        return [
                            e.transpose(out=ps3[0:1, 256:256 + qn], in_=cols_t[0:qn, 2:3],
                                        identity=ident[0:qn, 0:qn]),
                            e.transpose(out=ps3[0:1, 384:384 + qn], in_=cols_t[0:qn, 3:4],
                                        identity=ident[0:qn, 0:qn]),
                        ]
                    TK["rowtr"][it] = S.step("tensor", _, after=[TK["cols"][it]])

                    def _(e, qn=qn):
                        return [
                            e.tensor_copy(out=srcrow[0:1, 256:256 + qn], in_=ps3[0:1, 256:256 + qn]),
                            e.tensor_copy(out=srcrow[0:1, 384:384 + qn], in_=ps3[0:1, 384:384 + qn]),
                        ]
                    TK["rowcp"][it] = S.step("vector", _, after=[TK["rowtr"][it]])

                    def _(e, qn=qn):
                        return [
                            e.matmul(out=ps4[:, 256:256 + qn], lhsT=onesf[:],
                                     rhs=srcrow[:, 256:256 + qn], start=True, stop=False),
                            e.matmul(out=ps4[:, 384:384 + qn], lhsT=onesf[:],
                                     rhs=srcrow[:, 384:384 + qn], start=False, stop=True),
                        ]
                    TK["bc2"][it] = S.step("tensor", _, after=[TK["rowcp"][it]])

                    def _(e, qn=qn, it=it):
                        x3 = xln1t[:].rearrange("p (c t) -> p c t", c=8)[:, :, 0:qn]
                        t3 = tq[:].rearrange("p (c t) -> p c t", c=8)[:, :, 0:qn]
                        q3 = xqt[:].rearrange("p (c t) -> p c t", c=8)[:, :, 0:qn]
                        ops = [
                            e.tensor_tensor(out=t3, in0=x3,
                                            in1=ps4[:, None, 256:256 + qn].broadcast_to([128, 8, qn]),
                                            op=ALU.mult),
                            e.drain(),
                            e.tensor_copy(out=tq32[:], in_=tq[:]),
                            e.drain(),
                            e.tensor_copy(out=tq[:], in_=tq32[:]),
                            e.drain(),
                            e.tensor_tensor(out=q3, in0=t3,
                                            in1=ps4[:, None, 384:384 + qn].broadcast_to([128, 8, qn]),
                                            op=ALU.mult),
                        ]
                        if it == 0 or it == n_tiles - 1:
                            col = 0 if it == 0 else qn - 1
                            midx = 0 if it == 0 else 1
                            qm = xqt[:].rearrange("p (c t) -> p c t", c=8)[:, :, col:col + 1]
                            ops.append(e.drain())
                            ops.append(e.tensor_tensor(
                                out=qm, in0=qm,
                                in1=qmask[:, None, midx:midx + 1].broadcast_to([128, 8, 1]),
                                op=ALU.mult))
                        return ops
                    TK["quant"][it] = S.step("vector", _,
                                             after=[TK["bc2"][it]] +
                                             ([TK["qsp"][it - 1]] if it else []))

                    @dma_step(8)
                    def _(e, sc=sc, qn=qn):
                        return [e.dma_start(out=xq_s[dc * 128:(dc + 1) * 128, sc:sc + qn],
                                            in_=xqt[:, dc * QT: dc * QT + qn])
                                for dc in range(8)]
                    TK["qsp"][it] = S.step("gpsimd", _, dma=True, after=[TK["quant"][it]])

                front(0)
                for it in range(1, n_tiles):
                    front(it)
                    tail(it - 1)
                tail(n_tiles - 1)
                S.last = TK["qsp"][n_tiles - 1]

            PHASE_MARKS.append(("A2", S.nsteps))

        S.auto_pump = None
        drain_wq()

        with contextlib.ExitStack() as fctx:
            def fsb(name, shape, dt):
                return fctx.enter_context(nc.sbuf_tensor(name, shape, dt))
            convd = fsb("convd", [128, 96 * 128], BF16)
            convb = fsb("convb", [128, 32], F32)
            xqb = fsb("xqb", [128, 8 * 520], BF16)
            w1buf = fsb("w1buf", [128, 3 * 8 * 512], BF16)
            w2buf = fsb("w2buf", [128, 2 * 4 * 512], BF16)
            h_bf = fsb("h_bf", [128, 32 * HBW], BF16)
            h_silu = fsb("h_silu", [128, 32 * FT], BF16)
            M2 = fsb("M2", [128, 512], F32)
            tbuf = fsb("tbuf", [128, 4 * FT], F32)
            t32 = fsb("t32", [128, 4 * FT], I32)
            d2sb = fsb("d2sb", [128, 512], F32)
            xl1b = fsb("xl1b", [128, 8 * FT], F32)
            outst = fsb("outst", [128, 8 * FT], F32)
            vbf2 = fsb("vbf2", [128, 8 * FT], BF16)

            @dma_step(1)
            def _(e):
                return e.dma_start(out=convb[:], in_=convb_h[:])
            S.step("sync", _, dma=True)

            for part in range(6):
                @dma_step(1)
                def _(e, part=part):
                    return e.dma_start(out=tbuf[:, 0:2048],
                                       in_=convd_h[:, part * 2048:(part + 1) * 2048])
                S.step("sync", _, dma=True)

                def _(e, part=part):
                    return e.copy(out=convd[:, part * 2048:(part + 1) * 2048], in_=tbuf[:, 0:2048])
                S.step("scalar", _)

            tok_ln2_prev = None
            tok_evh_prev = None
            tok_w1dma = {}
            tok_w2dma = {}
            for jt in range(4):
                ts0 = 1 + jt * FT
                fn = FT
                hn = fn + 2

                @dma_step(2)
                def _(e, ts0=ts0, hn=hn, fn=fn):
                    return [
                        e.dma_start(
                            out=xqb[:].rearrange("p (c t) -> p c t", c=8)[:, :, 0:hn],
                            in_=xq_s.rearrange("(c p) t -> p c t", p=128)[:, :, ts0 - 1: ts0 - 1 + hn]),
                        e.dma_start(
                            out=xl1b[:].rearrange("p (c t) -> p c t", c=8),
                            in_=xln1_s.rearrange("(c p) t -> p c t", p=128)[:, :, ts0:ts0 + fn]),
                    ]
                tok_xq = S.step("sync", _, dma=True, sem_key="fx",
                                after=([tok_evh_prev] if tok_evh_prev else None))

                mm1_tok = [None] * 32
                evac_tok = [None] * 32
                conv_tok = [None] * 16
                silu_tok = [None] * 16

                def emit_mm1(k):
                    g = k // 4
                    if (k % 4) == 0:
                        @dma_step(1)
                        def _(e, g=g):
                            return e.dma_start(
                                out=w1buf[:, (g % 3) * 4096:(g % 3) * 4096 + 4096]
                                .rearrange("p (c t) -> p c t", c=8),
                                in_=w1q_s.rearrange("(c p) t -> p c t", p=128)[:, :, g * 512:(g + 1) * 512])
                        dep = [tok_xq]
                        if g >= 3:
                            dep.append(mm1_tok[4 * (g - 3) + 3])
                        tok_w1dma[g] = S.step("sync", _, dma=True, after=dep,
                                              sem_key="fw%d" % (g % 3))

                    base = (k % 2) * 1024

                    def _(e, k=k, g=g, base=base):
                        hcl = k % 4
                        wb = (g % 3) * 4096
                        mms = []
                        for dc in range(8):
                            mms.append(e.matmul(
                                out=pC[:, base: base + 258],
                                lhsT=w1buf[:, wb + dc * 512 + hcl * 128: wb + dc * 512 + (hcl + 1) * 128],
                                rhs=xqb[:, dc * 520: dc * 520 + 258],
                                start=(dc == 0), stop=(dc == 7)))
                        for dc in range(8):
                            mms.append(e.matmul(
                                out=pC[:, base + 512: base + 512 + 256],
                                lhsT=w1buf[:, wb + dc * 512 + hcl * 128: wb + dc * 512 + (hcl + 1) * 128],
                                rhs=xqb[:, dc * 520 + 258: dc * 520 + 514],
                                start=(dc == 0), stop=(dc == 7)))
                        return mms
                    dep = [tok_w1dma[g], tok_xq]
                    if k >= 2:
                        dep.append(evac_tok[k - 2])
                    mm1_tok[k] = S.step("tensor", _, after=dep)

                    def _(e, k=k, base=base):
                        hc = k
                        return [
                            e.copy(out=h_bf[:, hc * HBW: hc * HBW + 258], in_=pC[:, base: base + 258]),
                            e.copy(out=h_bf[:, hc * HBW + 258: hc * HBW + 514],
                                   in_=pC[:, base + 512: base + 512 + 256]),
                        ]
                    evac_tok[k] = S.step("scalar", _, after=[mm1_tok[k]])

                def emit_conv(ck):
                    psa, psb = (pA, pB) if (ck % 2) == 0 else (ps3, ps4)

                    def _(e, ck=ck, fn=fn, psa=psa, psb=psb):
                        mms = []
                        for cl in range(2):
                            c = 2 * ck + cl
                            po = psa if cl == 0 else psb
                            for tap in range(3):
                                mms.append(e.matmul(
                                    out=po[:, 0:fn],
                                    lhsT=convd[:, (3 * c + tap) * 128:(3 * c + tap + 1) * 128],
                                    rhs=h_bf[:, c * HBW + tap: c * HBW + tap + fn],
                                    start=(tap == 0), stop=(tap == 2)))
                        return mms
                    dep = [evac_tok[2 * ck + 1]]
                    if ck >= 2:
                        dep.append(silu_tok[ck - 2])
                    if ck < 2 and tok_ln2_prev is not None:
                        dep.append(tok_ln2_prev)
                    conv_tok[ck] = S.step("tensor", _, after=dep)

                    def _(e, ck=ck, fn=fn, psa=psa, psb=psb):
                        return [e.activation(out=h_silu[:, (2 * ck + cl) * FT: (2 * ck + cl) * FT + fn],
                                             in_=(psa if cl == 0 else psb)[:, 0:fn],
                                             func=AF.Silu, bias=convb[:, 2 * ck + cl: 2 * ck + cl + 1],
                                             scale=1.0)
                                for cl in range(2)]
                    silu_tok[ck] = S.step("scalar", _, after=[conv_tok[ck]])

                for k in range(32):
                    emit_mm1(k)
                    if k >= 3 and k % 2 == 1:
                        emit_conv((k - 3) // 2)
                emit_conv(15)

                def _(e, fn=fn):
                    hin = h_silu[:].rearrange("p (c t) -> p t c", c=32)
                    return e.tensor_reduce(
                        out=M2[:].rearrange("p (t o) -> p t o", o=1),
                        in_=hin, axis=AX.X, op=ALU.max, apply_absolute_value=True)
                tok_red = S.step("vector", _, after=[silu_tok[15]])

                def _(e):
                    return [e.transpose(out=ps4[0:128, t * 128:(t + 1) * 128],
                                        in_=M2[:, t * 128:(t + 1) * 128],
                                        identity=ident[:])
                            for t in range(4)]
                tok_tr = S.step("tensor", _, after=[tok_red, silu_tok[15]])

                def _(e):
                    ops = []
                    for t in range(4):
                        ops.append(e.reduce_max(out=cols_t[0:128, 4 + t:5 + t],
                                                in_=ps4[0:128, t * 128:(t + 1) * 128], axis=AX.X))
                    ops += [
                        e.drain(),
                        e.tensor_scalar(out=cols_t[:, 8:12], in0=cols_t[:, 4:8],
                                        scalar1=1e-5, scalar2=None, op0=ALU.max),
                        e.drain(),
                        e.tensor_scalar(out=cols_t[:, 8:12], in0=cols_t[:, 8:12],
                                        scalar1=1.0 / 127.0, scalar2=None, op0=ALU.mult),
                        e.drain(),
                        e.reciprocal(out=cols_t[:, 12:16], in_=cols_t[:, 8:12]),
                        e.drain(),
                        e.tensor_tensor(out=cols_t[:, 4:8], in0=cols_t[:, 8:12],
                                        in1=mc2col[:, 0:1].broadcast_to([128, 4]),
                                        op=ALU.mult),
                    ]
                    return ops
                tok_cols = S.step("vector", _, after=[tok_tr])

                def _(e):
                    return [e.transpose(out=ps3[0:1, t * 128:(t + 1) * 128],
                                        in_=cols_t[:, 12 + t:13 + t], identity=ident[:])
                            for t in range(4)]
                tok_s2t = S.step("tensor", _, after=[tok_cols, silu_tok[15]])

                def _(e):
                    return e.tensor_copy(out=srcrow[0:1, 0:512], in_=ps3[0:1, 0:512])
                tok_s2c = S.step("vector", _, after=[tok_s2t])

                def _(e, fn=fn):
                    return e.matmul(out=ps3[:, 0:fn], lhsT=onesf[:], rhs=srcrow[:, 0:fn],
                                    start=True, stop=True)
                tok_s2b = S.step("tensor", _, after=[tok_s2c])

                def _(e):
                    return [e.transpose(out=ps4[0:1, t * 128:(t + 1) * 128],
                                        in_=cols_t[:, 4 + t:5 + t], identity=ident[:])
                            for t in range(4)]
                tok_d2t = S.step("tensor", _, after=[tok_cols])

                def _(e):
                    return e.tensor_copy(out=srcrow[0:1, 0:512], in_=ps4[0:1, 0:512])
                tok_d2c = S.step("vector", _, after=[tok_d2t, tok_s2b])

                def _(e, fn=fn):
                    return e.matmul(out=ps4[:, 0:fn], lhsT=onesf[:], rhs=srcrow[:, 0:fn],
                                    start=True, stop=True)
                tok_d2b = S.step("tensor", _, after=[tok_d2c])

                def _(e, fn=fn):
                    return e.tensor_copy(out=d2sb[:, 0:fn], in_=ps4[:, 0:fn])
                tok_d2s = S.step("vector", _, after=[tok_d2b])

                qv_tok = [None] * 8
                qs_tok = [None] * 8
                for grp in range(8):
                    def _(e, grp=grp, fn=fn):
                        h3 = h_silu[:].rearrange("p (c t) -> p c t", c=32)[:, 4 * grp:4 * grp + 4, :]
                        t3 = tbuf[:].rearrange("p (c t) -> p c t", c=4)
                        return [
                            e.tensor_tensor(out=t3, in0=h3,
                                            in1=ps3[:, None, 0:fn].broadcast_to([128, 4, fn]),
                                            op=ALU.mult),
                            e.drain(),
                            e.tensor_copy(out=t32[:], in_=tbuf[:]),
                        ]
                    dep = [tok_s2b]
                    if grp >= 1:
                        dep.append(qs_tok[grp - 1])
                    qv_tok[grp] = S.step("vector", _, after=dep)

                    def _(e, grp=grp, fn=fn):
                        h3 = h_bf[:].rearrange("p (c t) -> p c t", t=HBW)[:, 4 * grp:4 * grp + 4, 0:fn]
                        t3 = t32[:].rearrange("p (c t) -> p c t", c=4)
                        return e.copy(out=h3, in_=t3)
                    qs_tok[grp] = S.step("scalar", _,
                                         after=[qv_tok[grp], conv_tok[2 * grp + 1],
                                                mm2_prev_reader.get(grp, None)] if False else
                                         [qv_tok[grp], conv_tok[2 * grp + 1]])

                mm2_tok = {}
                evh_tok = {}
                for half in range(2):
                    for g in range(8):
                        @dma_step(1)
                        def _(e, g=g, half=half):
                            gi = half * 8 + g
                            return e.dma_start(
                                out=w2buf[:, (gi % 2) * 2048:(gi % 2) * 2048 + 2048]
                                .rearrange("p (c t) -> p c t", c=4),
                                in_=w2q_s.rearrange("(c p) t -> p c t", p=128)
                                [:, 4 * g: 4 * (g + 1), half * 512:(half + 1) * 512])
                        gi = half * 8 + g
                        dep = [tok_xq]
                        if gi >= 2:
                            dep.append(mm2_tok[gi - 2])
                        tok_w2dma[gi] = S.step("sync", _, dma=True, after=dep,
                                               sem_key="gw%d" % (gi % 2))

                        def _(e, g=g, half=half, fn=fn):
                            gi = half * 8 + g
                            wb = (gi % 2) * 2048
                            mms = []
                            for hcl in range(4):
                                hc = 4 * g + hcl
                                for ocl in range(4):
                                    mms.append(e.matmul(
                                        out=pC[:, ocl * 512: ocl * 512 + fn],
                                        lhsT=w2buf[:, wb + hcl * 512 + ocl * 128: wb + hcl * 512 + (ocl + 1) * 128],
                                        rhs=h_bf[:, hc * HBW: hc * HBW + fn],
                                        start=(hc == 0), stop=(hc == 31)))
                            return mms
                        dep = [tok_w2dma[gi], qs_tok[g]]
                        if half == 0 and g == 0:
                            dep.append(silu_tok[15])
                            dep.append(evac_tok[31])
                        if half == 1 and g == 0:
                            dep.append(evh_tok[0])
                        mm2_tok[gi] = S.step("tensor", _, after=dep)

                    def _(e, half=half, fn=fn):
                        p3 = pC[:].rearrange("p (c t) -> p c t", t=512)
                        o3 = outst[:].rearrange("p (c t) -> p c t", c=8)[:, half * 4:(half + 1) * 4, :]
                        x3 = xl1b[:].rearrange("p (c t) -> p c t", c=8)[:, half * 4:(half + 1) * 4, :]
                        return [
                            e.tensor_tensor(out=o3, in0=p3,
                                            in1=d2sb[:, None, 0:fn].broadcast_to([128, 4, fn]),
                                            op=ALU.mult),
                            e.drain(),
                            e.tensor_tensor(out=o3, in0=o3, in1=x3, op=ALU.add),
                        ]
                    evh_tok[half] = S.step("vector", _,
                                           after=[mm2_tok[half * 8 + 7], tok_d2s])

                tok_evh_prev = evh_tok[1]

                def _(e, fn=fn):
                    return e.tensor_copy(out=vbf2[:], in_=outst[:])
                S.step("vector", _, after=[evh_tok[1]])

                def _(e, fn=fn):
                    return [e.matmul(out=ps3[0:1, 0:fn], lhsT=onesb[:, 0:1],
                                     rhs=vbf2[:, dc * FT: dc * FT + fn],
                                     start=(dc == 0), stop=(dc == 7))
                            for dc in range(8)]
                S.step("tensor", _, extra_waits=[("vector", S.cnt["vector"])])

                def _(e, fn=fn):
                    return e.activation(out=vbf2[:], in_=outst[:], func=AF.Square)
                S.step("scalar", _, extra_waits=[("tensor", S.cnt["tensor"])])

                def _(e, fn=fn):
                    return [e.matmul(out=ps4[0:1, 0:fn], lhsT=onesb[:, 0:1],
                                     rhs=vbf2[:, dc * FT: dc * FT + fn],
                                     start=(dc == 0), stop=(dc == 7))
                            for dc in range(8)]
                S.step("tensor", _)

                def _(e, fn=fn):
                    return [
                        e.tensor_scalar(out=rows_t[0:1, 0:fn], in0=ps3[0:1, 0:fn],
                                        scalar1=1.0 / 1024.0, scalar2=None, op0=ALU.mult),
                        e.drain(),
                        e.tensor_tensor(out=rows_t[0:1, 512:512 + fn], in0=rows_t[0:1, 0:fn],
                                        in1=rows_t[0:1, 0:fn], op=ALU.mult),
                        e.drain(),
                        e.scalar_tensor_tensor(out=rows_t[0:1, 1024:1024 + fn],
                                               in0=ps4[0:1, 0:fn], scalar=1.0 / 1024.0,
                                               in1=rows_t[0:1, 512:512 + fn],
                                               op0=ALU.mult, op1=ALU.subtract),
                        e.drain(),
                        e.tensor_scalar(out=rows_t[0:1, 1024:1024 + fn],
                                        in0=rows_t[0:1, 1024:1024 + fn],
                                        scalar1=EPS_LN, scalar2=None, op0=ALU.add),
                        e.drain(),
                        e.reciprocal(out=rows_t[0:1, 1536:1536 + fn],
                                     in_=rows_t[0:1, 1024:1024 + fn]),
                    ]
                S.step("vector", _)

                def _(e, fn=fn):
                    return e.activation(out=srcrow[0:1, 0:fn], in_=rows_t[0:1, 1536:1536 + fn],
                                        func=AF.Sqrt)
                S.step("scalar", _)

                def _(e, fn=fn):
                    return e.matmul(out=ps4[:, 0:fn], lhsT=onesf[:], rhs=srcrow[:, 0:fn],
                                    start=True, stop=True)
                S.step("tensor", _)

                def _(e, fn=fn):
                    return e.tensor_tensor(out=srcrow[0:1, 0:fn], in0=srcrow[0:1, 0:fn],
                                           in1=rows_t[0:1, 0:fn], op=ALU.mult)
                S.step("vector", _)

                def _(e, fn=fn):
                    return e.matmul(out=ps3[:, 0:fn], lhsT=onesf[:], rhs=srcrow[:, 0:fn],
                                    start=True, stop=True)
                S.step("tensor", _)

                def _(e, fn=fn):
                    o3 = outst[:].rearrange("p (c t) -> p c t", c=8)
                    return [
                        e.tensor_tensor(out=o3, in0=o3,
                                        in1=ps4[:, None, 0:fn].broadcast_to([128, 8, fn]),
                                        op=ALU.mult),
                        e.drain(),
                        e.tensor_tensor(out=o3, in0=o3,
                                        in1=ps3[:, None, 0:fn].broadcast_to([128, 8, fn]),
                                        op=ALU.subtract),
                    ]
                tok_ln2_prev = S.step("vector", _)

                @dma_step(1)
                def _(e, ts0=ts0, fn=fn):
                    return e.dma_start(
                        out=outT_h.rearrange("(c p) t -> p c t", p=128)[:, :, ts0 - 1: ts0 - 1 + fn],
                        in_=outst[:].rearrange("p (c t) -> p c t", c=8))
                S.step("gpsimd", _, dma=True, after=[tok_ln2_prev])
                S.last = tok_ln2_prev

        PHASE_MARKS.append(("FFN", S.nsteps))

        final_g = S.cnt["gpsimd"]

        def _(e):
            e.wait_ge(sems["gpsimd"], final_g)
            return e.engine_nop()
        S.step("gpsimd", _)

        with nc.Block() as block:
            @block.sync
            def _(eng):
                for op in S.ops["sync"]:
                    op(eng)

            @block.vector
            def _(eng):
                for op in S.ops["vector"]:
                    op(eng)

            @block.scalar
            def _(eng):
                for op in S.ops["scalar"]:
                    op(eng)

            @block.tensor
            def _(eng):
                for op in S.ops["tensor"]:
                    op(eng)

            @block.gpsimd
            def _(eng):
                for op in S.ops["gpsimd"]:
                    op(eng)

    return nc


def _band_mask(off, qn):
    m = np.zeros((128, 1024), np.float32)
    for h in range(8):
        mm = np.zeros((128, 128), np.float32)
        i = np.arange(128)[:, None]
        j = np.arange(128)[None, :]
        band = (np.abs(i - j - off) <= 3) & (j < qn)
        mm[band] = 1.0
        m[:, h * 128:(h + 1) * 128] = mm
    return m


def kernel(L0_attn_x, L0_attn_w_qkv, L0_attn_proj, L0_attn_w_out, L0_attn_b_out,
           L0_ffn_w1, L0_ffn_conv_w, L0_ffn_conv_b, L0_ffn_w2):
    x = np.ascontiguousarray(np.asarray(L0_attn_x, dtype=np.float32))
    wqkv = np.ascontiguousarray(np.asarray(L0_attn_w_qkv, dtype=np.float32))
    proj = np.ascontiguousarray(np.asarray(L0_attn_proj, dtype=np.float32))
    wout = np.ascontiguousarray(np.asarray(L0_attn_w_out, dtype=np.float32))
    bout = np.ascontiguousarray(np.asarray(L0_attn_b_out, dtype=np.float32))[None, :]
    w1 = np.asarray(L0_ffn_w1, dtype=np.float32)
    convw = np.asarray(L0_ffn_conv_w, dtype=np.float32)[:, 0, :]
    convb = np.asarray(L0_ffn_conv_b, dtype=np.float32)
    w2 = np.asarray(L0_ffn_w2, dtype=np.float32)

    w1T = np.ascontiguousarray(w1.T)
    w2T = np.ascontiguousarray(w2.T)

    convdiag = np.zeros((128, 96 * 128), np.float32)
    idx = np.arange(128)
    for c in range(32):
        for tap in range(3):
            blk = (3 * c + tap) * 128
            convdiag[idx, blk + idx] = convw[c * 128:(c + 1) * 128, tap]
    convb_m = np.ascontiguousarray(convb.reshape(32, 128).T)

    ident = np.eye(128, dtype=np.float32)
    n_tiles = (NQ + QT - 1) // QT
    qs_last = QS0 + (n_tiles - 1) * QT
    qn_last = QS0 + NQ - qs_last
    off_last = qs_last - min(qs_last - 3, TB - 128)
    bandmask = _band_mask(3, QT)
    bandmask2 = _band_mask(off_last, qn_last)

    in_maps = []
    for core in range(NCORES):
        b, r = core // 4, core % 4
        t0 = r * TOK
        lo, hi = t0 - HB, t0 + TOK + HB
        xT = np.zeros((D, TB), np.float32)
        s, e = max(lo, 0), min(hi, N)
        xT[:, s - lo: s - lo + (e - s)] = x[b, s:e, :].T
        kmask = np.zeros((128, 16), np.float32)
        kmask[:, 0:8] = 1.0 if lo >= 0 else 0.0
        kmask[:, 8:16] = 1.0 if hi <= N else 0.0
        qmask = np.zeros((128, 2), np.float32)
        qmask[:, 0] = 1.0 if t0 - 1 >= 0 else 0.0
        qmask[:, 1] = 1.0 if t0 + TOK + 1 <= N else 0.0
        in_maps.append({
            "xT": xT, "wqkv": wqkv, "proj": proj, "wout": wout, "boutrow": bout,
            "w1T": w1T, "w2T": w2T, "convdiag": convdiag, "convbH": convb_m,
            "identH": ident, "bandmask": bandmask, "bandmask2": bandmask2,
            "kmaskH": kmask, "qmaskH": qmask,
        })

    nc = build_program()
    res = run_bass_kernel_spmd(nc, in_maps, core_ids=list(range(NCORES)))

    out = np.empty((B, N, D), np.float32)
    for core in range(NCORES):
        b, r = core // 4, core % 4
        t0 = r * TOK
        out[b, t0:t0 + TOK, :] = res.results[core]["outT"].T
    return out


# version 23
# speedup vs baseline: 1.0949x; 1.0209x over previous
import numpy as np
import contextlib
import os
import concourse.bass as bass
import concourse.mybir as mybir
from concourse.bass_utils import run_bass_kernel_spmd

F32 = mybir.dt.float32
BF16 = mybir.dt.bfloat16
I32 = mybir.dt.int32
AF = mybir.ActivationFunctionType
ALU = mybir.AluOpType
AX = mybir.AxisListType

B, N, D = 2, 8192, 1024
NCORES = 8
TOK = 2048
HB = 8
TB = TOK + 2 * HB
HEADS, HD, FEAT = 8, 128, 64
DH = 4096
QT = 122
QS0, NQ = 7, 2050
FT = 512
HBW = 520
EPS_LN, EPS_Z = 1e-5, 1e-8

ENG = ("sync", "vector", "scalar", "tensor", "gpsimd")
BUILD_LIMIT = int(os.environ.get("KBISECT", "0")) or None
PHASE_MARKS = []


class Sched:

    def __init__(self, nc, sems):
        self.nc = nc
        self.sems = sems
        self.ops = {e: [] for e in ENG}
        self.cnt = {e: 0 for e in ENG}
        self.last = None
        self.nsteps = 0
        self.auto_pump = None
        self._pumping = False

    def step(self, eng, builder, dma=False, after=None, extra_waits=(), sem_key=None):
        self.nsteps += 1
        if BUILD_LIMIT is not None and self.nsteps > BUILD_LIMIT:
            return self.last
        if after is None:
            deps = [self.last] if self.last is not None else []
        elif isinstance(after, list):
            deps = list(after)
        else:
            deps = [after]
        deps = [d for d in deps if d is not None]
        deps += list(extra_waits)
        per = {}
        for (s, v) in deps:
            if s == eng:
                continue
            per[s] = max(per.get(s, 0), v)
        waits = sorted(per.items())
        key = sem_key if (dma and sem_key is not None) else eng
        sem = self.sems[key]
        if dma:
            nd = getattr(builder, "n_dma")
            self.cnt[key] += 16 * nd
            tok = (key, self.cnt[key])

            def run(e, waits=tuple(waits), builder=builder, nd=nd):
                for (s, v) in waits:
                    e.wait_ge(self.sems[s], v)
                insts = builder(e)
                if not isinstance(insts, (list, tuple)):
                    insts = [insts]
                assert len(insts) == nd
                for i in insts:
                    i.then_inc(sem, 16)

            self.ops[eng].append(run)
        else:
            self.cnt[eng] += 1
            tok = (eng, self.cnt[eng])

            def run(e, waits=tuple(waits), builder=builder, eng=eng):
                for (s, v) in waits:
                    e.wait_ge(self.sems[s], v)
                insts = builder(e)
                if not isinstance(insts, (list, tuple)):
                    insts = [insts]
                if eng in ("vector", "scalar", "gpsimd"):
                    e.drain().then_inc(sem, 1)
                else:
                    insts[-1].then_inc(sem, 1)

            self.ops[eng].append(run)
        self.last = tok
        if self.auto_pump is not None and not self._pumping:
            self.auto_pump()
        return tok


def dma_step(n):
    def deco(fn):
        fn.n_dma = n
        return fn
    return deco


def build_program():
    nc = bass.Bass("TRN2", target_bir_lowering=False)

    xT_h = nc.dram_tensor("xT", [D, TB], F32, kind="ExternalInput").ap()
    wqkv_h = nc.dram_tensor("wqkv", [D, 3 * D], F32, kind="ExternalInput").ap()
    proj_h = nc.dram_tensor("proj", [HD, FEAT], F32, kind="ExternalInput").ap()
    wout_h = nc.dram_tensor("wout", [D, D], F32, kind="ExternalInput").ap()
    bout_h = nc.dram_tensor("boutrow", [1, D], F32, kind="ExternalInput").ap()
    w1T_h = nc.dram_tensor("w1T", [D, DH], F32, kind="ExternalInput").ap()
    w2T_h = nc.dram_tensor("w2T", [DH, D], F32, kind="ExternalInput").ap()
    convd_h = nc.dram_tensor("convdiag", [128, 96 * 128], F32, kind="ExternalInput").ap()
    convb_h = nc.dram_tensor("convbH", [128, 32], F32, kind="ExternalInput").ap()
    ident_h = nc.dram_tensor("identH", [128, 128], F32, kind="ExternalInput").ap()
    bmask_h = nc.dram_tensor("bandmask", [128, 1024], F32, kind="ExternalInput").ap()
    bmask2_h = nc.dram_tensor("bandmask2", [128, 1024], F32, kind="ExternalInput").ap()
    kmask_h = nc.dram_tensor("kmaskH", [128, 16], F32, kind="ExternalInput").ap()
    qmask_h = nc.dram_tensor("qmaskH", [128, 2], F32, kind="ExternalInput").ap()
    outT_h = nc.dram_tensor("outT", [D, TOK], F32, kind="ExternalOutput").ap()
    _skind = "ExternalOutput" if os.environ.get("KDEBUG") else "Internal"
    xln1_s = nc.dram_tensor("xln1_s", [D, NQ], F32, kind=_skind).ap()
    xq_s = nc.dram_tensor("xq_s", [D, NQ], BF16, kind=_skind).ap()
    w1q_s = nc.dram_tensor("w1q_s", [D, DH], BF16, kind=_skind).ap()
    w2q_s = nc.dram_tensor("w2q_s", [DH, D], BF16, kind=_skind).ap()

    ctx = contextlib.ExitStack()
    with ctx:
        def sb(name, shape, dt):
            return ctx.enter_context(nc.sbuf_tensor(name, shape, dt))

        def psum(name, shape):
            return ctx.enter_context(nc.psum_tensor(name, shape, F32))

        ident = sb("ident", [128, 128], F32)
        onesb = sb("onesb", [128, 1024], BF16)
        onesf = sb("onesf", [128, 128], F32)
        eps128 = sb("eps128", [128, 128], BF16)
        c32 = sb("c32", [32, 520], BF16)
        pp = sb("pp", [128, 192], BF16)
        boutpad = sb("boutpad", [32, 1024], BF16)
        minus1 = sb("minus1", [128, 1], F32)
        c1024 = sb("c1024", [128, 1], F32)
        rows_t = sb("rows_t", [1, 2048], F32)
        srcrow = sb("srcrow", [128, 512], F32)
        cols_t = sb("cols_t", [128, 16], F32)
        mc1col = sb("mc1col", [128, 1], F32)
        mc2col = sb("mc2col", [128, 1], F32)
        ws1col = sb("ws1col", [128, 1], F32)
        ws2col = sb("ws2col", [128, 1], F32)
        wtot = sb("wtot", [1, 8], F32)
        kmask = sb("kmask", [128, 16], F32)
        qmask = sb("qmask", [128, 2], F32)

        pA = psum("pA", [128, 512])
        pB = psum("pB", [128, 512])
        pC = psum("pC", [128, 2048])
        ps3 = psum("ps3", [128, 512])
        ps4 = psum("ps4", [128, 512])

        sems = {}
        for e in ENG + ("wq0", "wq1", "wq2", "wq3", "fx", "fw0", "fw1", "fw2", "gw0", "gw1", "gw2", "ax", "sa0", "sa1"):
            sems[e] = ctx.enter_context(nc.semaphore("sem_" + e))
        S = Sched(nc, sems)
        for e in ("wq0", "wq1", "wq2", "wq3", "fx", "fw0", "fw1", "fw2", "gw0", "gw1", "gw2", "ax", "sa0", "sa1"):
            S.cnt[e] = 0

        @dma_step(4)
        def _(e):
            return [
                e.dma_start(out=ident[:], in_=ident_h[:]),
                e.dma_start(out=kmask[:], in_=kmask_h[:]),
                e.dma_start(out=qmask[:], in_=qmask_h[:]),
                e.dma_start(out=srcrow[:, 0:64], in_=proj_h[:]),
            ]
        S.step("sync", _, dma=True)

        def _(e):
            return [
                e.memset(onesb[:], 1.0),
                e.memset(onesf[:], 1.0),
                e.memset(eps128[:], EPS_Z / 128.0),
                e.memset(c32[:], 1.0 / 32.0),
                e.memset(pp[:], 0.0),
                e.memset(boutpad[:], 0.0),
                e.memset(minus1[:], -1.0),
                e.memset(c1024[:], 1.0 / 1024.0),
                e.memset(rows_t[:], 0.0),
                e.memset(cols_t[:], 0.0),
                e.drain(),
                e.tensor_copy(out=pp[:, 0:64], in_=srcrow[:, 0:64]),
                e.tensor_copy(out=pp[:, 128:192], in_=srcrow[:, 0:64]),
                e.drain(),
                e.memset(srcrow[:], 0.0),
            ]
        S.step("vector", _)

        PHASE_MARKS.append(("P0", S.nsteps))

        wq_done = [None, None]
        wq_mean = [None, None]

        def wquant_gen(wT_h, wq_s, n_iters, width, acc_base, mccol, wscol, slot,
                       rows128, colw, gidx):
            stg = [stageW, stageW2, stageW3, stageW4]
            tok_dma = [None] * n_iters
            tok_abs = [None] * n_iters

            def src_ap(it):
                rb, cb = it // colw, it % colw
                return wT_h[rb * 128:(rb + 1) * 128, cb * width:(cb + 1) * width]

            def dst_ap(it):
                rb, cb = it // colw, it % colw
                return wq_s[rb * 128:(rb + 1) * 128, cb * width:(cb + 1) * width]

            for it in range(n_iters):
                @dma_step(1)
                def _(e, it=it):
                    return e.dma_start(out=stg[it % 4][:, 0:width], in_=src_ap(it))
                if it >= 4:
                    dep = [tok_abs[it - 4]]
                elif gidx == 1:
                    dep = [wq_done[0]]
                else:
                    dep = [S.last]
                tok_dma[it] = S.step("sync", _, dma=True, after=dep,
                                     sem_key="wq%d" % (it % 4))

                def _(e, it=it):
                    return e.activation(out=wtmp[:, 0:width], in_=stg[it % 4][:, 0:width],
                                        func=AF.Abs, accum_out=wacc[:, acc_base + it: acc_base + it + 1])
                yield
                tok_abs[it] = S.step("scalar", _, after=[tok_dma[it]])
                yield

            def _(e):
                return e.reduce_sum(out=cols_t[:, 8:9], in_=wacc[:, acc_base:acc_base + n_iters], axis=AX.X)
            S.step("vector", _, after=[tok_abs[n_iters - 1]])

            def _(e):
                return e.transpose(out=ps3[0:1, 0:128], in_=cols_t[:, 8:9], identity=ident[:])
            S.step("tensor", _, after=[S.last])

            def _(e):
                return [
                    e.reduce_sum(out=wtot[0:1, slot:slot + 1], in_=ps3[0:1, 0:128], axis=AX.X),
                    e.drain(),
                    e.tensor_scalar(out=wtot[0:1, slot + 1:slot + 2], in0=wtot[0:1, slot:slot + 1],
                                    scalar1=1.0 / (n_iters * 128.0 * width), scalar2=None,
                                    op0=ALU.mult),
                    e.drain(),
                    e.tensor_scalar(out=wtot[0:1, slot + 1:slot + 2], in0=wtot[0:1, slot + 1:slot + 2],
                                    scalar1=1e-5, scalar2=None, op0=ALU.max),
                    e.drain(),
                    e.reciprocal(out=wtot[0:1, slot + 2:slot + 3], in_=wtot[0:1, slot + 1:slot + 2]),
                    e.drain(),
                    e.tensor_copy(out=srcrow[0:1, 0:1], in_=wtot[0:1, slot + 1:slot + 2]),
                    e.tensor_copy(out=srcrow[0:1, 1:2], in_=wtot[0:1, slot + 2:slot + 3]),
                ]
            S.step("vector", _, after=[S.last])

            def _(e):
                return e.matmul(out=ps4[:, 0:2], lhsT=onesf[:], rhs=srcrow[:, 0:2],
                                start=True, stop=True)
            S.step("tensor", _, after=[S.last])

            def _(e):
                return [
                    e.tensor_copy(out=mccol[:], in_=ps4[:, 0:1]),
                    e.tensor_copy(out=wscol[:], in_=ps4[:, 1:2]),
                ]
            tok_sc = S.step("vector", _, after=[S.last])
            wq_mean[gidx] = tok_sc
            yield

            tok_v = [None] * n_iters
            tok_s = [None] * n_iters
            tok_g = [None] * n_iters
            for it in range(n_iters):
                @dma_step(1)
                def _(e, it=it):
                    return e.dma_start(out=stg[it % 4][:, 0:width], in_=src_ap(it))
                tok_dma[it] = S.step("sync", _, dma=True,
                                     after=[tok_v[it - 4]] if it >= 4 else [tok_sc],
                                     sem_key="wq%d" % (it % 4))
                yield

                def _(e, it=it):
                    return e.activation(out=wtmp[:, 0:width], in_=stg[it % 4][:, 0:width],
                                        func=AF.Identity, scale=wscol[:])
                tok_sa = S.step("scalar", _,
                                after=[tok_dma[it], tok_sc] + ([tok_v[it - 1]] if it >= 1 else []))

                def _(e, it=it):
                    return e.tensor_scalar(out=w32[:, 0:width], in0=wtmp[:, 0:width],
                                           scalar1=1.0, scalar2=-1.0,
                                           op0=ALU.min, op1=ALU.max)
                tok_v[it] = S.step("vector", _,
                                   after=[tok_sa] + ([tok_s[it - 1]] if it >= 1 else []))

                def _(e, it=it):
                    return e.copy(out=wqout[:, 0:width], in_=w32[:, 0:width])
                tok_s[it] = S.step("scalar", _,
                                   after=[tok_v[it]] + ([tok_g[it - 1]] if it >= 1 else []))

                @dma_step(1)
                def _(e, it=it):
                    return e.dma_start(out=dst_ap(it), in_=wqout[:, 0:width])
                tok_g[it] = S.step("gpsimd", _, dma=True, after=[tok_s[it]])
                yield
            wq_done[gidx] = tok_g[n_iters - 1]

        wq_gens = [None, None]

        def pump_wq(n):
            saved = S.last
            S._pumping = True
            for _ in range(n):
                for gidx in range(2):
                    if wq_gens[gidx] is not None:
                        try:
                            next(wq_gens[gidx])
                            break
                        except StopIteration:
                            wq_gens[gidx] = None
            S._pumping = False
            S.last = saved

        def drain_wq():
            saved = S.last
            S._pumping = True
            for gidx in range(2):
                if wq_gens[gidx] is not None:
                    for _ in wq_gens[gidx]:
                        pass
                    wq_gens[gidx] = None
            S._pumping = False
            S.last = saved

            def _(e):
                return e.engine_nop()
            S.step("vector", _, after=[saved] + [t for t in wq_done if t is not None])

        PHASE_MARKS.append(("P1-wquant", S.nsteps))

        with contextlib.ExitStack() as actx:
            x_bf = actx.enter_context(nc.sbuf_tensor("x_bf", [128, 8 * TB], BF16))
            qphi = actx.enter_context(nc.sbuf_tensor("qphi", [128, 4 * TB], BF16))
            kphi = actx.enter_context(nc.sbuf_tensor("kphi", [128, 4 * TB], BF16))
            stage = actx.enter_context(nc.sbuf_tensor("stageA", [128, 1024], F32))
            stage2 = actx.enter_context(nc.sbuf_tensor("stageA2", [128, 1024], F32))
            wacc = actx.enter_context(nc.sbuf_tensor("wacc", [128, 80], F32))
            stageW = actx.enter_context(nc.sbuf_tensor("stageW", [128, 1024], F32))
            stageW2 = actx.enter_context(nc.sbuf_tensor("stageW2", [128, 1024], F32))
            stageW3 = actx.enter_context(nc.sbuf_tensor("stageW3", [128, 1024], F32))
            stageW4 = actx.enter_context(nc.sbuf_tensor("stageW4", [128, 1024], F32))
            wtmp = actx.enter_context(nc.sbuf_tensor("wtmp", [128, 1024], F32))
            w32 = actx.enter_context(nc.sbuf_tensor("w32", [128, 1024], I32))
            wqout = actx.enter_context(nc.sbuf_tensor("wqout", [128, 1024], BF16))
            wq_gens[0] = wquant_gen(w1T_h, w1q_s, 32, 1024, 0, mc1col, ws1col, 0, 8, 4, 0)
            wq_gens[1] = wquant_gen(w2T_h, w2q_s, 32, 1024, 32, mc2col, ws2col, 4, 32, 1, 1)
            bmask = actx.enter_context(nc.sbuf_tensor("bmask", [128, 1024], F32))
            bmask2 = actx.enter_context(nc.sbuf_tensor("bmask2", [128, 1024], F32))

            @dma_step(3)
            def _(e):
                return [e.dma_start(out=bmask[:], in_=bmask_h[:]),
                        e.dma_start(out=bmask2[:], in_=bmask2_h[:]),
                        e.dma_start(out=stage[0:1, 0:1024], in_=bout_h[:])]
            S.step("sync", _, dma=True)

            def _(e):
                return e.tensor_copy(out=boutpad[0:1, :], in_=stage[0:1, 0:1024])
            S.step("vector", _)

            with contextlib.ExitStack() as a1ctx:
                wqk = a1ctx.enter_context(nc.sbuf_tensor("wqk", [128, 8 * 2048], BF16))
                xres = a1ctx.enter_context(nc.sbuf_tensor("xres", [128, 8 * 528], F32))
                qtmp = a1ctx.enter_context(nc.sbuf_tensor("qtmp", [128, 8 * 512], BF16))
                stgA = [stage, stage2]
                tok_wd = [None] * 16
                tok_wc = [None] * 16
                for half in range(16):
                    @dma_step(1)
                    def _(e, half=half):
                        dc, hh = half // 2, half % 2
                        return e.dma_start(out=stgA[half % 2][:, 0:1024],
                                           in_=wqkv_h[dc * 128:(dc + 1) * 128, hh * 1024:(hh + 1) * 1024])
                    tok_wd[half] = S.step("sync", _, dma=True, sem_key="sa%d" % (half % 2),
                                          after=[tok_wc[half - 2]] if half >= 2 else None)

                    def _(e, half=half):
                        return e.copy(out=wqk[:, half * 1024:(half + 1) * 1024],
                                      in_=stgA[half % 2][:, 0:1024])
                    tok_wc[half] = S.step("scalar", _, after=[tok_wd[half]])

                S.auto_pump = lambda: pump_wq(3)
                tiles = [(t, min(512, TB - t)) for t in range(0, TB, 512)]
                for (ts, tl) in tiles:
                    @dma_step(1)
                    def _(e, ts=ts, tl=tl):
                        return e.dma_start(
                            out=xres[:].rearrange("p (c t) -> p c t", c=8)[:, :, 0:tl],
                            in_=xT_h.rearrange("(c p) t -> p c t", p=128)[:, :, ts:ts + tl])
                    S.step("sync", _, dma=True)

                    def _(e, ts=ts, tl=tl):
                        return e.tensor_copy(
                            out=x_bf[:].rearrange("p (c t) -> p c t", c=8)[:, :, ts:ts + tl],
                            in_=xres[:].rearrange("p (c t) -> p c t", c=8)[:, :, 0:tl])
                    S.step("vector", _)

                    for (wbase, phidst) in ((0, qphi), (1024, kphi)):
                        for hb in range(2):
                            def _(e, hb=hb, wbase=wbase, ts=ts, tl=tl):
                                mms = []
                                for hl in range(4):
                                    h = hb * 4 + hl
                                    for dc in range(8):
                                        mms.append(e.matmul(
                                            out=pC[:, hl * 512: hl * 512 + tl],
                                            lhsT=wqk[:, dc * 2048 + wbase + h * 128: dc * 2048 + wbase + (h + 1) * 128],
                                            rhs=x_bf[:, dc * TB + ts: dc * TB + ts + tl],
                                            start=(dc == 0), stop=(dc == 7)))
                                return mms
                            S.step("tensor", _)

                            def _(e, hb=hb, tl=tl):
                                return [e.copy(out=qtmp[:, (hb * 4 + hl) * 512: (hb * 4 + hl) * 512 + tl],
                                               in_=pC[:, hl * 512: hl * 512 + tl])
                                        for hl in range(4)]
                            S.step("scalar", _)

                        def _(e, ts=ts, tl=tl):
                            mms = []
                            for hp in range(4):
                                mms.append(e.matmul(out=pC[:, hp * 512: hp * 512 + tl], lhsT=pp[:, 0:128],
                                                    rhs=qtmp[:, hp * 512: hp * 512 + tl],
                                                    start=True, stop=False))
                                mms.append(e.matmul(out=pC[:, hp * 512: hp * 512 + tl], lhsT=pp[:, 64:192],
                                                    rhs=qtmp[:, (hp + 4) * 512: (hp + 4) * 512 + tl],
                                                    start=False, stop=False))
                                mms.append(e.matmul(out=pC[:, hp * 512: hp * 512 + tl],
                                                    lhsT=onesb[0:32, 0:128],
                                                    rhs=c32[0:32, 0:tl], start=False, stop=True))
                            return mms
                        S.step("tensor", _)

                        def _(e, tl=tl):
                            return [e.activation(out=xres[:, hp * 1056: hp * 1056 + tl],
                                                 in_=pC[:, hp * 512: hp * 512 + tl],
                                                 func=AF.Exp, bias=minus1[:], scale=1.0)
                                    for hp in range(4)]
                        S.step("scalar", _)

                        def _(e, ts=ts, tl=tl, phidst=phidst):
                            return [e.scalar_tensor_tensor(
                                out=phidst[:, hp * TB + ts: hp * TB + ts + tl],
                                in0=xres[:, hp * 1056: hp * 1056 + tl], scalar=1.0,
                                in1=pC[:, hp * 512: hp * 512 + tl],
                                op0=ALU.min, op1=ALU.max) for hp in range(4)]
                        S.step("vector", _)

                def _(e):
                    km3 = kphi[:].rearrange("p (c t) -> p c t", c=4)
                    return [
                        e.tensor_tensor(out=km3[:, :, 0:HB], in0=km3[:, :, 0:HB],
                                        in1=kmask[:, None, 0:8].broadcast_to([128, 4, 8]),
                                        op=ALU.mult),
                        e.tensor_tensor(out=km3[:, :, TB - HB:TB], in0=km3[:, :, TB - HB:TB],
                                        in1=kmask[:, None, 8:16].broadcast_to([128, 4, 8]),
                                        op=ALU.mult),
                    ]
                S.step("vector", _)
                while wq_mean[1] is None and wq_gens[1] is not None:
                    pump_wq(1)
                S.auto_pump = lambda: pump_wq(1)

                def _(e):
                    return e.engine_nop()
                S.step("vector", _,
                       after=[S.last] + [t for t in wq_mean if t is not None])

            PHASE_MARKS.append(("A1", S.nsteps))

            with contextlib.ExitStack() as a2ctx:
                def a2sb(name, shape, dt):
                    return a2ctx.enter_context(nc.sbuf_tensor(name, shape, dt))
                wv = a2sb("wv", [128, 8 * 1024], BF16)
                woutb = a2sb("woutb", [128, 8 * 1024], BF16)
                v_sb = a2sb("v_sb", [128, 1024], BF16)
                Sm = a2sb("Sm", [128, 1024], BF16)
                zinv = a2sb("zinv", [128, 1024], F32)
                attn_cp = a2sb("attn_cp", [128, 1024], F32)
                attn_sb = a2sb("attn_sb", [128, 1024], BF16)
                vres = a2sb("vres", [128, 976], F32)
                vresB = a2sb("vresB", [128, 976], F32)
                xresA2 = a2sb("xresA2", [128, 976], F32)
                vbf = a2sb("vbf", [128, 976], BF16)
                v2bf = a2sb("v2bf", [128, 976], BF16)
                xln1t = a2sb("xln1t", [128, 976], F32)
                tq = a2sb("tq", [128, 976], F32)
                tq32 = a2sb("tq32", [128, 976], I32)
                xqt = a2sb("xqt", [128, 976], BF16)
                Mx = a2sb("Mx", [128, 976], F32)

                stgA = [stage, stage2]
                tok_vd = [None] * 16
                tok_vc = [None] * 16
                for i, (dst, src_h, cbase, dc) in enumerate(
                        [(wv, wqkv_h, 2048, dc) for dc in range(8)] +
                        [(woutb, wout_h, 0, dc) for dc in range(8)]):
                    @dma_step(1)
                    def _(e, src_h=src_h, cbase=cbase, dc=dc, i=i):
                        return e.dma_start(out=stgA[i % 2][:, 0:1024],
                                           in_=src_h[dc * 128:(dc + 1) * 128, cbase:cbase + 1024])
                    tok_vd[i] = S.step("sync", _, dma=True, sem_key="sa%d" % (i % 2),
                                       after=[tok_vc[i - 2]] if i >= 2 else None)

                    def _(e, dst=dst, dc=dc, i=i):
                        return e.copy(out=dst[:, dc * 1024:(dc + 1) * 1024],
                                      in_=stgA[i % 2][:, 0:1024])
                    tok_vc[i] = S.step("scalar", _, after=[tok_vd[i]])

                n_tiles = (NQ + QT - 1) // QT
                t0_tok = S.last
                TK = {k: [None] * n_tiles for k in (
                    "xdma", "vmm", "vevac", "smm", "mask", "zmm", "recip", "amm",
                    "acp", "zmul", "oproj", "vres", "cast", "stats", "chain",
                    "sqrt", "mr", "bc1", "apply", "mxtr", "cols", "rowtr",
                    "rowcp", "bc2", "quant", "qsp")}
                vres2 = [vres, vresB]

                def front(it):
                    qs = QS0 + it * QT
                    qn = min(QT, QS0 + NQ - qs)
                    hs = min(qs - 3, TB - 128)
                    mask = bmask if it < n_tiles - 1 else bmask2
                    vr = vres2[it % 2]

                    @dma_step(1)
                    def _(e, qs=qs, qn=qn):
                        return e.dma_start(
                            out=xresA2[:].rearrange("p (c t) -> p c t", c=8)[:, :, 0:qn],
                            in_=xT_h.rearrange("(c p) t -> p c t", p=128)[:, :, qs:qs + qn])
                    TK["xdma"][it] = S.step("sync", _, dma=True, sem_key="ax",
                                            after=[TK["vres"][it - 1]] if it else [t0_tok])

                    def _(e, hs=hs):
                        mms = []
                        for dc in range(8):
                            for hb in range(2):
                                mms.append(e.matmul(
                                    out=pC[:, hb * 512:(hb + 1) * 512],
                                    lhsT=x_bf[:, dc * TB + hs: dc * TB + hs + 128],
                                    rhs=wv[:, dc * 1024 + hb * 512: dc * 1024 + (hb + 1) * 512],
                                    start=(dc == 0), stop=(dc == 7)))
                        return mms
                    TK["vmm"][it] = S.step("tensor", _,
                                           after=[TK["vres"][it - 1]] if it else [t0_tok])

                    def _(e):
                        return e.tensor_copy(out=v_sb[:], in_=pC[:, 0:1024])
                    TK["vevac"][it] = S.step("vector", _, after=[TK["vmm"][it]])

                    def _(e, hs=hs, qs=qs, qn=qn):
                        mms = []
                        for h in range(HEADS):
                            par = (h // 4) * 64
                            blk = (h % 4) * TB
                            mms.append(e.matmul(
                                out=pC[:, 1024 + h * 128: 1024 + h * 128 + qn],
                                lhsT=kphi[par:par + 64, blk + hs: blk + hs + 128],
                                rhs=qphi[par:par + 64, blk + qs: blk + qs + qn],
                                start=(h % 4 == 0), stop=(h % 4 == 3),
                                tile_position=(par, 0)))
                        return mms
                    TK["smm"][it] = S.step("tensor", _,
                                           after=[TK["recip"][it - 1]] if it else [t0_tok])

                    def _(e, mask=mask):
                        return e.tensor_tensor(out=Sm[:], in0=pC[:, 1024:2048],
                                               in1=mask[:], op=ALU.mult)
                    TK["mask"][it] = S.step("vector", _, after=[TK["smm"][it]])

                    def _(e, qn=qn):
                        mms = []
                        for h in range(HEADS):
                            mms.append(e.matmul(
                                out=pC[:, 1024 + h * 128: 1024 + h * 128 + qn],
                                lhsT=onesb[:, 0:128], rhs=Sm[:, h * 128: h * 128 + qn],
                                start=(h % 4 == 0), stop=False))
                            mms.append(e.matmul(
                                out=pC[:, 1024 + h * 128: 1024 + h * 128 + qn],
                                lhsT=onesb[:, 0:128], rhs=eps128[:, 0:qn],
                                start=False, stop=(h % 4 == 3)))
                        return mms
                    TK["zmm"][it] = S.step("tensor", _, after=[TK["mask"][it]])

                    def _(e):
                        return e.reciprocal(out=zinv[:], in_=pC[:, 1024:2048])
                    TK["recip"][it] = S.step("vector", _, after=[TK["zmm"][it]])

                    def _(e, qn=qn):
                        mms = []
                        for h in range(HEADS):
                            dst = pA if h < 4 else pB
                            off = (h % 4) * 128
                            mms.append(e.matmul(
                                out=dst[:, off: off + qn],
                                lhsT=v_sb[:, h * 128:(h + 1) * 128],
                                rhs=Sm[:, h * 128: h * 128 + qn],
                                start=(h % 4 == 0), stop=(h % 4 == 3)))
                        return mms
                    TK["amm"][it] = S.step("tensor", _,
                                           after=[TK["mask"][it], TK["vevac"][it]] +
                                           ([TK["acp"][it - 1]] if it else []))

                    def _(e):
                        return [e.copy(out=attn_cp[:, 0:512], in_=pA[:]),
                                e.copy(out=attn_cp[:, 512:1024], in_=pB[:])]
                    TK["acp"][it] = S.step("scalar", _, after=[TK["amm"][it]])

                    def _(e):
                        return e.tensor_tensor(out=attn_sb[:], in0=attn_cp[:],
                                               in1=zinv[:], op=ALU.mult)
                    TK["zmul"][it] = S.step("vector", _, after=[TK["acp"][it]])

                    def _(e, qn=qn):
                        mms = []
                        for h in range(HEADS):
                            for oc in range(8):
                                mms.append(e.matmul(
                                    out=pC[:, oc * 128: oc * 128 + qn],
                                    lhsT=woutb[:, h * 1024 + oc * 128: h * 1024 + (oc + 1) * 128],
                                    rhs=attn_sb[:, h * 128: h * 128 + qn],
                                    start=(h == 0 and oc % 4 == 0), stop=False))
                        for oc in range(8):
                            mms.append(e.matmul(
                                out=pC[:, oc * 128: oc * 128 + qn],
                                lhsT=boutpad[0:32, oc * 128:(oc + 1) * 128],
                                rhs=onesb[0:32, 0:qn],
                                start=False, stop=(oc % 4 == 3)))
                        return mms
                    TK["oproj"][it] = S.step("tensor", _,
                                             after=[TK["zmul"][it], TK["vevac"][it]])

                    def _(e, qn=qn, vr=vr):
                        return e.tensor_tensor(
                            out=vr[:].rearrange("p (c t) -> p c t", c=8)[:, :, 0:qn],
                            in0=pC[:, 0:1024].rearrange("p (c t) -> p c t", t=128)[:, :, 0:qn],
                            in1=xresA2[:].rearrange("p (c t) -> p c t", c=8)[:, :, 0:qn],
                            op=ALU.add)
                    TK["vres"][it] = S.step("vector", _,
                                            after=[TK["oproj"][it], TK["xdma"][it]] +
                                            ([TK["apply"][it - 2]] if it >= 2 else []))

                def tail(it):
                    qs = QS0 + it * QT
                    qn = min(QT, QS0 + NQ - qs)
                    sc = qs - QS0
                    vr = vres2[it % 2]

                    def _(e, qn=qn, vr=vr):
                        return [
                            e.copy(out=vbf[:], in_=vr[:]),
                            e.activation(out=v2bf[:], in_=vr[:], func=AF.Square),
                        ]
                    TK["cast"][it] = S.step("scalar", _,
                                            after=[TK["vres"][it]] +
                                            ([TK["quant"][it - 1]] if it else []))

                    def _(e, qn=qn):
                        mms = []
                        for dc in range(8):
                            mms.append(e.matmul(out=ps3[0:1, 0:qn], lhsT=onesb[:, 0:1],
                                                rhs=vbf[:, dc * QT: dc * QT + qn],
                                                start=(dc == 0), stop=False))
                        for dc in range(8):
                            mms.append(e.matmul(out=ps3[0:1, 128:128 + qn], lhsT=onesb[:, 0:1],
                                                rhs=v2bf[:, dc * QT: dc * QT + qn],
                                                start=False, stop=(dc == 7)))
                        return mms
                    TK["stats"][it] = S.step("tensor", _, after=[TK["cast"][it]])

                    def _(e, qn=qn):
                        return [
                            e.tensor_scalar(out=rows_t[0:1, 0:qn], in0=ps3[0:1, 0:qn],
                                            scalar1=1.0 / 1024.0, scalar2=None, op0=ALU.mult),
                            e.drain(),
                            e.tensor_tensor(out=rows_t[0:1, 512:512 + qn], in0=rows_t[0:1, 0:qn],
                                            in1=rows_t[0:1, 0:qn], op=ALU.mult),
                            e.drain(),
                            e.scalar_tensor_tensor(out=rows_t[0:1, 1024:1024 + qn],
                                                   in0=ps3[0:1, 128:128 + qn], scalar=1.0 / 1024.0,
                                                   in1=rows_t[0:1, 512:512 + qn],
                                                   op0=ALU.mult, op1=ALU.subtract),
                            e.drain(),
                            e.tensor_scalar(out=rows_t[0:1, 1024:1024 + qn],
                                            in0=rows_t[0:1, 1024:1024 + qn],
                                            scalar1=EPS_LN, scalar2=None, op0=ALU.add),
                            e.drain(),
                            e.reciprocal(out=rows_t[0:1, 1536:1536 + qn],
                                         in_=rows_t[0:1, 1024:1024 + qn]),
                        ]
                    TK["chain"][it] = S.step("vector", _, after=[TK["stats"][it]])

                    def _(e, qn=qn):
                        return e.activation(out=srcrow[0:1, 0:qn], in_=rows_t[0:1, 1536:1536 + qn],
                                            func=AF.Sqrt)
                    TK["sqrt"][it] = S.step("scalar", _, after=[TK["chain"][it]])

                    def _(e, qn=qn):
                        return e.tensor_tensor(out=srcrow[0:1, 128:128 + qn],
                                               in0=srcrow[0:1, 0:qn], in1=rows_t[0:1, 0:qn],
                                               op=ALU.mult)
                    TK["mr"][it] = S.step("vector", _, after=[TK["sqrt"][it]])

                    def _(e, qn=qn):
                        return [
                            e.matmul(out=ps4[:, 0:qn], lhsT=onesf[:], rhs=srcrow[:, 0:qn],
                                     start=True, stop=False),
                            e.matmul(out=ps4[:, 128:128 + qn], lhsT=onesf[:], rhs=srcrow[:, 128:128 + qn],
                                     start=False, stop=True),
                        ]
                    TK["bc1"][it] = S.step("tensor", _, after=[TK["mr"][it]])

                    def _(e, qn=qn, vr=vr):
                        v3 = vr[:].rearrange("p (c t) -> p c t", c=8)[:, :, 0:qn]
                        x3 = xln1t[:].rearrange("p (c t) -> p c t", c=8)[:, :, 0:qn]
                        xin = xln1t[:].rearrange("p (c t) -> p t c", c=8)[:, 0:qn, :]
                        return [
                            e.tensor_tensor(out=x3, in0=v3,
                                            in1=ps4[:, None, 0:qn].broadcast_to([128, 8, qn]),
                                            op=ALU.mult),
                            e.drain(),
                            e.tensor_tensor(out=x3, in0=x3,
                                            in1=ps4[:, None, 128:128 + qn].broadcast_to([128, 8, qn]),
                                            op=ALU.subtract),
                            e.drain(),
                            e.tensor_reduce(
                                out=Mx[:, 0:qn].rearrange("p (t o) -> p t o", o=1),
                                in_=xin, axis=AX.X, op=ALU.max, apply_absolute_value=True),
                        ]
                    TK["apply"][it] = S.step("vector", _, after=[TK["bc1"][it]])

                    @dma_step(8)
                    def _(e, sc=sc, qn=qn):
                        return [e.dma_start(out=xln1_s[dc * 128:(dc + 1) * 128, sc:sc + qn],
                                            in_=xln1t[:, dc * QT: dc * QT + qn])
                                for dc in range(8)]
                    S.step("gpsimd", _, dma=True, after=[TK["apply"][it]])

                    def _(e, qn=qn):
                        return e.transpose(out=ps4[0:qn, 384:512], in_=Mx[:, 0:qn], identity=ident[:])
                    TK["mxtr"][it] = S.step("tensor", _, after=[TK["apply"][it]])

                    def _(e, qn=qn):
                        return [
                            e.reduce_max(out=cols_t[0:qn, 0:1], in_=ps4[0:qn, 384:512], axis=AX.X),
                            e.drain(),
                            e.tensor_scalar(out=cols_t[0:qn, 1:2], in0=cols_t[0:qn, 0:1],
                                            scalar1=1e-5, scalar2=None, op0=ALU.max),
                            e.drain(),
                            e.tensor_scalar(out=cols_t[0:qn, 1:2], in0=cols_t[0:qn, 1:2],
                                            scalar1=1.0 / 127.0, scalar2=None, op0=ALU.mult),
                            e.drain(),
                            e.reciprocal(out=cols_t[0:qn, 2:3], in_=cols_t[0:qn, 1:2]),
                            e.drain(),
                            e.tensor_tensor(out=cols_t[0:qn, 3:4], in0=cols_t[0:qn, 1:2],
                                            in1=mc1col[0:qn, :], op=ALU.mult),
                        ]
                    TK["cols"][it] = S.step("vector", _, after=[TK["mxtr"][it]])

                    def _(e, qn=qn):
                        return [
                            e.transpose(out=ps3[0:1, 256:256 + qn], in_=cols_t[0:qn, 2:3],
                                        identity=ident[0:qn, 0:qn]),
                            e.transpose(out=ps3[0:1, 384:384 + qn], in_=cols_t[0:qn, 3:4],
                                        identity=ident[0:qn, 0:qn]),
                        ]
                    TK["rowtr"][it] = S.step("tensor", _, after=[TK["cols"][it]])

                    def _(e, qn=qn):
                        return [
                            e.tensor_copy(out=srcrow[0:1, 256:256 + qn], in_=ps3[0:1, 256:256 + qn]),
                            e.tensor_copy(out=srcrow[0:1, 384:384 + qn], in_=ps3[0:1, 384:384 + qn]),
                        ]
                    TK["rowcp"][it] = S.step("vector", _, after=[TK["rowtr"][it]])

                    def _(e, qn=qn):
                        return [
                            e.matmul(out=ps4[:, 256:256 + qn], lhsT=onesf[:],
                                     rhs=srcrow[:, 256:256 + qn], start=True, stop=False),
                            e.matmul(out=ps4[:, 384:384 + qn], lhsT=onesf[:],
                                     rhs=srcrow[:, 384:384 + qn], start=False, stop=True),
                        ]
                    TK["bc2"][it] = S.step("tensor", _, after=[TK["rowcp"][it]])

                    def _(e, qn=qn, it=it):
                        x3 = xln1t[:].rearrange("p (c t) -> p c t", c=8)[:, :, 0:qn]
                        t3 = tq[:].rearrange("p (c t) -> p c t", c=8)[:, :, 0:qn]
                        q3 = xqt[:].rearrange("p (c t) -> p c t", c=8)[:, :, 0:qn]
                        ops = [
                            e.tensor_tensor(out=t3, in0=x3,
                                            in1=ps4[:, None, 256:256 + qn].broadcast_to([128, 8, qn]),
                                            op=ALU.mult),
                            e.drain(),
                            e.tensor_copy(out=tq32[:], in_=tq[:]),
                            e.drain(),
                            e.tensor_copy(out=tq[:], in_=tq32[:]),
                            e.drain(),
                            e.tensor_tensor(out=q3, in0=t3,
                                            in1=ps4[:, None, 384:384 + qn].broadcast_to([128, 8, qn]),
                                            op=ALU.mult),
                        ]
                        if it == 0 or it == n_tiles - 1:
                            col = 0 if it == 0 else qn - 1
                            midx = 0 if it == 0 else 1
                            qm = xqt[:].rearrange("p (c t) -> p c t", c=8)[:, :, col:col + 1]
                            ops.append(e.drain())
                            ops.append(e.tensor_tensor(
                                out=qm, in0=qm,
                                in1=qmask[:, None, midx:midx + 1].broadcast_to([128, 8, 1]),
                                op=ALU.mult))
                        return ops
                    TK["quant"][it] = S.step("vector", _,
                                             after=[TK["bc2"][it]] +
                                             ([TK["qsp"][it - 1]] if it else []))

                    @dma_step(8)
                    def _(e, sc=sc, qn=qn):
                        return [e.dma_start(out=xq_s[dc * 128:(dc + 1) * 128, sc:sc + qn],
                                            in_=xqt[:, dc * QT: dc * QT + qn])
                                for dc in range(8)]
                    TK["qsp"][it] = S.step("gpsimd", _, dma=True, after=[TK["quant"][it]])

                front(0)
                for it in range(1, n_tiles):
                    front(it)
                    tail(it - 1)
                tail(n_tiles - 1)
                S.last = TK["qsp"][n_tiles - 1]

            PHASE_MARKS.append(("A2", S.nsteps))

        S.auto_pump = None
        drain_wq()

        with contextlib.ExitStack() as fctx:
            def fsb(name, shape, dt):
                return fctx.enter_context(nc.sbuf_tensor(name, shape, dt))
            convd = fsb("convd", [128, 96 * 128], BF16)
            convb = fsb("convb", [128, 32], F32)
            xqb = fsb("xqb", [128, 8 * 520], BF16)
            w1buf = fsb("w1buf", [128, 3 * 8 * 512], BF16)
            w2buf = fsb("w2buf", [128, 2 * 4 * 512], BF16)
            h_bf = fsb("h_bf", [128, 32 * HBW], BF16)
            h_silu = fsb("h_silu", [128, 32 * FT], BF16)
            M2 = fsb("M2", [128, 512], F32)
            tbuf = fsb("tbuf", [128, 4 * FT], F32)
            t32 = fsb("t32", [128, 4 * FT], I32)
            d2sb = fsb("d2sb", [128, 512], F32)
            xl1b = fsb("xl1b", [128, 8 * FT], F32)
            outst = fsb("outst", [128, 8 * FT], F32)
            vbf2 = fsb("vbf2", [128, 8 * FT], BF16)

            @dma_step(1)
            def _(e):
                return e.dma_start(out=convb[:], in_=convb_h[:])
            S.step("sync", _, dma=True)

            for part in range(6):
                @dma_step(1)
                def _(e, part=part):
                    return e.dma_start(out=tbuf[:, 0:2048],
                                       in_=convd_h[:, part * 2048:(part + 1) * 2048])
                S.step("sync", _, dma=True)

                def _(e, part=part):
                    return e.copy(out=convd[:, part * 2048:(part + 1) * 2048], in_=tbuf[:, 0:2048])
                S.step("scalar", _)

            tok_ln2_prev = None
            tok_evh_prev = None
            tok_w1dma = {}
            tok_w2dma = {}
            for jt in range(4):
                ts0 = 1 + jt * FT
                fn = FT
                hn = fn + 2

                @dma_step(2)
                def _(e, ts0=ts0, hn=hn, fn=fn):
                    return [
                        e.dma_start(
                            out=xqb[:].rearrange("p (c t) -> p c t", c=8)[:, :, 0:hn],
                            in_=xq_s.rearrange("(c p) t -> p c t", p=128)[:, :, ts0 - 1: ts0 - 1 + hn]),
                        e.dma_start(
                            out=xl1b[:].rearrange("p (c t) -> p c t", c=8),
                            in_=xln1_s.rearrange("(c p) t -> p c t", p=128)[:, :, ts0:ts0 + fn]),
                    ]
                tok_xq = S.step("sync", _, dma=True, sem_key="fx",
                                after=([tok_evh_prev] if tok_evh_prev else None))

                mm1_tok = [None] * 32
                evac_tok = [None] * 32
                conv_tok = [None] * 16
                silu_tok = [None] * 16

                def emit_mm1(k):
                    g = k // 4
                    if (k % 4) == 0:
                        @dma_step(1)
                        def _(e, g=g):
                            return e.dma_start(
                                out=w1buf[:, (g % 3) * 4096:(g % 3) * 4096 + 4096]
                                .rearrange("p (c t) -> p c t", c=8),
                                in_=w1q_s.rearrange("(c p) t -> p c t", p=128)[:, :, g * 512:(g + 1) * 512])
                        dep = [tok_xq]
                        if g >= 3:
                            dep.append(mm1_tok[4 * (g - 3) + 3])
                        tok_w1dma[g] = S.step("sync", _, dma=True, after=dep,
                                              sem_key="fw%d" % (g % 3))

                    base = (k % 2) * 1024

                    def _(e, k=k, g=g, base=base):
                        hcl = k % 4
                        wb = (g % 3) * 4096
                        mms = []
                        for dc in range(8):
                            mms.append(e.matmul(
                                out=pC[:, base: base + 258],
                                lhsT=w1buf[:, wb + dc * 512 + hcl * 128: wb + dc * 512 + (hcl + 1) * 128],
                                rhs=xqb[:, dc * 520: dc * 520 + 258],
                                start=(dc == 0), stop=(dc == 7)))
                        for dc in range(8):
                            mms.append(e.matmul(
                                out=pC[:, base + 512: base + 512 + 256],
                                lhsT=w1buf[:, wb + dc * 512 + hcl * 128: wb + dc * 512 + (hcl + 1) * 128],
                                rhs=xqb[:, dc * 520 + 258: dc * 520 + 514],
                                start=(dc == 0), stop=(dc == 7)))
                        return mms
                    dep = [tok_w1dma[g], tok_xq]
                    if k >= 2:
                        dep.append(evac_tok[k - 2])
                    mm1_tok[k] = S.step("tensor", _, after=dep)

                    def _(e, k=k, base=base):
                        hc = k
                        return [
                            e.copy(out=h_bf[:, hc * HBW: hc * HBW + 258], in_=pC[:, base: base + 258]),
                            e.copy(out=h_bf[:, hc * HBW + 258: hc * HBW + 514],
                                   in_=pC[:, base + 512: base + 512 + 256]),
                        ]
                    evac_tok[k] = S.step("scalar", _, after=[mm1_tok[k]])

                def emit_conv(ck):
                    psa, psb = (pA, pB) if (ck % 2) == 0 else (ps3, ps4)

                    def _(e, ck=ck, fn=fn, psa=psa, psb=psb):
                        mms = []
                        for cl in range(2):
                            c = 2 * ck + cl
                            po = psa if cl == 0 else psb
                            for tap in range(3):
                                mms.append(e.matmul(
                                    out=po[:, 0:fn],
                                    lhsT=convd[:, (3 * c + tap) * 128:(3 * c + tap + 1) * 128],
                                    rhs=h_bf[:, c * HBW + tap: c * HBW + tap + fn],
                                    start=(tap == 0), stop=(tap == 2)))
                        return mms
                    dep = [evac_tok[2 * ck + 1]]
                    if ck >= 2:
                        dep.append(silu_tok[ck - 2])
                    if ck < 2 and tok_ln2_prev is not None:
                        dep.append(tok_ln2_prev)
                    conv_tok[ck] = S.step("tensor", _, after=dep)

                    def _(e, ck=ck, fn=fn, psa=psa, psb=psb):
                        return [e.activation(out=h_silu[:, (2 * ck + cl) * FT: (2 * ck + cl) * FT + fn],
                                             in_=(psa if cl == 0 else psb)[:, 0:fn],
                                             func=AF.Silu, bias=convb[:, 2 * ck + cl: 2 * ck + cl + 1],
                                             scale=1.0)
                                for cl in range(2)]
                    silu_tok[ck] = S.step("scalar", _, after=[conv_tok[ck]])

                for k in range(32):
                    emit_mm1(k)
                    if k >= 3 and k % 2 == 1:
                        emit_conv((k - 3) // 2)
                emit_conv(15)

                def _(e, fn=fn):
                    hin = h_silu[:].rearrange("p (c t) -> p t c", c=32)
                    return e.tensor_reduce(
                        out=M2[:].rearrange("p (t o) -> p t o", o=1),
                        in_=hin, axis=AX.X, op=ALU.max, apply_absolute_value=True)
                tok_red = S.step("vector", _, after=[silu_tok[15]])

                def _(e):
                    return [e.transpose(out=ps4[0:128, t * 128:(t + 1) * 128],
                                        in_=M2[:, t * 128:(t + 1) * 128],
                                        identity=ident[:])
                            for t in range(4)]
                tok_tr = S.step("tensor", _, after=[tok_red, silu_tok[15]])

                def _(e):
                    ops = []
                    for t in range(4):
                        ops.append(e.reduce_max(out=cols_t[0:128, 4 + t:5 + t],
                                                in_=ps4[0:128, t * 128:(t + 1) * 128], axis=AX.X))
                    ops += [
                        e.drain(),
                        e.tensor_scalar(out=cols_t[:, 8:12], in0=cols_t[:, 4:8],
                                        scalar1=1e-5, scalar2=None, op0=ALU.max),
                        e.drain(),
                        e.tensor_scalar(out=cols_t[:, 8:12], in0=cols_t[:, 8:12],
                                        scalar1=1.0 / 127.0, scalar2=None, op0=ALU.mult),
                        e.drain(),
                        e.reciprocal(out=cols_t[:, 12:16], in_=cols_t[:, 8:12]),
                        e.drain(),
                        e.tensor_tensor(out=cols_t[:, 4:8], in0=cols_t[:, 8:12],
                                        in1=mc2col[:, 0:1].broadcast_to([128, 4]),
                                        op=ALU.mult),
                    ]
                    return ops
                tok_cols = S.step("vector", _, after=[tok_tr])

                def _(e):
                    return [e.transpose(out=ps3[0:1, t * 128:(t + 1) * 128],
                                        in_=cols_t[:, 12 + t:13 + t], identity=ident[:])
                            for t in range(4)]
                tok_s2t = S.step("tensor", _, after=[tok_cols, silu_tok[15]])

                def _(e):
                    return e.tensor_copy(out=srcrow[0:1, 0:512], in_=ps3[0:1, 0:512])
                tok_s2c = S.step("vector", _, after=[tok_s2t])

                def _(e, fn=fn):
                    return e.matmul(out=ps3[:, 0:fn], lhsT=onesf[:], rhs=srcrow[:, 0:fn],
                                    start=True, stop=True)
                tok_s2b = S.step("tensor", _, after=[tok_s2c])

                def _(e):
                    return [e.transpose(out=ps4[0:1, t * 128:(t + 1) * 128],
                                        in_=cols_t[:, 4 + t:5 + t], identity=ident[:])
                            for t in range(4)]
                tok_d2t = S.step("tensor", _, after=[tok_cols])

                def _(e):
                    return e.tensor_copy(out=srcrow[0:1, 0:512], in_=ps4[0:1, 0:512])
                tok_d2c = S.step("vector", _, after=[tok_d2t, tok_s2b])

                def _(e, fn=fn):
                    return e.matmul(out=ps4[:, 0:fn], lhsT=onesf[:], rhs=srcrow[:, 0:fn],
                                    start=True, stop=True)
                tok_d2b = S.step("tensor", _, after=[tok_d2c])

                def _(e, fn=fn):
                    return e.tensor_copy(out=d2sb[:, 0:fn], in_=ps4[:, 0:fn])
                tok_d2s = S.step("vector", _, after=[tok_d2b])

                qv_tok = [None] * 8
                qs_tok = [None] * 8
                for grp in range(8):
                    def _(e, grp=grp, fn=fn):
                        h3 = h_silu[:].rearrange("p (c t) -> p c t", c=32)[:, 4 * grp:4 * grp + 4, :]
                        t3 = tbuf[:].rearrange("p (c t) -> p c t", c=4)
                        return [
                            e.tensor_tensor(out=t3, in0=h3,
                                            in1=ps3[:, None, 0:fn].broadcast_to([128, 4, fn]),
                                            op=ALU.mult),
                            e.drain(),
                            e.tensor_copy(out=t32[:], in_=tbuf[:]),
                        ]
                    dep = [tok_s2b]
                    if grp >= 1:
                        dep.append(qs_tok[grp - 1])
                    qv_tok[grp] = S.step("vector", _, after=dep)

                    def _(e, grp=grp, fn=fn):
                        h3 = h_bf[:].rearrange("p (c t) -> p c t", t=HBW)[:, 4 * grp:4 * grp + 4, 0:fn]
                        t3 = t32[:].rearrange("p (c t) -> p c t", c=4)
                        return e.copy(out=h3, in_=t3)
                    qs_tok[grp] = S.step("scalar", _,
                                         after=[qv_tok[grp], conv_tok[2 * grp + 1],
                                                mm2_prev_reader.get(grp, None)] if False else
                                         [qv_tok[grp], conv_tok[2 * grp + 1]])

                mm2_tok = {}
                evh_tok = {}
                for half in range(2):
                    for g in range(8):
                        @dma_step(1)
                        def _(e, g=g, half=half):
                            gi = half * 8 + g
                            return e.dma_start(
                                out=w2buf[:, (gi % 2) * 2048:(gi % 2) * 2048 + 2048]
                                .rearrange("p (c t) -> p c t", c=4),
                                in_=w2q_s.rearrange("(c p) t -> p c t", p=128)
                                [:, 4 * g: 4 * (g + 1), half * 512:(half + 1) * 512])
                        gi = half * 8 + g
                        dep = [tok_xq]
                        if gi >= 2:
                            dep.append(mm2_tok[gi - 2])
                        tok_w2dma[gi] = S.step("sync", _, dma=True, after=dep,
                                               sem_key="gw%d" % (gi % 2))

                        def _(e, g=g, half=half, fn=fn):
                            gi = half * 8 + g
                            wb = (gi % 2) * 2048
                            mms = []
                            for hcl in range(4):
                                hc = 4 * g + hcl
                                for ocl in range(4):
                                    mms.append(e.matmul(
                                        out=pC[:, ocl * 512: ocl * 512 + fn],
                                        lhsT=w2buf[:, wb + hcl * 512 + ocl * 128: wb + hcl * 512 + (ocl + 1) * 128],
                                        rhs=h_bf[:, hc * HBW: hc * HBW + fn],
                                        start=(hc == 0), stop=(hc == 31)))
                            return mms
                        dep = [tok_w2dma[gi], qs_tok[g]]
                        if half == 0 and g == 0:
                            dep.append(silu_tok[15])
                            dep.append(evac_tok[31])
                        if half == 1 and g == 0:
                            dep.append(evh_tok[0])
                        mm2_tok[gi] = S.step("tensor", _, after=dep)

                    def _(e, half=half, fn=fn):
                        p3 = pC[:].rearrange("p (c t) -> p c t", t=512)
                        o3 = outst[:].rearrange("p (c t) -> p c t", c=8)[:, half * 4:(half + 1) * 4, :]
                        x3 = xl1b[:].rearrange("p (c t) -> p c t", c=8)[:, half * 4:(half + 1) * 4, :]
                        return [
                            e.tensor_tensor(out=o3, in0=p3,
                                            in1=d2sb[:, None, 0:fn].broadcast_to([128, 4, fn]),
                                            op=ALU.mult),
                            e.drain(),
                            e.tensor_tensor(out=o3, in0=o3, in1=x3, op=ALU.add),
                        ]
                    evh_tok[half] = S.step("vector", _,
                                           after=[mm2_tok[half * 8 + 7], tok_d2s])

                tok_evh_prev = evh_tok[1]

                def _(e, fn=fn):
                    return e.tensor_copy(out=vbf2[:], in_=outst[:])
                S.step("vector", _, after=[evh_tok[1]])

                def _(e, fn=fn):
                    return [e.matmul(out=ps3[0:1, 0:fn], lhsT=onesb[:, 0:1],
                                     rhs=vbf2[:, dc * FT: dc * FT + fn],
                                     start=(dc == 0), stop=(dc == 7))
                            for dc in range(8)]
                S.step("tensor", _, extra_waits=[("vector", S.cnt["vector"])])

                def _(e, fn=fn):
                    return e.activation(out=vbf2[:], in_=outst[:], func=AF.Square)
                S.step("scalar", _, extra_waits=[("tensor", S.cnt["tensor"])])

                def _(e, fn=fn):
                    return [e.matmul(out=ps4[0:1, 0:fn], lhsT=onesb[:, 0:1],
                                     rhs=vbf2[:, dc * FT: dc * FT + fn],
                                     start=(dc == 0), stop=(dc == 7))
                            for dc in range(8)]
                S.step("tensor", _)

                def _(e, fn=fn):
                    return [
                        e.tensor_scalar(out=rows_t[0:1, 0:fn], in0=ps3[0:1, 0:fn],
                                        scalar1=1.0 / 1024.0, scalar2=None, op0=ALU.mult),
                        e.drain(),
                        e.tensor_tensor(out=rows_t[0:1, 512:512 + fn], in0=rows_t[0:1, 0:fn],
                                        in1=rows_t[0:1, 0:fn], op=ALU.mult),
                        e.drain(),
                        e.scalar_tensor_tensor(out=rows_t[0:1, 1024:1024 + fn],
                                               in0=ps4[0:1, 0:fn], scalar=1.0 / 1024.0,
                                               in1=rows_t[0:1, 512:512 + fn],
                                               op0=ALU.mult, op1=ALU.subtract),
                        e.drain(),
                        e.tensor_scalar(out=rows_t[0:1, 1024:1024 + fn],
                                        in0=rows_t[0:1, 1024:1024 + fn],
                                        scalar1=EPS_LN, scalar2=None, op0=ALU.add),
                        e.drain(),
                        e.reciprocal(out=rows_t[0:1, 1536:1536 + fn],
                                     in_=rows_t[0:1, 1024:1024 + fn]),
                    ]
                S.step("vector", _)

                def _(e, fn=fn):
                    return e.activation(out=srcrow[0:1, 0:fn], in_=rows_t[0:1, 1536:1536 + fn],
                                        func=AF.Sqrt)
                S.step("scalar", _)

                def _(e, fn=fn):
                    return e.matmul(out=ps4[:, 0:fn], lhsT=onesf[:], rhs=srcrow[:, 0:fn],
                                    start=True, stop=True)
                S.step("tensor", _)

                def _(e, fn=fn):
                    return e.tensor_tensor(out=srcrow[0:1, 0:fn], in0=srcrow[0:1, 0:fn],
                                           in1=rows_t[0:1, 0:fn], op=ALU.mult)
                S.step("vector", _)

                def _(e, fn=fn):
                    return e.matmul(out=ps3[:, 0:fn], lhsT=onesf[:], rhs=srcrow[:, 0:fn],
                                    start=True, stop=True)
                S.step("tensor", _)

                def _(e, fn=fn):
                    o3 = outst[:].rearrange("p (c t) -> p c t", c=8)
                    return [
                        e.tensor_tensor(out=o3, in0=o3,
                                        in1=ps4[:, None, 0:fn].broadcast_to([128, 8, fn]),
                                        op=ALU.mult),
                        e.drain(),
                        e.tensor_tensor(out=o3, in0=o3,
                                        in1=ps3[:, None, 0:fn].broadcast_to([128, 8, fn]),
                                        op=ALU.subtract),
                    ]
                tok_ln2_prev = S.step("vector", _)

                @dma_step(1)
                def _(e, ts0=ts0, fn=fn):
                    return e.dma_start(
                        out=outT_h.rearrange("(c p) t -> p c t", p=128)[:, :, ts0 - 1: ts0 - 1 + fn],
                        in_=outst[:].rearrange("p (c t) -> p c t", c=8))
                S.step("gpsimd", _, dma=True, after=[tok_ln2_prev])
                S.last = tok_ln2_prev

        PHASE_MARKS.append(("FFN", S.nsteps))

        final_g = S.cnt["gpsimd"]

        def _(e):
            e.wait_ge(sems["gpsimd"], final_g)
            return e.engine_nop()
        S.step("gpsimd", _)

        with nc.Block() as block:
            @block.sync
            def _(eng):
                for op in S.ops["sync"]:
                    op(eng)

            @block.vector
            def _(eng):
                for op in S.ops["vector"]:
                    op(eng)

            @block.scalar
            def _(eng):
                for op in S.ops["scalar"]:
                    op(eng)

            @block.tensor
            def _(eng):
                for op in S.ops["tensor"]:
                    op(eng)

            @block.gpsimd
            def _(eng):
                for op in S.ops["gpsimd"]:
                    op(eng)

    return nc


def _band_mask(off, qn):
    m = np.zeros((128, 1024), np.float32)
    for h in range(8):
        mm = np.zeros((128, 128), np.float32)
        i = np.arange(128)[:, None]
        j = np.arange(128)[None, :]
        band = (np.abs(i - j - off) <= 3) & (j < qn)
        mm[band] = 1.0
        m[:, h * 128:(h + 1) * 128] = mm
    return m


def kernel(L0_attn_x, L0_attn_w_qkv, L0_attn_proj, L0_attn_w_out, L0_attn_b_out,
           L0_ffn_w1, L0_ffn_conv_w, L0_ffn_conv_b, L0_ffn_w2):
    x = np.ascontiguousarray(np.asarray(L0_attn_x, dtype=np.float32))
    wqkv = np.ascontiguousarray(np.asarray(L0_attn_w_qkv, dtype=np.float32))
    proj = np.ascontiguousarray(np.asarray(L0_attn_proj, dtype=np.float32))
    wout = np.ascontiguousarray(np.asarray(L0_attn_w_out, dtype=np.float32))
    bout = np.ascontiguousarray(np.asarray(L0_attn_b_out, dtype=np.float32))[None, :]
    w1 = np.asarray(L0_ffn_w1, dtype=np.float32)
    convw = np.asarray(L0_ffn_conv_w, dtype=np.float32)[:, 0, :]
    convb = np.asarray(L0_ffn_conv_b, dtype=np.float32)
    w2 = np.asarray(L0_ffn_w2, dtype=np.float32)

    w1T = np.ascontiguousarray(w1.T)
    w2T = np.ascontiguousarray(w2.T)

    convdiag = np.zeros((128, 96 * 128), np.float32)
    idx = np.arange(128)
    for c in range(32):
        for tap in range(3):
            blk = (3 * c + tap) * 128
            convdiag[idx, blk + idx] = convw[c * 128:(c + 1) * 128, tap]
    convb_m = np.ascontiguousarray(convb.reshape(32, 128).T)

    ident = np.eye(128, dtype=np.float32)
    n_tiles = (NQ + QT - 1) // QT
    qs_last = QS0 + (n_tiles - 1) * QT
    qn_last = QS0 + NQ - qs_last
    off_last = qs_last - min(qs_last - 3, TB - 128)
    bandmask = _band_mask(3, QT)
    bandmask2 = _band_mask(off_last, qn_last)

    in_maps = []
    for core in range(NCORES):
        b, r = core // 4, core % 4
        t0 = r * TOK
        lo, hi = t0 - HB, t0 + TOK + HB
        xT = np.zeros((D, TB), np.float32)
        s, e = max(lo, 0), min(hi, N)
        xT[:, s - lo: s - lo + (e - s)] = x[b, s:e, :].T
        kmask = np.zeros((128, 16), np.float32)
        kmask[:, 0:8] = 1.0 if lo >= 0 else 0.0
        kmask[:, 8:16] = 1.0 if hi <= N else 0.0
        qmask = np.zeros((128, 2), np.float32)
        qmask[:, 0] = 1.0 if t0 - 1 >= 0 else 0.0
        qmask[:, 1] = 1.0 if t0 + TOK + 1 <= N else 0.0
        in_maps.append({
            "xT": xT, "wqkv": wqkv, "proj": proj, "wout": wout, "boutrow": bout,
            "w1T": w1T, "w2T": w2T, "convdiag": convdiag, "convbH": convb_m,
            "identH": ident, "bandmask": bandmask, "bandmask2": bandmask2,
            "kmaskH": kmask, "qmaskH": qmask,
        })

    nc = build_program()
    res = run_bass_kernel_spmd(nc, in_maps, core_ids=list(range(NCORES)))

    out = np.empty((B, N, D), np.float32)
    for core in range(NCORES):
        b, r = core // 4, core % 4
        t0 = r * TOK
        out[b, t0:t0 + TOK, :] = res.results[core]["outT"].T
    return out


# version 24 (final)
# speedup vs baseline: 1.1131x; 1.0167x over previous
import numpy as np
import contextlib
import os
import concourse.bass as bass
import concourse.mybir as mybir
from concourse.bass_utils import run_bass_kernel_spmd

F32 = mybir.dt.float32
BF16 = mybir.dt.bfloat16
I32 = mybir.dt.int32
AF = mybir.ActivationFunctionType
ALU = mybir.AluOpType
AX = mybir.AxisListType

B, N, D = 2, 8192, 1024
NCORES = 8
TOK = 2048
HB = 8
TB = TOK + 2 * HB
HEADS, HD, FEAT = 8, 128, 64
DH = 4096
QT = 122
QS0, NQ = 7, 2050
FT = 512
HBW = 520
EPS_LN, EPS_Z = 1e-5, 1e-8

ENG = ("sync", "vector", "scalar", "tensor", "gpsimd")
BUILD_LIMIT = int(os.environ.get("KBISECT", "0")) or None
PHASE_MARKS = []


class Sched:

    def __init__(self, nc, sems):
        self.nc = nc
        self.sems = sems
        self.ops = {e: [] for e in ENG}
        self.cnt = {e: 0 for e in ENG}
        self.last = None
        self.nsteps = 0
        self.auto_pump = None
        self._pumping = False

    def step(self, eng, builder, dma=False, after=None, extra_waits=(), sem_key=None):
        self.nsteps += 1
        if BUILD_LIMIT is not None and self.nsteps > BUILD_LIMIT:
            return self.last
        if after is None:
            deps = [self.last] if self.last is not None else []
        elif isinstance(after, list):
            deps = list(after)
        else:
            deps = [after]
        deps = [d for d in deps if d is not None]
        deps += list(extra_waits)
        per = {}
        for (s, v) in deps:
            if s == eng:
                continue
            per[s] = max(per.get(s, 0), v)
        waits = sorted(per.items())
        key = sem_key if (dma and sem_key is not None) else eng
        sem = self.sems[key]
        if dma:
            nd = getattr(builder, "n_dma")
            self.cnt[key] += 16 * nd
            tok = (key, self.cnt[key])

            def run(e, waits=tuple(waits), builder=builder, nd=nd):
                for (s, v) in waits:
                    e.wait_ge(self.sems[s], v)
                insts = builder(e)
                if not isinstance(insts, (list, tuple)):
                    insts = [insts]
                assert len(insts) == nd
                for i in insts:
                    i.then_inc(sem, 16)

            self.ops[eng].append(run)
        else:
            self.cnt[eng] += 1
            tok = (eng, self.cnt[eng])

            def run(e, waits=tuple(waits), builder=builder, eng=eng):
                for (s, v) in waits:
                    e.wait_ge(self.sems[s], v)
                insts = builder(e)
                if not isinstance(insts, (list, tuple)):
                    insts = [insts]
                if eng in ("vector", "scalar", "gpsimd"):
                    e.drain().then_inc(sem, 1)
                else:
                    insts[-1].then_inc(sem, 1)

            self.ops[eng].append(run)
        self.last = tok
        if self.auto_pump is not None and not self._pumping:
            self.auto_pump()
        return tok


def dma_step(n):
    def deco(fn):
        fn.n_dma = n
        return fn
    return deco


def build_program():
    nc = bass.Bass("TRN2", target_bir_lowering=False)

    xT_h = nc.dram_tensor("xT", [D, TB], F32, kind="ExternalInput").ap()
    wqkv_h = nc.dram_tensor("wqkv", [D, 3 * D], F32, kind="ExternalInput").ap()
    proj_h = nc.dram_tensor("proj", [HD, FEAT], F32, kind="ExternalInput").ap()
    wout_h = nc.dram_tensor("wout", [D, D], F32, kind="ExternalInput").ap()
    bout_h = nc.dram_tensor("boutrow", [1, D], F32, kind="ExternalInput").ap()
    w1T_h = nc.dram_tensor("w1T", [D, DH], F32, kind="ExternalInput").ap()
    w2T_h = nc.dram_tensor("w2T", [DH, D], F32, kind="ExternalInput").ap()
    convd_h = nc.dram_tensor("convdiag", [128, 96 * 128], F32, kind="ExternalInput").ap()
    convb_h = nc.dram_tensor("convbH", [128, 32], F32, kind="ExternalInput").ap()
    ident_h = nc.dram_tensor("identH", [128, 128], F32, kind="ExternalInput").ap()
    bmask_h = nc.dram_tensor("bandmask", [128, 1024], F32, kind="ExternalInput").ap()
    bmask2_h = nc.dram_tensor("bandmask2", [128, 1024], F32, kind="ExternalInput").ap()
    kmask_h = nc.dram_tensor("kmaskH", [128, 16], F32, kind="ExternalInput").ap()
    qmask_h = nc.dram_tensor("qmaskH", [128, 2], F32, kind="ExternalInput").ap()
    outT_h = nc.dram_tensor("outT", [D, TOK], F32, kind="ExternalOutput").ap()
    _skind = "ExternalOutput" if os.environ.get("KDEBUG") else "Internal"
    xln1_s = nc.dram_tensor("xln1_s", [D, NQ], F32, kind=_skind).ap()
    xq_s = nc.dram_tensor("xq_s", [D, NQ], BF16, kind=_skind).ap()
    w1q_s = nc.dram_tensor("w1q_s", [D, DH], BF16, kind=_skind).ap()
    w2q_s = nc.dram_tensor("w2q_s", [DH, D], BF16, kind=_skind).ap()

    ctx = contextlib.ExitStack()
    with ctx:
        def sb(name, shape, dt):
            return ctx.enter_context(nc.sbuf_tensor(name, shape, dt))

        def psum(name, shape):
            return ctx.enter_context(nc.psum_tensor(name, shape, F32))

        ident = sb("ident", [128, 128], F32)
        onesb = sb("onesb", [128, 1024], BF16)
        onesf = sb("onesf", [128, 128], F32)
        eps128 = sb("eps128", [128, 128], BF16)
        c32 = sb("c32", [32, 520], BF16)
        pp = sb("pp", [128, 192], BF16)
        boutpad = sb("boutpad", [32, 1024], BF16)
        minus1 = sb("minus1", [128, 1], F32)
        c1024 = sb("c1024", [128, 1], F32)
        rows_t = sb("rows_t", [1, 2048], F32)
        srcrow = sb("srcrow", [128, 512], F32)
        cols_t = sb("cols_t", [128, 16], F32)
        mc1col = sb("mc1col", [128, 1], F32)
        mc2col = sb("mc2col", [128, 1], F32)
        ws1col = sb("ws1col", [128, 1], F32)
        ws2col = sb("ws2col", [128, 1], F32)
        wtot = sb("wtot", [1, 8], F32)
        kmask = sb("kmask", [128, 16], F32)
        qmask = sb("qmask", [128, 2], F32)

        pA = psum("pA", [128, 512])
        pB = psum("pB", [128, 512])
        pC = psum("pC", [128, 2048])
        ps3 = psum("ps3", [128, 512])
        ps4 = psum("ps4", [128, 512])

        sems = {}
        for e in ENG + ("wq0", "wq1", "wq2", "wq3", "fx", "fw0", "fw1", "fw2", "gw0", "gw1", "gw2", "ax", "sa0", "sa1"):
            sems[e] = ctx.enter_context(nc.semaphore("sem_" + e))
        S = Sched(nc, sems)
        for e in ("wq0", "wq1", "wq2", "wq3", "fx", "fw0", "fw1", "fw2", "gw0", "gw1", "gw2", "ax", "sa0", "sa1"):
            S.cnt[e] = 0

        @dma_step(4)
        def _(e):
            return [
                e.dma_start(out=ident[:], in_=ident_h[:]),
                e.dma_start(out=kmask[:], in_=kmask_h[:]),
                e.dma_start(out=qmask[:], in_=qmask_h[:]),
                e.dma_start(out=srcrow[:, 0:64], in_=proj_h[:]),
            ]
        S.step("sync", _, dma=True)

        def _(e):
            return [
                e.memset(onesb[:], 1.0),
                e.memset(onesf[:], 1.0),
                e.memset(eps128[:], EPS_Z / 128.0),
                e.memset(c32[:], 1.0 / 32.0),
                e.memset(pp[:], 0.0),
                e.memset(boutpad[:], 0.0),
                e.memset(minus1[:], -1.0),
                e.memset(c1024[:], 1.0 / 1024.0),
                e.memset(rows_t[:], 0.0),
                e.memset(cols_t[:], 0.0),
                e.drain(),
                e.tensor_copy(out=pp[:, 0:64], in_=srcrow[:, 0:64]),
                e.tensor_copy(out=pp[:, 128:192], in_=srcrow[:, 0:64]),
                e.drain(),
                e.memset(srcrow[:], 0.0),
            ]
        S.step("vector", _)

        PHASE_MARKS.append(("P0", S.nsteps))

        wq_done = [None, None]
        wq_mean = [None, None]

        def wquant_gen(wT_h, wq_s, n_iters, width, acc_base, mccol, wscol, slot,
                       rows128, colw, gidx):
            stg = [stageW, stageW2, stageW3, stageW4]
            tok_dma = [None] * n_iters
            tok_abs = [None] * n_iters

            def src_ap(it):
                rb, cb = it // colw, it % colw
                return wT_h[rb * 128:(rb + 1) * 128, cb * width:(cb + 1) * width]

            def dst_ap(it):
                rb, cb = it // colw, it % colw
                return wq_s[rb * 128:(rb + 1) * 128, cb * width:(cb + 1) * width]

            for it in range(n_iters):
                @dma_step(1)
                def _(e, it=it):
                    return e.dma_start(out=stg[it % 4][:, 0:width], in_=src_ap(it))
                if it >= 4:
                    dep = [tok_abs[it - 4]]
                elif gidx == 1:
                    dep = [wq_done[0]]
                else:
                    dep = [S.last]
                tok_dma[it] = S.step("sync", _, dma=True, after=dep,
                                     sem_key="wq%d" % (it % 4))

                def _(e, it=it):
                    return e.activation(out=wtmp[:, 0:width], in_=stg[it % 4][:, 0:width],
                                        func=AF.Abs, accum_out=wacc[:, acc_base + it: acc_base + it + 1])
                yield
                tok_abs[it] = S.step("scalar", _, after=[tok_dma[it]])
                yield

            def _(e):
                return e.reduce_sum(out=cols_t[:, 8:9], in_=wacc[:, acc_base:acc_base + n_iters], axis=AX.X)
            S.step("vector", _, after=[tok_abs[n_iters - 1]])

            def _(e):
                return e.transpose(out=ps3[0:1, 0:128], in_=cols_t[:, 8:9], identity=ident[:])
            S.step("tensor", _, after=[S.last])

            def _(e):
                return [
                    e.reduce_sum(out=wtot[0:1, slot:slot + 1], in_=ps3[0:1, 0:128], axis=AX.X),
                    e.drain(),
                    e.tensor_scalar(out=wtot[0:1, slot + 1:slot + 2], in0=wtot[0:1, slot:slot + 1],
                                    scalar1=1.0 / (n_iters * 128.0 * width), scalar2=None,
                                    op0=ALU.mult),
                    e.drain(),
                    e.tensor_scalar(out=wtot[0:1, slot + 1:slot + 2], in0=wtot[0:1, slot + 1:slot + 2],
                                    scalar1=1e-5, scalar2=None, op0=ALU.max),
                    e.drain(),
                    e.reciprocal(out=wtot[0:1, slot + 2:slot + 3], in_=wtot[0:1, slot + 1:slot + 2]),
                    e.drain(),
                    e.tensor_copy(out=srcrow[0:1, 0:1], in_=wtot[0:1, slot + 1:slot + 2]),
                    e.tensor_copy(out=srcrow[0:1, 1:2], in_=wtot[0:1, slot + 2:slot + 3]),
                ]
            S.step("vector", _, after=[S.last])

            def _(e):
                return e.matmul(out=ps4[:, 0:2], lhsT=onesf[:], rhs=srcrow[:, 0:2],
                                start=True, stop=True)
            S.step("tensor", _, after=[S.last])

            def _(e):
                return [
                    e.tensor_copy(out=mccol[:], in_=ps4[:, 0:1]),
                    e.tensor_copy(out=wscol[:], in_=ps4[:, 1:2]),
                ]
            tok_sc = S.step("vector", _, after=[S.last])
            wq_mean[gidx] = tok_sc
            yield

            tok_v = [None] * n_iters
            tok_s = [None] * n_iters
            tok_g = [None] * n_iters
            for it in range(n_iters):
                @dma_step(1)
                def _(e, it=it):
                    return e.dma_start(out=stg[it % 4][:, 0:width], in_=src_ap(it))
                tok_dma[it] = S.step("sync", _, dma=True,
                                     after=[tok_v[it - 4]] if it >= 4 else [tok_sc],
                                     sem_key="wq%d" % (it % 4))
                yield

                def _(e, it=it):
                    return e.activation(out=wtmp[:, 0:width], in_=stg[it % 4][:, 0:width],
                                        func=AF.Identity, scale=wscol[:])
                tok_sa = S.step("scalar", _,
                                after=[tok_dma[it], tok_sc] + ([tok_v[it - 1]] if it >= 1 else []))

                def _(e, it=it):
                    return e.tensor_scalar(out=w32[:, 0:width], in0=wtmp[:, 0:width],
                                           scalar1=1.0, scalar2=-1.0,
                                           op0=ALU.min, op1=ALU.max)
                tok_v[it] = S.step("vector", _,
                                   after=[tok_sa] + ([tok_s[it - 1]] if it >= 1 else []))

                def _(e, it=it):
                    return e.copy(out=wqout[:, 0:width], in_=w32[:, 0:width])
                tok_s[it] = S.step("scalar", _,
                                   after=[tok_v[it]] + ([tok_g[it - 1]] if it >= 1 else []))

                @dma_step(1)
                def _(e, it=it):
                    return e.dma_start(out=dst_ap(it), in_=wqout[:, 0:width])
                tok_g[it] = S.step("gpsimd", _, dma=True, after=[tok_s[it]])
                yield
            wq_done[gidx] = tok_g[n_iters - 1]

        wq_gens = [None, None]

        def pump_wq(n):
            saved = S.last
            S._pumping = True
            for _ in range(n):
                for gidx in range(2):
                    if wq_gens[gidx] is not None:
                        try:
                            next(wq_gens[gidx])
                            break
                        except StopIteration:
                            wq_gens[gidx] = None
            S._pumping = False
            S.last = saved

        def drain_wq():
            saved = S.last
            S._pumping = True
            for gidx in range(2):
                if wq_gens[gidx] is not None:
                    for _ in wq_gens[gidx]:
                        pass
                    wq_gens[gidx] = None
            S._pumping = False
            S.last = saved

            def _(e):
                return e.engine_nop()
            S.step("vector", _, after=[saved] + [t for t in wq_done if t is not None])

        PHASE_MARKS.append(("P1-wquant", S.nsteps))

        with contextlib.ExitStack() as actx:
            x_bf = actx.enter_context(nc.sbuf_tensor("x_bf", [128, 8 * TB], BF16))
            qphi = actx.enter_context(nc.sbuf_tensor("qphi", [128, 4 * TB], BF16))
            kphi = actx.enter_context(nc.sbuf_tensor("kphi", [128, 4 * TB], BF16))
            stage = actx.enter_context(nc.sbuf_tensor("stageA", [128, 1024], F32))
            stage2 = actx.enter_context(nc.sbuf_tensor("stageA2", [128, 1024], F32))
            wacc = actx.enter_context(nc.sbuf_tensor("wacc", [128, 80], F32))
            stageW = actx.enter_context(nc.sbuf_tensor("stageW", [128, 1024], F32))
            stageW2 = actx.enter_context(nc.sbuf_tensor("stageW2", [128, 1024], F32))
            stageW3 = actx.enter_context(nc.sbuf_tensor("stageW3", [128, 1024], F32))
            stageW4 = actx.enter_context(nc.sbuf_tensor("stageW4", [128, 1024], F32))
            wtmp = actx.enter_context(nc.sbuf_tensor("wtmp", [128, 1024], F32))
            w32 = actx.enter_context(nc.sbuf_tensor("w32", [128, 1024], I32))
            wqout = actx.enter_context(nc.sbuf_tensor("wqout", [128, 1024], BF16))
            wq_gens[0] = wquant_gen(w1T_h, w1q_s, 32, 1024, 0, mc1col, ws1col, 0, 8, 4, 0)
            wq_gens[1] = wquant_gen(w2T_h, w2q_s, 32, 1024, 32, mc2col, ws2col, 4, 32, 1, 1)
            bmask = actx.enter_context(nc.sbuf_tensor("bmask", [128, 1024], F32))
            bmask2 = actx.enter_context(nc.sbuf_tensor("bmask2", [128, 1024], F32))

            @dma_step(3)
            def _(e):
                return [e.dma_start(out=bmask[:], in_=bmask_h[:]),
                        e.dma_start(out=bmask2[:], in_=bmask2_h[:]),
                        e.dma_start(out=stage[0:1, 0:1024], in_=bout_h[:])]
            S.step("sync", _, dma=True)

            def _(e):
                return e.tensor_copy(out=boutpad[0:1, :], in_=stage[0:1, 0:1024])
            S.step("vector", _)

            with contextlib.ExitStack() as a1ctx:
                wqk = a1ctx.enter_context(nc.sbuf_tensor("wqk", [128, 8 * 2048], BF16))
                xres = a1ctx.enter_context(nc.sbuf_tensor("xres", [128, 8 * 528], F32))
                qtmp = a1ctx.enter_context(nc.sbuf_tensor("qtmp", [128, 8 * 512], BF16))
                stgA = [stage, stage2]
                tok_wd = [None] * 16
                tok_wc = [None] * 16
                for half in range(16):
                    @dma_step(1)
                    def _(e, half=half):
                        dc, hh = half // 2, half % 2
                        return e.dma_start(out=stgA[half % 2][:, 0:1024],
                                           in_=wqkv_h[dc * 128:(dc + 1) * 128, hh * 1024:(hh + 1) * 1024])
                    tok_wd[half] = S.step("sync", _, dma=True, sem_key="sa%d" % (half % 2),
                                          after=[tok_wc[half - 2]] if half >= 2 else None)

                    def _(e, half=half):
                        return e.copy(out=wqk[:, half * 1024:(half + 1) * 1024],
                                      in_=stgA[half % 2][:, 0:1024])
                    tok_wc[half] = S.step("scalar", _, after=[tok_wd[half]])

                S.auto_pump = lambda: pump_wq(3)
                tiles = [(t, min(512, TB - t)) for t in range(0, TB, 512)]
                t_castA1 = None
                t_sttPrev = None
                for (ts, tl) in tiles:
                    @dma_step(1)
                    def _(e, ts=ts, tl=tl):
                        return e.dma_start(
                            out=xres[:].rearrange("p (c t) -> p c t", c=8)[:, :, 0:tl],
                            in_=xT_h.rearrange("(c p) t -> p c t", p=128)[:, :, ts:ts + tl])
                    S.step("sync", _, dma=True, sem_key="ax",
                           after=[t_castA1] if t_castA1 else None)

                    def _(e, ts=ts, tl=tl):
                        return e.tensor_copy(
                            out=x_bf[:].rearrange("p (c t) -> p c t", c=8)[:, :, ts:ts + tl],
                            in_=xres[:].rearrange("p (c t) -> p c t", c=8)[:, :, 0:tl])
                    t_castA1 = S.step("vector", _)

                    for (wbase, phidst) in ((0, qphi), (1024, kphi)):
                        for hb in range(2):
                            def _(e, hb=hb, wbase=wbase, ts=ts, tl=tl):
                                mms = []
                                for hl in range(4):
                                    h = hb * 4 + hl
                                    for dc in range(8):
                                        mms.append(e.matmul(
                                            out=pC[:, hl * 512: hl * 512 + tl],
                                            lhsT=wqk[:, dc * 2048 + wbase + h * 128: dc * 2048 + wbase + (h + 1) * 128],
                                            rhs=x_bf[:, dc * TB + ts: dc * TB + ts + tl],
                                            start=(dc == 0), stop=(dc == 7)))
                                return mms
                            if wbase == 0 and hb == 0 and t_sttPrev is not None:
                                S.step("tensor", _, after=[S.last, t_sttPrev])
                            else:
                                S.step("tensor", _)

                            def _(e, hb=hb, tl=tl):
                                return [e.copy(out=qtmp[:, (hb * 4 + hl) * 512: (hb * 4 + hl) * 512 + tl],
                                               in_=pC[:, hl * 512: hl * 512 + tl])
                                        for hl in range(4)]
                            S.step("scalar", _)

                        def _(e, ts=ts, tl=tl):
                            mms = []
                            for hp in range(4):
                                mms.append(e.matmul(out=pC[:, hp * 512: hp * 512 + tl], lhsT=pp[:, 0:128],
                                                    rhs=qtmp[:, hp * 512: hp * 512 + tl],
                                                    start=True, stop=False))
                                mms.append(e.matmul(out=pC[:, hp * 512: hp * 512 + tl], lhsT=pp[:, 64:192],
                                                    rhs=qtmp[:, (hp + 4) * 512: (hp + 4) * 512 + tl],
                                                    start=False, stop=False))
                                mms.append(e.matmul(out=pC[:, hp * 512: hp * 512 + tl],
                                                    lhsT=onesb[0:32, 0:128],
                                                    rhs=c32[0:32, 0:tl], start=False, stop=True))
                            return mms
                        S.step("tensor", _)

                        def _(e, tl=tl):
                            return [e.activation(out=stgA[hp // 2][:, (hp % 2) * 512: (hp % 2) * 512 + tl],
                                                 in_=pC[:, hp * 512: hp * 512 + tl],
                                                 func=AF.Exp, bias=minus1[:], scale=1.0)
                                    for hp in range(4)]
                        S.step("scalar", _)

                        def _(e, ts=ts, tl=tl, phidst=phidst):
                            return [e.scalar_tensor_tensor(
                                out=phidst[:, hp * TB + ts: hp * TB + ts + tl],
                                in0=stgA[hp // 2][:, (hp % 2) * 512: (hp % 2) * 512 + tl], scalar=1.0,
                                in1=pC[:, hp * 512: hp * 512 + tl],
                                op0=ALU.min, op1=ALU.max) for hp in range(4)]
                        t_sttPrev = S.step("vector", _)

                def _(e):
                    km3 = kphi[:].rearrange("p (c t) -> p c t", c=4)
                    return [
                        e.tensor_tensor(out=km3[:, :, 0:HB], in0=km3[:, :, 0:HB],
                                        in1=kmask[:, None, 0:8].broadcast_to([128, 4, 8]),
                                        op=ALU.mult),
                        e.tensor_tensor(out=km3[:, :, TB - HB:TB], in0=km3[:, :, TB - HB:TB],
                                        in1=kmask[:, None, 8:16].broadcast_to([128, 4, 8]),
                                        op=ALU.mult),
                    ]
                S.step("vector", _)
                while wq_mean[1] is None and wq_gens[1] is not None:
                    pump_wq(1)
                S.auto_pump = lambda: pump_wq(1)

                def _(e):
                    return e.engine_nop()
                S.step("vector", _,
                       after=[S.last] + [t for t in wq_mean if t is not None])

            PHASE_MARKS.append(("A1", S.nsteps))

            with contextlib.ExitStack() as a2ctx:
                def a2sb(name, shape, dt):
                    return a2ctx.enter_context(nc.sbuf_tensor(name, shape, dt))
                wv = a2sb("wv", [128, 8 * 1024], BF16)
                woutb = a2sb("woutb", [128, 8 * 1024], BF16)
                v_sb = a2sb("v_sb", [128, 1024], BF16)
                Sm = a2sb("Sm", [128, 1024], BF16)
                zinv = a2sb("zinv", [128, 1024], F32)
                attn_cp = a2sb("attn_cp", [128, 1024], F32)
                attn_sb = a2sb("attn_sb", [128, 1024], BF16)
                vres = a2sb("vres", [128, 976], F32)
                vresB = a2sb("vresB", [128, 976], F32)
                xresA2 = a2sb("xresA2", [128, 976], F32)
                vbf = a2sb("vbf", [128, 976], BF16)
                v2bf = a2sb("v2bf", [128, 976], BF16)
                xln1t = a2sb("xln1t", [128, 976], F32)
                tq = a2sb("tq", [128, 976], F32)
                tq32 = a2sb("tq32", [128, 976], I32)
                xqt = a2sb("xqt", [128, 976], BF16)
                Mx = a2sb("Mx", [128, 976], F32)

                stgA = [stage, stage2]
                tok_vd = [None] * 16
                tok_vc = [None] * 16
                for i, (dst, src_h, cbase, dc) in enumerate(
                        [(wv, wqkv_h, 2048, dc) for dc in range(8)] +
                        [(woutb, wout_h, 0, dc) for dc in range(8)]):
                    @dma_step(1)
                    def _(e, src_h=src_h, cbase=cbase, dc=dc, i=i):
                        return e.dma_start(out=stgA[i % 2][:, 0:1024],
                                           in_=src_h[dc * 128:(dc + 1) * 128, cbase:cbase + 1024])
                    tok_vd[i] = S.step("sync", _, dma=True, sem_key="sa%d" % (i % 2),
                                       after=[tok_vc[i - 2]] if i >= 2 else None)

                    def _(e, dst=dst, dc=dc, i=i):
                        return e.copy(out=dst[:, dc * 1024:(dc + 1) * 1024],
                                      in_=stgA[i % 2][:, 0:1024])
                    tok_vc[i] = S.step("scalar", _, after=[tok_vd[i]])

                n_tiles = (NQ + QT - 1) // QT
                t0_tok = S.last
                TK = {k: [None] * n_tiles for k in (
                    "xdma", "vmm", "vevac", "smm", "mask", "zmm", "recip", "amm",
                    "acp", "zmul", "oproj", "vres", "cast", "stats", "chain",
                    "sqrt", "mr", "bc1", "apply", "spill", "mxtr", "cols", "rowtr",
                    "rowcp", "bc2", "quant", "qsp")}
                vres2 = [vres, vresB]

                def front(it):
                    qs = QS0 + it * QT
                    qn = min(QT, QS0 + NQ - qs)
                    hs = min(qs - 3, TB - 128)
                    mask = bmask if it < n_tiles - 1 else bmask2
                    vr = vres2[it % 2]

                    @dma_step(1)
                    def _(e, qs=qs, qn=qn):
                        return e.dma_start(
                            out=xresA2[:].rearrange("p (c t) -> p c t", c=8)[:, :, 0:qn],
                            in_=xT_h.rearrange("(c p) t -> p c t", p=128)[:, :, qs:qs + qn])
                    TK["xdma"][it] = S.step("sync", _, dma=True, sem_key="ax",
                                            after=[TK["vres"][it - 1]] if it else [t0_tok])

                    def _(e, hs=hs):
                        mms = []
                        for dc in range(8):
                            for hb in range(2):
                                mms.append(e.matmul(
                                    out=pC[:, hb * 512:(hb + 1) * 512],
                                    lhsT=x_bf[:, dc * TB + hs: dc * TB + hs + 128],
                                    rhs=wv[:, dc * 1024 + hb * 512: dc * 1024 + (hb + 1) * 512],
                                    start=(dc == 0), stop=(dc == 7)))
                        return mms
                    TK["vmm"][it] = S.step("tensor", _,
                                           after=[TK["vres"][it - 1]] if it else [t0_tok])

                    def _(e):
                        return e.tensor_copy(out=v_sb[:], in_=pC[:, 0:1024])
                    TK["vevac"][it] = S.step("vector", _, after=[TK["vmm"][it]])

                    def _(e, hs=hs, qs=qs, qn=qn):
                        mms = []
                        for h in range(HEADS):
                            par = (h // 4) * 64
                            blk = (h % 4) * TB
                            mms.append(e.matmul(
                                out=pC[:, 1024 + h * 128: 1024 + h * 128 + qn],
                                lhsT=kphi[par:par + 64, blk + hs: blk + hs + 128],
                                rhs=qphi[par:par + 64, blk + qs: blk + qs + qn],
                                start=(h % 4 == 0), stop=(h % 4 == 3),
                                tile_position=(par, 0)))
                        return mms
                    TK["smm"][it] = S.step("tensor", _,
                                           after=[TK["recip"][it - 1]] if it else [t0_tok])

                    def _(e, mask=mask):
                        return e.tensor_tensor(out=Sm[:], in0=pC[:, 1024:2048],
                                               in1=mask[:], op=ALU.mult)
                    TK["mask"][it] = S.step("vector", _, after=[TK["smm"][it]])

                    def _(e, qn=qn):
                        mms = []
                        for h in range(HEADS):
                            mms.append(e.matmul(
                                out=pC[:, 1024 + h * 128: 1024 + h * 128 + qn],
                                lhsT=onesb[:, 0:128], rhs=Sm[:, h * 128: h * 128 + qn],
                                start=(h % 4 == 0), stop=False))
                            mms.append(e.matmul(
                                out=pC[:, 1024 + h * 128: 1024 + h * 128 + qn],
                                lhsT=onesb[:, 0:128], rhs=eps128[:, 0:qn],
                                start=False, stop=(h % 4 == 3)))
                        return mms
                    TK["zmm"][it] = S.step("tensor", _, after=[TK["mask"][it]])

                    def _(e):
                        return e.reciprocal(out=zinv[:], in_=pC[:, 1024:2048])
                    TK["recip"][it] = S.step("vector", _, after=[TK["zmm"][it]])

                    def _(e, qn=qn):
                        mms = []
                        for h in range(HEADS):
                            dst = pA if h < 4 else pB
                            off = (h % 4) * 128
                            mms.append(e.matmul(
                                out=dst[:, off: off + qn],
                                lhsT=v_sb[:, h * 128:(h + 1) * 128],
                                rhs=Sm[:, h * 128: h * 128 + qn],
                                start=(h % 4 == 0), stop=(h % 4 == 3)))
                        return mms
                    TK["amm"][it] = S.step("tensor", _,
                                           after=[TK["mask"][it], TK["vevac"][it]] +
                                           ([TK["acp"][it - 1]] if it else []))

                    def _(e):
                        return [e.copy(out=attn_cp[:, 0:512], in_=pA[:]),
                                e.copy(out=attn_cp[:, 512:1024], in_=pB[:])]
                    TK["acp"][it] = S.step("scalar", _, after=[TK["amm"][it]])

                    def _(e):
                        return e.tensor_tensor(out=attn_sb[:], in0=attn_cp[:],
                                               in1=zinv[:], op=ALU.mult)
                    TK["zmul"][it] = S.step("vector", _, after=[TK["acp"][it]])

                    def _(e, qn=qn):
                        mms = []
                        for h in range(HEADS):
                            for oc in range(8):
                                mms.append(e.matmul(
                                    out=pC[:, oc * 128: oc * 128 + qn],
                                    lhsT=woutb[:, h * 1024 + oc * 128: h * 1024 + (oc + 1) * 128],
                                    rhs=attn_sb[:, h * 128: h * 128 + qn],
                                    start=(h == 0 and oc % 4 == 0), stop=False))
                        for oc in range(8):
                            mms.append(e.matmul(
                                out=pC[:, oc * 128: oc * 128 + qn],
                                lhsT=boutpad[0:32, oc * 128:(oc + 1) * 128],
                                rhs=onesb[0:32, 0:qn],
                                start=False, stop=(oc % 4 == 3)))
                        return mms
                    TK["oproj"][it] = S.step("tensor", _,
                                             after=[TK["zmul"][it], TK["vevac"][it]])

                    def _(e, qn=qn, vr=vr):
                        return e.tensor_tensor(
                            out=vr[:].rearrange("p (c t) -> p c t", c=8)[:, :, 0:qn],
                            in0=pC[:, 0:1024].rearrange("p (c t) -> p c t", t=128)[:, :, 0:qn],
                            in1=xresA2[:].rearrange("p (c t) -> p c t", c=8)[:, :, 0:qn],
                            op=ALU.add)
                    TK["vres"][it] = S.step("vector", _,
                                            after=[TK["oproj"][it], TK["xdma"][it]] +
                                            ([TK["apply"][it - 2]] if it >= 2 else []))

                def tail(it):
                    qs = QS0 + it * QT
                    qn = min(QT, QS0 + NQ - qs)
                    sc = qs - QS0
                    vr = vres2[it % 2]

                    def _(e, qn=qn, vr=vr):
                        return [
                            e.copy(out=vbf[:], in_=vr[:]),
                            e.activation(out=v2bf[:], in_=vr[:], func=AF.Square),
                        ]
                    TK["cast"][it] = S.step("scalar", _,
                                            after=[TK["vres"][it]] +
                                            ([TK["stats"][it - 1]] if it else []))

                    def _(e, qn=qn):
                        mms = []
                        for dc in range(8):
                            mms.append(e.matmul(out=ps3[0:1, 0:qn], lhsT=onesb[:, 0:1],
                                                rhs=vbf[:, dc * QT: dc * QT + qn],
                                                start=(dc == 0), stop=False))
                        for dc in range(8):
                            mms.append(e.matmul(out=ps3[0:1, 128:128 + qn], lhsT=onesb[:, 0:1],
                                                rhs=v2bf[:, dc * QT: dc * QT + qn],
                                                start=False, stop=(dc == 7)))
                        return mms
                    TK["stats"][it] = S.step("tensor", _,
                                             after=[TK["cast"][it]] +
                                             ([TK["rowcp"][it - 1]] if it else []))

                    def _(e, qn=qn):
                        return [
                            e.tensor_scalar(out=rows_t[0:1, 0:qn], in0=ps3[0:1, 0:qn],
                                            scalar1=1.0 / 1024.0, scalar2=None, op0=ALU.mult),
                            e.drain(),
                            e.tensor_tensor(out=rows_t[0:1, 512:512 + qn], in0=rows_t[0:1, 0:qn],
                                            in1=rows_t[0:1, 0:qn], op=ALU.mult),
                            e.drain(),
                            e.scalar_tensor_tensor(out=rows_t[0:1, 1024:1024 + qn],
                                                   in0=ps3[0:1, 128:128 + qn], scalar=1.0 / 1024.0,
                                                   in1=rows_t[0:1, 512:512 + qn],
                                                   op0=ALU.mult, op1=ALU.subtract),
                            e.drain(),
                            e.tensor_scalar(out=rows_t[0:1, 1024:1024 + qn],
                                            in0=rows_t[0:1, 1024:1024 + qn],
                                            scalar1=EPS_LN, scalar2=None, op0=ALU.add),
                            e.drain(),
                            e.reciprocal(out=rows_t[0:1, 1536:1536 + qn],
                                         in_=rows_t[0:1, 1024:1024 + qn]),
                        ]
                    TK["chain"][it] = S.step("vector", _,
                                             after=[TK["stats"][it]] +
                                             ([TK["mr"][it - 1]] if it else []))

                    def _(e, qn=qn):
                        return e.activation(out=srcrow[0:1, 0:qn], in_=rows_t[0:1, 1536:1536 + qn],
                                            func=AF.Sqrt)
                    TK["sqrt"][it] = S.step("scalar", _,
                                            after=[TK["chain"][it]] +
                                            ([TK["bc1"][it - 1]] if it else []))

                    def _(e, qn=qn):
                        return e.tensor_tensor(out=srcrow[0:1, 128:128 + qn],
                                               in0=srcrow[0:1, 0:qn], in1=rows_t[0:1, 0:qn],
                                               op=ALU.mult)
                    TK["mr"][it] = S.step("vector", _, after=[TK["sqrt"][it]])

                    def _(e, qn=qn):
                        return [
                            e.matmul(out=ps4[:, 0:qn], lhsT=onesf[:], rhs=srcrow[:, 0:qn],
                                     start=True, stop=False),
                            e.matmul(out=ps4[:, 128:128 + qn], lhsT=onesf[:], rhs=srcrow[:, 128:128 + qn],
                                     start=False, stop=True),
                        ]
                    TK["bc1"][it] = S.step("tensor", _,
                                           after=[TK["mr"][it]] +
                                           ([TK["quant"][it - 1]] if it else []))

                    def _(e, qn=qn, vr=vr):
                        v3 = vr[:].rearrange("p (c t) -> p c t", c=8)[:, :, 0:qn]
                        x3 = xln1t[:].rearrange("p (c t) -> p c t", c=8)[:, :, 0:qn]
                        xin = xln1t[:].rearrange("p (c t) -> p t c", c=8)[:, 0:qn, :]
                        return [
                            e.tensor_tensor(out=x3, in0=v3,
                                            in1=ps4[:, None, 0:qn].broadcast_to([128, 8, qn]),
                                            op=ALU.mult),
                            e.drain(),
                            e.tensor_tensor(out=x3, in0=x3,
                                            in1=ps4[:, None, 128:128 + qn].broadcast_to([128, 8, qn]),
                                            op=ALU.subtract),
                            e.drain(),
                            e.tensor_reduce(
                                out=Mx[:, 0:qn].rearrange("p (t o) -> p t o", o=1),
                                in_=xin, axis=AX.X, op=ALU.max, apply_absolute_value=True),
                        ]
                    TK["apply"][it] = S.step("vector", _,
                                             after=[TK["bc1"][it]] +
                                             ([TK["spill"][it - 1]] if it else []))

                    @dma_step(8)
                    def _(e, sc=sc, qn=qn):
                        return [e.dma_start(out=xln1_s[dc * 128:(dc + 1) * 128, sc:sc + qn],
                                            in_=xln1t[:, dc * QT: dc * QT + qn])
                                for dc in range(8)]
                    TK["spill"][it] = S.step("gpsimd", _, dma=True, after=[TK["apply"][it]])

                    def _(e, qn=qn):
                        return e.transpose(out=ps4[0:qn, 384:512], in_=Mx[:, 0:qn], identity=ident[:])
                    TK["mxtr"][it] = S.step("tensor", _, after=[TK["apply"][it]])

                    def _(e, qn=qn):
                        return [
                            e.reduce_max(out=cols_t[0:qn, 0:1], in_=ps4[0:qn, 384:512], axis=AX.X),
                            e.drain(),
                            e.tensor_scalar(out=cols_t[0:qn, 1:2], in0=cols_t[0:qn, 0:1],
                                            scalar1=1e-5, scalar2=None, op0=ALU.max),
                            e.drain(),
                            e.tensor_scalar(out=cols_t[0:qn, 1:2], in0=cols_t[0:qn, 1:2],
                                            scalar1=1.0 / 127.0, scalar2=None, op0=ALU.mult),
                            e.drain(),
                            e.reciprocal(out=cols_t[0:qn, 2:3], in_=cols_t[0:qn, 1:2]),
                            e.drain(),
                            e.tensor_tensor(out=cols_t[0:qn, 3:4], in0=cols_t[0:qn, 1:2],
                                            in1=mc1col[0:qn, :], op=ALU.mult),
                        ]
                    TK["cols"][it] = S.step("vector", _, after=[TK["mxtr"][it]])

                    def _(e, qn=qn):
                        return [
                            e.transpose(out=ps3[0:1, 256:256 + qn], in_=cols_t[0:qn, 2:3],
                                        identity=ident[0:qn, 0:qn]),
                            e.transpose(out=ps3[0:1, 384:384 + qn], in_=cols_t[0:qn, 3:4],
                                        identity=ident[0:qn, 0:qn]),
                        ]
                    TK["rowtr"][it] = S.step("tensor", _, after=[TK["cols"][it]])

                    def _(e, qn=qn):
                        return [
                            e.tensor_copy(out=srcrow[0:1, 256:256 + qn], in_=ps3[0:1, 256:256 + qn]),
                            e.tensor_copy(out=srcrow[0:1, 384:384 + qn], in_=ps3[0:1, 384:384 + qn]),
                        ]
                    TK["rowcp"][it] = S.step("vector", _, after=[TK["rowtr"][it]])

                    def _(e, qn=qn):
                        return [
                            e.matmul(out=ps4[:, 256:256 + qn], lhsT=onesf[:],
                                     rhs=srcrow[:, 256:256 + qn], start=True, stop=False),
                            e.matmul(out=ps4[:, 384:384 + qn], lhsT=onesf[:],
                                     rhs=srcrow[:, 384:384 + qn], start=False, stop=True),
                        ]
                    TK["bc2"][it] = S.step("tensor", _, after=[TK["rowcp"][it]])

                    def _(e, qn=qn, it=it):
                        x3 = xln1t[:].rearrange("p (c t) -> p c t", c=8)[:, :, 0:qn]
                        t3 = tq[:].rearrange("p (c t) -> p c t", c=8)[:, :, 0:qn]
                        q3 = xqt[:].rearrange("p (c t) -> p c t", c=8)[:, :, 0:qn]
                        ops = [
                            e.tensor_tensor(out=t3, in0=x3,
                                            in1=ps4[:, None, 256:256 + qn].broadcast_to([128, 8, qn]),
                                            op=ALU.mult),
                            e.drain(),
                            e.tensor_copy(out=tq32[:], in_=tq[:]),
                            e.drain(),
                            e.tensor_copy(out=tq[:], in_=tq32[:]),
                            e.drain(),
                            e.tensor_tensor(out=q3, in0=t3,
                                            in1=ps4[:, None, 384:384 + qn].broadcast_to([128, 8, qn]),
                                            op=ALU.mult),
                        ]
                        if it == 0 or it == n_tiles - 1:
                            col = 0 if it == 0 else qn - 1
                            midx = 0 if it == 0 else 1
                            qm = xqt[:].rearrange("p (c t) -> p c t", c=8)[:, :, col:col + 1]
                            ops.append(e.drain())
                            ops.append(e.tensor_tensor(
                                out=qm, in0=qm,
                                in1=qmask[:, None, midx:midx + 1].broadcast_to([128, 8, 1]),
                                op=ALU.mult))
                        return ops
                    TK["quant"][it] = S.step("vector", _,
                                             after=[TK["bc2"][it]] +
                                             ([TK["qsp"][it - 1]] if it else []))

                    @dma_step(8)
                    def _(e, sc=sc, qn=qn):
                        return [e.dma_start(out=xq_s[dc * 128:(dc + 1) * 128, sc:sc + qn],
                                            in_=xqt[:, dc * QT: dc * QT + qn])
                                for dc in range(8)]
                    TK["qsp"][it] = S.step("gpsimd", _, dma=True, after=[TK["quant"][it]])

                front(0)
                for it in range(1, n_tiles):
                    front(it)
                    tail(it - 1)
                tail(n_tiles - 1)
                S.last = TK["qsp"][n_tiles - 1]

            PHASE_MARKS.append(("A2", S.nsteps))

        S.auto_pump = None
        drain_wq()

        with contextlib.ExitStack() as fctx:
            def fsb(name, shape, dt):
                return fctx.enter_context(nc.sbuf_tensor(name, shape, dt))
            convd = fsb("convd", [128, 96 * 128], BF16)
            convb = fsb("convb", [128, 32], F32)
            xqb = fsb("xqb", [128, 8 * 520], BF16)
            w1buf = fsb("w1buf", [128, 3 * 8 * 512], BF16)
            w2buf = fsb("w2buf", [128, 2 * 4 * 512], BF16)
            h_bf = fsb("h_bf", [128, 32 * HBW], BF16)
            h_silu = fsb("h_silu", [128, 32 * FT], BF16)
            M2 = fsb("M2", [128, 512], F32)
            tbuf = fsb("tbuf", [128, 4 * FT], F32)
            t32 = fsb("t32", [128, 4 * FT], I32)
            d2sb = fsb("d2sb", [128, 512], F32)
            xl1b = fsb("xl1b", [128, 8 * FT], F32)
            outst = fsb("outst", [128, 8 * FT], F32)
            vbf2 = fsb("vbf2", [128, 8 * FT], BF16)

            @dma_step(1)
            def _(e):
                return e.dma_start(out=convb[:], in_=convb_h[:])
            S.step("sync", _, dma=True)

            for part in range(6):
                @dma_step(1)
                def _(e, part=part):
                    return e.dma_start(out=tbuf[:, 0:2048],
                                       in_=convd_h[:, part * 2048:(part + 1) * 2048])
                S.step("sync", _, dma=True)

                def _(e, part=part):
                    return e.copy(out=convd[:, part * 2048:(part + 1) * 2048], in_=tbuf[:, 0:2048])
                S.step("scalar", _)

            tok_ln2_prev = None
            tok_evh_prev = None
            tok_w1dma = {}
            tok_w2dma = {}
            for jt in range(4):
                ts0 = 1 + jt * FT
                fn = FT
                hn = fn + 2

                @dma_step(2)
                def _(e, ts0=ts0, hn=hn, fn=fn):
                    return [
                        e.dma_start(
                            out=xqb[:].rearrange("p (c t) -> p c t", c=8)[:, :, 0:hn],
                            in_=xq_s.rearrange("(c p) t -> p c t", p=128)[:, :, ts0 - 1: ts0 - 1 + hn]),
                        e.dma_start(
                            out=xl1b[:].rearrange("p (c t) -> p c t", c=8),
                            in_=xln1_s.rearrange("(c p) t -> p c t", p=128)[:, :, ts0:ts0 + fn]),
                    ]
                tok_xq = S.step("sync", _, dma=True, sem_key="fx",
                                after=([tok_evh_prev] if tok_evh_prev else None))

                mm1_tok = [None] * 32
                evac_tok = [None] * 32
                conv_tok = [None] * 16
                silu_tok = [None] * 16

                def emit_mm1(k):
                    g = k // 4
                    if (k % 4) == 0:
                        @dma_step(1)
                        def _(e, g=g):
                            return e.dma_start(
                                out=w1buf[:, (g % 3) * 4096:(g % 3) * 4096 + 4096]
                                .rearrange("p (c t) -> p c t", c=8),
                                in_=w1q_s.rearrange("(c p) t -> p c t", p=128)[:, :, g * 512:(g + 1) * 512])
                        dep = [tok_xq]
                        if g >= 3:
                            dep.append(mm1_tok[4 * (g - 3) + 3])
                        tok_w1dma[g] = S.step("sync", _, dma=True, after=dep,
                                              sem_key="fw%d" % (g % 3))

                    base = (k % 2) * 1024

                    def _(e, k=k, g=g, base=base):
                        hcl = k % 4
                        wb = (g % 3) * 4096
                        mms = []
                        for dc in range(8):
                            mms.append(e.matmul(
                                out=pC[:, base: base + 258],
                                lhsT=w1buf[:, wb + dc * 512 + hcl * 128: wb + dc * 512 + (hcl + 1) * 128],
                                rhs=xqb[:, dc * 520: dc * 520 + 258],
                                start=(dc == 0), stop=(dc == 7)))
                        for dc in range(8):
                            mms.append(e.matmul(
                                out=pC[:, base + 512: base + 512 + 256],
                                lhsT=w1buf[:, wb + dc * 512 + hcl * 128: wb + dc * 512 + (hcl + 1) * 128],
                                rhs=xqb[:, dc * 520 + 258: dc * 520 + 514],
                                start=(dc == 0), stop=(dc == 7)))
                        return mms
                    dep = [tok_w1dma[g], tok_xq]
                    if k >= 2:
                        dep.append(evac_tok[k - 2])
                    mm1_tok[k] = S.step("tensor", _, after=dep)

                    def _(e, k=k, base=base):
                        hc = k
                        return [
                            e.copy(out=h_bf[:, hc * HBW: hc * HBW + 258], in_=pC[:, base: base + 258]),
                            e.copy(out=h_bf[:, hc * HBW + 258: hc * HBW + 514],
                                   in_=pC[:, base + 512: base + 512 + 256]),
                        ]
                    evac_tok[k] = S.step("scalar", _, after=[mm1_tok[k]])

                def emit_conv(ck):
                    psa, psb = (pA, pB) if (ck % 2) == 0 else (ps3, ps4)

                    def _(e, ck=ck, fn=fn, psa=psa, psb=psb):
                        mms = []
                        for cl in range(2):
                            c = 2 * ck + cl
                            po = psa if cl == 0 else psb
                            for tap in range(3):
                                mms.append(e.matmul(
                                    out=po[:, 0:fn],
                                    lhsT=convd[:, (3 * c + tap) * 128:(3 * c + tap + 1) * 128],
                                    rhs=h_bf[:, c * HBW + tap: c * HBW + tap + fn],
                                    start=(tap == 0), stop=(tap == 2)))
                        return mms
                    dep = [evac_tok[2 * ck + 1]]
                    if ck >= 2:
                        dep.append(silu_tok[ck - 2])
                    if ck < 2 and tok_ln2_prev is not None:
                        dep.append(tok_ln2_prev)
                    conv_tok[ck] = S.step("tensor", _, after=dep)

                    def _(e, ck=ck, fn=fn, psa=psa, psb=psb):
                        return [e.activation(out=h_silu[:, (2 * ck + cl) * FT: (2 * ck + cl) * FT + fn],
                                             in_=(psa if cl == 0 else psb)[:, 0:fn],
                                             func=AF.Silu, bias=convb[:, 2 * ck + cl: 2 * ck + cl + 1],
                                             scale=1.0)
                                for cl in range(2)]
                    silu_tok[ck] = S.step("scalar", _, after=[conv_tok[ck]])

                for k in range(32):
                    emit_mm1(k)
                    if k >= 3 and k % 2 == 1:
                        emit_conv((k - 3) // 2)
                emit_conv(15)

                def _(e, fn=fn):
                    hin = h_silu[:].rearrange("p (c t) -> p t c", c=32)
                    return e.tensor_reduce(
                        out=M2[:].rearrange("p (t o) -> p t o", o=1),
                        in_=hin, axis=AX.X, op=ALU.max, apply_absolute_value=True)
                tok_red = S.step("vector", _, after=[silu_tok[15]])

                def _(e):
                    return [e.transpose(out=ps4[0:128, t * 128:(t + 1) * 128],
                                        in_=M2[:, t * 128:(t + 1) * 128],
                                        identity=ident[:])
                            for t in range(4)]
                tok_tr = S.step("tensor", _, after=[tok_red, silu_tok[15]])

                def _(e):
                    ops = []
                    for t in range(4):
                        ops.append(e.reduce_max(out=cols_t[0:128, 4 + t:5 + t],
                                                in_=ps4[0:128, t * 128:(t + 1) * 128], axis=AX.X))
                    ops += [
                        e.drain(),
                        e.tensor_scalar(out=cols_t[:, 8:12], in0=cols_t[:, 4:8],
                                        scalar1=1e-5, scalar2=None, op0=ALU.max),
                        e.drain(),
                        e.tensor_scalar(out=cols_t[:, 8:12], in0=cols_t[:, 8:12],
                                        scalar1=1.0 / 127.0, scalar2=None, op0=ALU.mult),
                        e.drain(),
                        e.reciprocal(out=cols_t[:, 12:16], in_=cols_t[:, 8:12]),
                        e.drain(),
                        e.tensor_tensor(out=cols_t[:, 4:8], in0=cols_t[:, 8:12],
                                        in1=mc2col[:, 0:1].broadcast_to([128, 4]),
                                        op=ALU.mult),
                    ]
                    return ops
                tok_cols = S.step("vector", _, after=[tok_tr])

                def _(e):
                    return [e.transpose(out=ps3[0:1, t * 128:(t + 1) * 128],
                                        in_=cols_t[:, 12 + t:13 + t], identity=ident[:])
                            for t in range(4)]
                tok_s2t = S.step("tensor", _, after=[tok_cols, silu_tok[15]])

                def _(e):
                    return e.tensor_copy(out=srcrow[0:1, 0:512], in_=ps3[0:1, 0:512])
                tok_s2c = S.step("vector", _, after=[tok_s2t])

                def _(e, fn=fn):
                    return e.matmul(out=ps3[:, 0:fn], lhsT=onesf[:], rhs=srcrow[:, 0:fn],
                                    start=True, stop=True)
                tok_s2b = S.step("tensor", _, after=[tok_s2c])

                def _(e):
                    return [e.transpose(out=ps4[0:1, t * 128:(t + 1) * 128],
                                        in_=cols_t[:, 4 + t:5 + t], identity=ident[:])
                            for t in range(4)]
                tok_d2t = S.step("tensor", _, after=[tok_cols])

                def _(e):
                    return e.tensor_copy(out=srcrow[0:1, 0:512], in_=ps4[0:1, 0:512])
                tok_d2c = S.step("vector", _, after=[tok_d2t, tok_s2b])

                def _(e, fn=fn):
                    return e.matmul(out=ps4[:, 0:fn], lhsT=onesf[:], rhs=srcrow[:, 0:fn],
                                    start=True, stop=True)
                tok_d2b = S.step("tensor", _, after=[tok_d2c])

                def _(e, fn=fn):
                    return e.tensor_copy(out=d2sb[:, 0:fn], in_=ps4[:, 0:fn])
                tok_d2s = S.step("vector", _, after=[tok_d2b])

                qv_tok = [None] * 8
                qs_tok = [None] * 8
                for grp in range(8):
                    def _(e, grp=grp, fn=fn):
                        h3 = h_silu[:].rearrange("p (c t) -> p c t", c=32)[:, 4 * grp:4 * grp + 4, :]
                        t3 = tbuf[:].rearrange("p (c t) -> p c t", c=4)
                        return [
                            e.tensor_tensor(out=t3, in0=h3,
                                            in1=ps3[:, None, 0:fn].broadcast_to([128, 4, fn]),
                                            op=ALU.mult),
                            e.drain(),
                            e.tensor_copy(out=t32[:], in_=tbuf[:]),
                        ]
                    dep = [tok_s2b]
                    if grp >= 1:
                        dep.append(qs_tok[grp - 1])
                    qv_tok[grp] = S.step("vector", _, after=dep)

                    def _(e, grp=grp, fn=fn):
                        h3 = h_bf[:].rearrange("p (c t) -> p c t", t=HBW)[:, 4 * grp:4 * grp + 4, 0:fn]
                        t3 = t32[:].rearrange("p (c t) -> p c t", c=4)
                        return e.copy(out=h3, in_=t3)
                    qs_tok[grp] = S.step("scalar", _,
                                         after=[qv_tok[grp], conv_tok[2 * grp + 1],
                                                mm2_prev_reader.get(grp, None)] if False else
                                         [qv_tok[grp], conv_tok[2 * grp + 1]])

                mm2_tok = {}
                evh_tok = {}
                for half in range(2):
                    for g in range(8):
                        @dma_step(1)
                        def _(e, g=g, half=half):
                            gi = half * 8 + g
                            return e.dma_start(
                                out=w2buf[:, (gi % 2) * 2048:(gi % 2) * 2048 + 2048]
                                .rearrange("p (c t) -> p c t", c=4),
                                in_=w2q_s.rearrange("(c p) t -> p c t", p=128)
                                [:, 4 * g: 4 * (g + 1), half * 512:(half + 1) * 512])
                        gi = half * 8 + g
                        dep = [tok_xq]
                        if gi >= 2:
                            dep.append(mm2_tok[gi - 2])
                        tok_w2dma[gi] = S.step("sync", _, dma=True, after=dep,
                                               sem_key="gw%d" % (gi % 2))

                        def _(e, g=g, half=half, fn=fn):
                            gi = half * 8 + g
                            wb = (gi % 2) * 2048
                            mms = []
                            for hcl in range(4):
                                hc = 4 * g + hcl
                                for ocl in range(4):
                                    mms.append(e.matmul(
                                        out=pC[:, ocl * 512: ocl * 512 + fn],
                                        lhsT=w2buf[:, wb + hcl * 512 + ocl * 128: wb + hcl * 512 + (ocl + 1) * 128],
                                        rhs=h_bf[:, hc * HBW: hc * HBW + fn],
                                        start=(hc == 0), stop=(hc == 31)))
                            return mms
                        dep = [tok_w2dma[gi], qs_tok[g]]
                        if half == 0 and g == 0:
                            dep.append(silu_tok[15])
                            dep.append(evac_tok[31])
                        if half == 1 and g == 0:
                            dep.append(evh_tok[0])
                        mm2_tok[gi] = S.step("tensor", _, after=dep)

                    def _(e, half=half, fn=fn):
                        p3 = pC[:].rearrange("p (c t) -> p c t", t=512)
                        o3 = outst[:].rearrange("p (c t) -> p c t", c=8)[:, half * 4:(half + 1) * 4, :]
                        x3 = xl1b[:].rearrange("p (c t) -> p c t", c=8)[:, half * 4:(half + 1) * 4, :]
                        return [
                            e.tensor_tensor(out=o3, in0=p3,
                                            in1=d2sb[:, None, 0:fn].broadcast_to([128, 4, fn]),
                                            op=ALU.mult),
                            e.drain(),
                            e.tensor_tensor(out=o3, in0=o3, in1=x3, op=ALU.add),
                        ]
                    evh_tok[half] = S.step("vector", _,
                                           after=[mm2_tok[half * 8 + 7], tok_d2s])

                tok_evh_prev = evh_tok[1]

                def _(e, fn=fn):
                    return e.tensor_copy(out=vbf2[:], in_=outst[:])
                S.step("vector", _, after=[evh_tok[1]])

                def _(e, fn=fn):
                    return [e.matmul(out=ps3[0:1, 0:fn], lhsT=onesb[:, 0:1],
                                     rhs=vbf2[:, dc * FT: dc * FT + fn],
                                     start=(dc == 0), stop=(dc == 7))
                            for dc in range(8)]
                S.step("tensor", _, extra_waits=[("vector", S.cnt["vector"])])

                def _(e, fn=fn):
                    return e.activation(out=vbf2[:], in_=outst[:], func=AF.Square)
                S.step("scalar", _, extra_waits=[("tensor", S.cnt["tensor"])])

                def _(e, fn=fn):
                    return [e.matmul(out=ps4[0:1, 0:fn], lhsT=onesb[:, 0:1],
                                     rhs=vbf2[:, dc * FT: dc * FT + fn],
                                     start=(dc == 0), stop=(dc == 7))
                            for dc in range(8)]
                S.step("tensor", _)

                def _(e, fn=fn):
                    return [
                        e.tensor_scalar(out=rows_t[0:1, 0:fn], in0=ps3[0:1, 0:fn],
                                        scalar1=1.0 / 1024.0, scalar2=None, op0=ALU.mult),
                        e.drain(),
                        e.tensor_tensor(out=rows_t[0:1, 512:512 + fn], in0=rows_t[0:1, 0:fn],
                                        in1=rows_t[0:1, 0:fn], op=ALU.mult),
                        e.drain(),
                        e.scalar_tensor_tensor(out=rows_t[0:1, 1024:1024 + fn],
                                               in0=ps4[0:1, 0:fn], scalar=1.0 / 1024.0,
                                               in1=rows_t[0:1, 512:512 + fn],
                                               op0=ALU.mult, op1=ALU.subtract),
                        e.drain(),
                        e.tensor_scalar(out=rows_t[0:1, 1024:1024 + fn],
                                        in0=rows_t[0:1, 1024:1024 + fn],
                                        scalar1=EPS_LN, scalar2=None, op0=ALU.add),
                        e.drain(),
                        e.reciprocal(out=rows_t[0:1, 1536:1536 + fn],
                                     in_=rows_t[0:1, 1024:1024 + fn]),
                    ]
                S.step("vector", _)

                def _(e, fn=fn):
                    return e.activation(out=srcrow[0:1, 0:fn], in_=rows_t[0:1, 1536:1536 + fn],
                                        func=AF.Sqrt)
                S.step("scalar", _)

                def _(e, fn=fn):
                    return e.matmul(out=ps4[:, 0:fn], lhsT=onesf[:], rhs=srcrow[:, 0:fn],
                                    start=True, stop=True)
                S.step("tensor", _)

                def _(e, fn=fn):
                    return e.tensor_tensor(out=srcrow[0:1, 0:fn], in0=srcrow[0:1, 0:fn],
                                           in1=rows_t[0:1, 0:fn], op=ALU.mult)
                S.step("vector", _)

                def _(e, fn=fn):
                    return e.matmul(out=ps3[:, 0:fn], lhsT=onesf[:], rhs=srcrow[:, 0:fn],
                                    start=True, stop=True)
                S.step("tensor", _)

                def _(e, fn=fn):
                    o3 = outst[:].rearrange("p (c t) -> p c t", c=8)
                    return [
                        e.tensor_tensor(out=o3, in0=o3,
                                        in1=ps4[:, None, 0:fn].broadcast_to([128, 8, fn]),
                                        op=ALU.mult),
                        e.drain(),
                        e.tensor_tensor(out=o3, in0=o3,
                                        in1=ps3[:, None, 0:fn].broadcast_to([128, 8, fn]),
                                        op=ALU.subtract),
                    ]
                tok_ln2_prev = S.step("vector", _)

                @dma_step(1)
                def _(e, ts0=ts0, fn=fn):
                    return e.dma_start(
                        out=outT_h.rearrange("(c p) t -> p c t", p=128)[:, :, ts0 - 1: ts0 - 1 + fn],
                        in_=outst[:].rearrange("p (c t) -> p c t", c=8))
                S.step("gpsimd", _, dma=True, after=[tok_ln2_prev])
                S.last = tok_ln2_prev

        PHASE_MARKS.append(("FFN", S.nsteps))

        final_g = S.cnt["gpsimd"]

        def _(e):
            e.wait_ge(sems["gpsimd"], final_g)
            return e.engine_nop()
        S.step("gpsimd", _)

        with nc.Block() as block:
            @block.sync
            def _(eng):
                for op in S.ops["sync"]:
                    op(eng)

            @block.vector
            def _(eng):
                for op in S.ops["vector"]:
                    op(eng)

            @block.scalar
            def _(eng):
                for op in S.ops["scalar"]:
                    op(eng)

            @block.tensor
            def _(eng):
                for op in S.ops["tensor"]:
                    op(eng)

            @block.gpsimd
            def _(eng):
                for op in S.ops["gpsimd"]:
                    op(eng)

    return nc


def _band_mask(off, qn):
    m = np.zeros((128, 1024), np.float32)
    for h in range(8):
        mm = np.zeros((128, 128), np.float32)
        i = np.arange(128)[:, None]
        j = np.arange(128)[None, :]
        band = (np.abs(i - j - off) <= 3) & (j < qn)
        mm[band] = 1.0
        m[:, h * 128:(h + 1) * 128] = mm
    return m


def kernel(L0_attn_x, L0_attn_w_qkv, L0_attn_proj, L0_attn_w_out, L0_attn_b_out,
           L0_ffn_w1, L0_ffn_conv_w, L0_ffn_conv_b, L0_ffn_w2):
    x = np.ascontiguousarray(np.asarray(L0_attn_x, dtype=np.float32))
    wqkv = np.ascontiguousarray(np.asarray(L0_attn_w_qkv, dtype=np.float32))
    proj = np.ascontiguousarray(np.asarray(L0_attn_proj, dtype=np.float32))
    wout = np.ascontiguousarray(np.asarray(L0_attn_w_out, dtype=np.float32))
    bout = np.ascontiguousarray(np.asarray(L0_attn_b_out, dtype=np.float32))[None, :]
    w1 = np.asarray(L0_ffn_w1, dtype=np.float32)
    convw = np.asarray(L0_ffn_conv_w, dtype=np.float32)[:, 0, :]
    convb = np.asarray(L0_ffn_conv_b, dtype=np.float32)
    w2 = np.asarray(L0_ffn_w2, dtype=np.float32)

    w1T = np.ascontiguousarray(w1.T)
    w2T = np.ascontiguousarray(w2.T)

    convdiag = np.zeros((128, 96 * 128), np.float32)
    idx = np.arange(128)
    for c in range(32):
        for tap in range(3):
            blk = (3 * c + tap) * 128
            convdiag[idx, blk + idx] = convw[c * 128:(c + 1) * 128, tap]
    convb_m = np.ascontiguousarray(convb.reshape(32, 128).T)

    ident = np.eye(128, dtype=np.float32)
    n_tiles = (NQ + QT - 1) // QT
    qs_last = QS0 + (n_tiles - 1) * QT
    qn_last = QS0 + NQ - qs_last
    off_last = qs_last - min(qs_last - 3, TB - 128)
    bandmask = _band_mask(3, QT)
    bandmask2 = _band_mask(off_last, qn_last)

    in_maps = []
    for core in range(NCORES):
        b, r = core // 4, core % 4
        t0 = r * TOK
        lo, hi = t0 - HB, t0 + TOK + HB
        xT = np.zeros((D, TB), np.float32)
        s, e = max(lo, 0), min(hi, N)
        xT[:, s - lo: s - lo + (e - s)] = x[b, s:e, :].T
        kmask = np.zeros((128, 16), np.float32)
        kmask[:, 0:8] = 1.0 if lo >= 0 else 0.0
        kmask[:, 8:16] = 1.0 if hi <= N else 0.0
        qmask = np.zeros((128, 2), np.float32)
        qmask[:, 0] = 1.0 if t0 - 1 >= 0 else 0.0
        qmask[:, 1] = 1.0 if t0 + TOK + 1 <= N else 0.0
        in_maps.append({
            "xT": xT, "wqkv": wqkv, "proj": proj, "wout": wout, "boutrow": bout,
            "w1T": w1T, "w2T": w2T, "convdiag": convdiag, "convbH": convb_m,
            "identH": ident, "bandmask": bandmask, "bandmask2": bandmask2,
            "kmaskH": kmask, "qmaskH": qmask,
        })

    nc = build_program()
    res = run_bass_kernel_spmd(nc, in_maps, core_ids=list(range(NCORES)))

    out = np.empty((B, N, D), np.float32)
    for core in range(NCORES):
        b, r = core // 4, core % 4
        t0 = r * TOK
        out[b, t0:t0 + TOK, :] = res.results[core]["outT"].T
    return out


# version 25
# speedup vs baseline: 1.1146x; 1.0013x over previous
import numpy as np
import contextlib
import os
import concourse.bass as bass
import concourse.mybir as mybir
from concourse.bass_utils import run_bass_kernel_spmd

F32 = mybir.dt.float32
BF16 = mybir.dt.bfloat16
I32 = mybir.dt.int32
AF = mybir.ActivationFunctionType
ALU = mybir.AluOpType
AX = mybir.AxisListType

B, N, D = 2, 8192, 1024
NCORES = 8
TOK = 2048
HB = 8
TB = TOK + 2 * HB
HEADS, HD, FEAT = 8, 128, 64
DH = 4096
QT = 122
QS0, NQ = 7, 2050
FT = 512
HBW = 520
EPS_LN, EPS_Z = 1e-5, 1e-8

ENG = ("sync", "vector", "scalar", "tensor", "gpsimd")
BUILD_LIMIT = int(os.environ.get("KBISECT", "0")) or None
PHASE_MARKS = []


class Sched:

    def __init__(self, nc, sems):
        self.nc = nc
        self.sems = sems
        self.ops = {e: [] for e in ENG}
        self.cnt = {e: 0 for e in ENG}
        self.last = None
        self.nsteps = 0
        self.auto_pump = None
        self._pumping = False

    def step(self, eng, builder, dma=False, after=None, extra_waits=(), sem_key=None):
        self.nsteps += 1
        if BUILD_LIMIT is not None and self.nsteps > BUILD_LIMIT:
            return self.last
        if after is None:
            deps = [self.last] if self.last is not None else []
        elif isinstance(after, list):
            deps = list(after)
        else:
            deps = [after]
        deps = [d for d in deps if d is not None]
        deps += list(extra_waits)
        per = {}
        for (s, v) in deps:
            if s == eng:
                continue
            per[s] = max(per.get(s, 0), v)
        waits = sorted(per.items())
        key = sem_key if (dma and sem_key is not None) else eng
        sem = self.sems[key]
        if dma:
            nd = getattr(builder, "n_dma")
            self.cnt[key] += 16 * nd
            tok = (key, self.cnt[key])

            def run(e, waits=tuple(waits), builder=builder, nd=nd):
                for (s, v) in waits:
                    e.wait_ge(self.sems[s], v)
                insts = builder(e)
                if not isinstance(insts, (list, tuple)):
                    insts = [insts]
                assert len(insts) == nd
                for i in insts:
                    i.then_inc(sem, 16)

            self.ops[eng].append(run)
        else:
            self.cnt[eng] += 1
            tok = (eng, self.cnt[eng])

            def run(e, waits=tuple(waits), builder=builder, eng=eng):
                for (s, v) in waits:
                    e.wait_ge(self.sems[s], v)
                insts = builder(e)
                if not isinstance(insts, (list, tuple)):
                    insts = [insts]
                if eng in ("vector", "scalar", "gpsimd"):
                    e.drain().then_inc(sem, 1)
                else:
                    insts[-1].then_inc(sem, 1)

            self.ops[eng].append(run)
        self.last = tok
        if self.auto_pump is not None and not self._pumping:
            self.auto_pump()
        return tok


def dma_step(n):
    def deco(fn):
        fn.n_dma = n
        return fn
    return deco


def build_program():
    nc = bass.Bass("TRN2", target_bir_lowering=False)

    xT_h = nc.dram_tensor("xT", [D, TB], F32, kind="ExternalInput").ap()
    wqkv_h = nc.dram_tensor("wqkv", [D, 3 * D], F32, kind="ExternalInput").ap()
    proj_h = nc.dram_tensor("proj", [HD, FEAT], F32, kind="ExternalInput").ap()
    wout_h = nc.dram_tensor("wout", [D, D], F32, kind="ExternalInput").ap()
    bout_h = nc.dram_tensor("boutrow", [1, D], F32, kind="ExternalInput").ap()
    w1T_h = nc.dram_tensor("w1T", [D, DH], F32, kind="ExternalInput").ap()
    w2T_h = nc.dram_tensor("w2T", [DH, D], F32, kind="ExternalInput").ap()
    convd_h = nc.dram_tensor("convdiag", [128, 96 * 128], F32, kind="ExternalInput").ap()
    convb_h = nc.dram_tensor("convbH", [128, 32], F32, kind="ExternalInput").ap()
    ident_h = nc.dram_tensor("identH", [128, 128], F32, kind="ExternalInput").ap()
    bmask_h = nc.dram_tensor("bandmask", [128, 1024], F32, kind="ExternalInput").ap()
    bmask2_h = nc.dram_tensor("bandmask2", [128, 1024], F32, kind="ExternalInput").ap()
    kmask_h = nc.dram_tensor("kmaskH", [128, 16], F32, kind="ExternalInput").ap()
    qmask_h = nc.dram_tensor("qmaskH", [128, 2], F32, kind="ExternalInput").ap()
    outT_h = nc.dram_tensor("outT", [D, TOK], F32, kind="ExternalOutput").ap()
    _skind = "ExternalOutput" if os.environ.get("KDEBUG") else "Internal"
    xln1_s = nc.dram_tensor("xln1_s", [D, NQ], F32, kind=_skind).ap()
    xq_s = nc.dram_tensor("xq_s", [D, NQ], BF16, kind=_skind).ap()
    w1q_s = nc.dram_tensor("w1q_s", [D, DH], BF16, kind=_skind).ap()
    w2q_s = nc.dram_tensor("w2q_s", [DH, D], BF16, kind=_skind).ap()

    ctx = contextlib.ExitStack()
    with ctx:
        def sb(name, shape, dt):
            return ctx.enter_context(nc.sbuf_tensor(name, shape, dt))

        def psum(name, shape):
            return ctx.enter_context(nc.psum_tensor(name, shape, F32))

        ident = sb("ident", [128, 128], F32)
        onesb = sb("onesb", [128, 1024], BF16)
        onesf = sb("onesf", [128, 128], F32)
        eps128 = sb("eps128", [128, 128], BF16)
        c32 = sb("c32", [32, 520], BF16)
        pp = sb("pp", [128, 192], BF16)
        boutpad = sb("boutpad", [32, 1024], BF16)
        minus1 = sb("minus1", [128, 1], F32)
        c1024 = sb("c1024", [128, 1], F32)
        rows_t = sb("rows_t", [1, 2048], F32)
        srcrow = sb("srcrow", [128, 512], F32)
        cols_t = sb("cols_t", [128, 16], F32)
        mc1col = sb("mc1col", [128, 1], F32)
        mc2col = sb("mc2col", [128, 1], F32)
        ws1col = sb("ws1col", [128, 1], F32)
        ws2col = sb("ws2col", [128, 1], F32)
        wtot = sb("wtot", [1, 8], F32)
        kmask = sb("kmask", [128, 16], F32)
        qmask = sb("qmask", [128, 2], F32)

        pA = psum("pA", [128, 512])
        pB = psum("pB", [128, 512])
        pC = psum("pC", [128, 2048])
        ps3 = psum("ps3", [128, 512])
        ps4 = psum("ps4", [128, 512])

        sems = {}
        for e in ENG + ("wq0", "wq1", "wq2", "wq3", "fx", "fw0", "fw1", "fw2", "gw0", "gw1", "gw2", "ax", "sa0", "sa1"):
            sems[e] = ctx.enter_context(nc.semaphore("sem_" + e))
        S = Sched(nc, sems)
        for e in ("wq0", "wq1", "wq2", "wq3", "fx", "fw0", "fw1", "fw2", "gw0", "gw1", "gw2", "ax", "sa0", "sa1"):
            S.cnt[e] = 0

        @dma_step(4)
        def _(e):
            return [
                e.dma_start(out=ident[:], in_=ident_h[:]),
                e.dma_start(out=kmask[:], in_=kmask_h[:]),
                e.dma_start(out=qmask[:], in_=qmask_h[:]),
                e.dma_start(out=srcrow[:, 0:64], in_=proj_h[:]),
            ]
        S.step("sync", _, dma=True)

        def _(e):
            return [
                e.memset(onesb[:], 1.0),
                e.memset(onesf[:], 1.0),
                e.memset(eps128[:], EPS_Z / 128.0),
                e.memset(c32[:], 1.0 / 32.0),
                e.memset(pp[:], 0.0),
                e.memset(boutpad[:], 0.0),
                e.memset(minus1[:], -1.0),
                e.memset(c1024[:], 1.0 / 1024.0),
                e.memset(rows_t[:], 0.0),
                e.memset(cols_t[:], 0.0),
                e.drain(),
                e.tensor_copy(out=pp[:, 0:64], in_=srcrow[:, 0:64]),
                e.tensor_copy(out=pp[:, 128:192], in_=srcrow[:, 0:64]),
                e.drain(),
                e.memset(srcrow[:], 0.0),
            ]
        S.step("vector", _)

        PHASE_MARKS.append(("P0", S.nsteps))

        wq_done = [None, None]
        wq_mean = [None, None]

        def wquant_gen(wT_h, wq_s, n_iters, width, acc_base, mccol, wscol, slot,
                       rows128, colw, gidx):
            stg = [stageW, stageW2, stageW3, stageW4]
            tok_dma = [None] * n_iters
            tok_abs = [None] * n_iters

            def src_ap(it):
                rb, cb = it // colw, it % colw
                return wT_h[rb * 128:(rb + 1) * 128, cb * width:(cb + 1) * width]

            def dst_ap(it):
                rb, cb = it // colw, it % colw
                return wq_s[rb * 128:(rb + 1) * 128, cb * width:(cb + 1) * width]

            for it in range(n_iters):
                @dma_step(1)
                def _(e, it=it):
                    return e.dma_start(out=stg[it % 4][:, 0:width], in_=src_ap(it))
                if it >= 4:
                    dep = [tok_abs[it - 4]]
                elif gidx == 1:
                    dep = [wq_done[0]]
                else:
                    dep = [S.last]
                tok_dma[it] = S.step("sync", _, dma=True, after=dep,
                                     sem_key="wq%d" % (it % 4))

                def _(e, it=it):
                    return e.activation(out=wtmp[:, 0:width], in_=stg[it % 4][:, 0:width],
                                        func=AF.Abs, accum_out=wacc[:, acc_base + it: acc_base + it + 1])
                yield
                tok_abs[it] = S.step("scalar", _, after=[tok_dma[it]])
                yield

            def _(e):
                return e.reduce_sum(out=cols_t[:, 8:9], in_=wacc[:, acc_base:acc_base + n_iters], axis=AX.X)
            S.step("vector", _, after=[tok_abs[n_iters - 1]])

            def _(e):
                return e.transpose(out=ps3[0:1, 0:128], in_=cols_t[:, 8:9], identity=ident[:])
            S.step("tensor", _, after=[S.last])

            def _(e):
                return [
                    e.reduce_sum(out=wtot[0:1, slot:slot + 1], in_=ps3[0:1, 0:128], axis=AX.X),
                    e.drain(),
                    e.tensor_scalar(out=wtot[0:1, slot + 1:slot + 2], in0=wtot[0:1, slot:slot + 1],
                                    scalar1=1.0 / (n_iters * 128.0 * width), scalar2=None,
                                    op0=ALU.mult),
                    e.drain(),
                    e.tensor_scalar(out=wtot[0:1, slot + 1:slot + 2], in0=wtot[0:1, slot + 1:slot + 2],
                                    scalar1=1e-5, scalar2=None, op0=ALU.max),
                    e.drain(),
                    e.reciprocal(out=wtot[0:1, slot + 2:slot + 3], in_=wtot[0:1, slot + 1:slot + 2]),
                    e.drain(),
                    e.tensor_copy(out=srcrow[0:1, 0:1], in_=wtot[0:1, slot + 1:slot + 2]),
                    e.tensor_copy(out=srcrow[0:1, 1:2], in_=wtot[0:1, slot + 2:slot + 3]),
                ]
            S.step("vector", _, after=[S.last])

            def _(e):
                return e.matmul(out=ps4[:, 0:2], lhsT=onesf[:], rhs=srcrow[:, 0:2],
                                start=True, stop=True)
            S.step("tensor", _, after=[S.last])

            def _(e):
                return [
                    e.tensor_copy(out=mccol[:], in_=ps4[:, 0:1]),
                    e.tensor_copy(out=wscol[:], in_=ps4[:, 1:2]),
                ]
            tok_sc = S.step("vector", _, after=[S.last])
            wq_mean[gidx] = tok_sc
            yield

            tok_v = [None] * n_iters
            tok_s = [None] * n_iters
            tok_g = [None] * n_iters
            for it in range(n_iters):
                @dma_step(1)
                def _(e, it=it):
                    return e.dma_start(out=stg[it % 4][:, 0:width], in_=src_ap(it))
                tok_dma[it] = S.step("sync", _, dma=True,
                                     after=[tok_v[it - 4]] if it >= 4 else [tok_sc],
                                     sem_key="wq%d" % (it % 4))
                yield

                def _(e, it=it):
                    return e.activation(out=wtmp[:, 0:width], in_=stg[it % 4][:, 0:width],
                                        func=AF.Identity, scale=wscol[:])
                tok_sa = S.step("scalar", _,
                                after=[tok_dma[it], tok_sc] + ([tok_v[it - 1]] if it >= 1 else []))

                def _(e, it=it):
                    return e.tensor_scalar(out=w32[:, 0:width], in0=wtmp[:, 0:width],
                                           scalar1=1.0, scalar2=-1.0,
                                           op0=ALU.min, op1=ALU.max)
                tok_v[it] = S.step("vector", _,
                                   after=[tok_sa] + ([tok_s[it - 1]] if it >= 1 else []))

                def _(e, it=it):
                    return e.copy(out=wqout[:, 0:width], in_=w32[:, 0:width])
                tok_s[it] = S.step("scalar", _,
                                   after=[tok_v[it]] + ([tok_g[it - 1]] if it >= 1 else []))

                @dma_step(1)
                def _(e, it=it):
                    return e.dma_start(out=dst_ap(it), in_=wqout[:, 0:width])
                tok_g[it] = S.step("gpsimd", _, dma=True, after=[tok_s[it]])
                yield
            wq_done[gidx] = tok_g[n_iters - 1]

        wq_gens = [None, None]

        def pump_wq(n):
            saved = S.last
            S._pumping = True
            for _ in range(n):
                for gidx in range(2):
                    if wq_gens[gidx] is not None:
                        try:
                            next(wq_gens[gidx])
                            break
                        except StopIteration:
                            wq_gens[gidx] = None
            S._pumping = False
            S.last = saved

        def drain_wq():
            saved = S.last
            S._pumping = True
            for gidx in range(2):
                if wq_gens[gidx] is not None:
                    for _ in wq_gens[gidx]:
                        pass
                    wq_gens[gidx] = None
            S._pumping = False
            S.last = saved

            def _(e):
                return e.engine_nop()
            S.step("vector", _, after=[saved] + [t for t in wq_done if t is not None])

        PHASE_MARKS.append(("P1-wquant", S.nsteps))

        with contextlib.ExitStack() as actx:
            x_bf = actx.enter_context(nc.sbuf_tensor("x_bf", [128, 8 * TB], BF16))
            qphi = actx.enter_context(nc.sbuf_tensor("qphi", [128, 4 * TB], BF16))
            kphi = actx.enter_context(nc.sbuf_tensor("kphi", [128, 4 * TB], BF16))
            stage = actx.enter_context(nc.sbuf_tensor("stageA", [128, 1024], F32))
            stage2 = actx.enter_context(nc.sbuf_tensor("stageA2", [128, 1024], F32))
            wacc = actx.enter_context(nc.sbuf_tensor("wacc", [128, 80], F32))
            stageW = actx.enter_context(nc.sbuf_tensor("stageW", [128, 1024], F32))
            stageW2 = actx.enter_context(nc.sbuf_tensor("stageW2", [128, 1024], F32))
            stageW3 = actx.enter_context(nc.sbuf_tensor("stageW3", [128, 1024], F32))
            stageW4 = actx.enter_context(nc.sbuf_tensor("stageW4", [128, 1024], F32))
            wtmp = actx.enter_context(nc.sbuf_tensor("wtmp", [128, 1024], F32))
            w32 = actx.enter_context(nc.sbuf_tensor("w32", [128, 1024], I32))
            wqout = actx.enter_context(nc.sbuf_tensor("wqout", [128, 1024], BF16))
            wq_gens[0] = wquant_gen(w1T_h, w1q_s, 32, 1024, 0, mc1col, ws1col, 0, 8, 4, 0)
            wq_gens[1] = wquant_gen(w2T_h, w2q_s, 32, 1024, 32, mc2col, ws2col, 4, 32, 1, 1)
            bmask = actx.enter_context(nc.sbuf_tensor("bmask", [128, 1024], F32))
            bmask2 = actx.enter_context(nc.sbuf_tensor("bmask2", [128, 1024], F32))

            @dma_step(3)
            def _(e):
                return [e.dma_start(out=bmask[:], in_=bmask_h[:]),
                        e.dma_start(out=bmask2[:], in_=bmask2_h[:]),
                        e.dma_start(out=stage[0:1, 0:1024], in_=bout_h[:])]
            S.step("sync", _, dma=True)

            def _(e):
                return e.tensor_copy(out=boutpad[0:1, :], in_=stage[0:1, 0:1024])
            S.step("vector", _)

            with contextlib.ExitStack() as a1ctx:
                wqk = a1ctx.enter_context(nc.sbuf_tensor("wqk", [128, 8 * 2048], BF16))
                xres = a1ctx.enter_context(nc.sbuf_tensor("xres", [128, 8 * 528], F32))
                qtmp = a1ctx.enter_context(nc.sbuf_tensor("qtmp", [128, 8 * 512], BF16))
                stgA = [stage, stage2]
                tok_wd = [None] * 16
                tok_wc = [None] * 16
                for half in range(16):
                    @dma_step(1)
                    def _(e, half=half):
                        dc, hh = half // 2, half % 2
                        return e.dma_start(out=stgA[half % 2][:, 0:1024],
                                           in_=wqkv_h[dc * 128:(dc + 1) * 128, hh * 1024:(hh + 1) * 1024])
                    tok_wd[half] = S.step("sync", _, dma=True, sem_key="sa%d" % (half % 2),
                                          after=[tok_wc[half - 2]] if half >= 2 else None)

                    def _(e, half=half):
                        return e.copy(out=wqk[:, half * 1024:(half + 1) * 1024],
                                      in_=stgA[half % 2][:, 0:1024])
                    tok_wc[half] = S.step("scalar", _, after=[tok_wd[half]])

                S.auto_pump = lambda: pump_wq(3)
                tiles = [(t, min(512, TB - t)) for t in range(0, TB, 512)]
                t_castA1 = None
                t_sttPrev = None
                for (ts, tl) in tiles:
                    @dma_step(1)
                    def _(e, ts=ts, tl=tl):
                        return e.dma_start(
                            out=xres[:].rearrange("p (c t) -> p c t", c=8)[:, :, 0:tl],
                            in_=xT_h.rearrange("(c p) t -> p c t", p=128)[:, :, ts:ts + tl])
                    S.step("sync", _, dma=True, sem_key="ax",
                           after=[t_castA1] if t_castA1 else None)

                    def _(e, ts=ts, tl=tl):
                        return e.tensor_copy(
                            out=x_bf[:].rearrange("p (c t) -> p c t", c=8)[:, :, ts:ts + tl],
                            in_=xres[:].rearrange("p (c t) -> p c t", c=8)[:, :, 0:tl])
                    t_castA1 = S.step("vector", _)

                    for (wbase, phidst) in ((0, qphi), (1024, kphi)):
                        for hb in range(2):
                            def _(e, hb=hb, wbase=wbase, ts=ts, tl=tl):
                                mms = []
                                for hl in range(4):
                                    h = hb * 4 + hl
                                    for dc in range(8):
                                        mms.append(e.matmul(
                                            out=pC[:, hl * 512: hl * 512 + tl],
                                            lhsT=wqk[:, dc * 2048 + wbase + h * 128: dc * 2048 + wbase + (h + 1) * 128],
                                            rhs=x_bf[:, dc * TB + ts: dc * TB + ts + tl],
                                            start=(dc == 0), stop=(dc == 7)))
                                return mms
                            if wbase == 0 and hb == 0 and t_sttPrev is not None:
                                S.step("tensor", _, after=[S.last, t_sttPrev])
                            else:
                                S.step("tensor", _)

                            def _(e, hb=hb, tl=tl):
                                return [e.copy(out=qtmp[:, (hb * 4 + hl) * 512: (hb * 4 + hl) * 512 + tl],
                                               in_=pC[:, hl * 512: hl * 512 + tl])
                                        for hl in range(4)]
                            S.step("scalar", _)

                        def _(e, ts=ts, tl=tl):
                            mms = []
                            for hp in range(4):
                                mms.append(e.matmul(out=pC[:, hp * 512: hp * 512 + tl], lhsT=pp[:, 0:128],
                                                    rhs=qtmp[:, hp * 512: hp * 512 + tl],
                                                    start=True, stop=False))
                                mms.append(e.matmul(out=pC[:, hp * 512: hp * 512 + tl], lhsT=pp[:, 64:192],
                                                    rhs=qtmp[:, (hp + 4) * 512: (hp + 4) * 512 + tl],
                                                    start=False, stop=False))
                                mms.append(e.matmul(out=pC[:, hp * 512: hp * 512 + tl],
                                                    lhsT=onesb[0:32, 0:128],
                                                    rhs=c32[0:32, 0:tl], start=False, stop=True))
                            return mms
                        S.step("tensor", _)

                        def _(e, tl=tl):
                            return [e.activation(out=stgA[hp // 2][:, (hp % 2) * 512: (hp % 2) * 512 + tl],
                                                 in_=pC[:, hp * 512: hp * 512 + tl],
                                                 func=AF.Exp, bias=minus1[:], scale=1.0)
                                    for hp in range(4)]
                        S.step("scalar", _)

                        def _(e, ts=ts, tl=tl, phidst=phidst):
                            return [e.scalar_tensor_tensor(
                                out=phidst[:, hp * TB + ts: hp * TB + ts + tl],
                                in0=stgA[hp // 2][:, (hp % 2) * 512: (hp % 2) * 512 + tl], scalar=1.0,
                                in1=pC[:, hp * 512: hp * 512 + tl],
                                op0=ALU.min, op1=ALU.max) for hp in range(4)]
                        t_sttPrev = S.step("vector", _)

                def _(e):
                    km3 = kphi[:].rearrange("p (c t) -> p c t", c=4)
                    return [
                        e.tensor_tensor(out=km3[:, :, 0:HB], in0=km3[:, :, 0:HB],
                                        in1=kmask[:, None, 0:8].broadcast_to([128, 4, 8]),
                                        op=ALU.mult),
                        e.tensor_tensor(out=km3[:, :, TB - HB:TB], in0=km3[:, :, TB - HB:TB],
                                        in1=kmask[:, None, 8:16].broadcast_to([128, 4, 8]),
                                        op=ALU.mult),
                    ]
                S.step("vector", _)
                while wq_mean[1] is None and wq_gens[1] is not None:
                    pump_wq(1)
                S.auto_pump = lambda: pump_wq(1)

                def _(e):
                    return e.engine_nop()
                S.step("vector", _,
                       after=[S.last] + [t for t in wq_mean if t is not None])

            PHASE_MARKS.append(("A1", S.nsteps))

            with contextlib.ExitStack() as a2ctx:
                def a2sb(name, shape, dt):
                    return a2ctx.enter_context(nc.sbuf_tensor(name, shape, dt))
                wv = a2sb("wv", [128, 8 * 1024], BF16)
                woutb = a2sb("woutb", [128, 8 * 1024], BF16)
                v_sb = a2sb("v_sb", [128, 1024], BF16)
                Sm = a2sb("Sm", [128, 1024], BF16)
                zinv = a2sb("zinv", [128, 1024], F32)
                attn_cp = a2sb("attn_cp", [128, 1024], F32)
                attn_sb = a2sb("attn_sb", [128, 1024], BF16)
                vres = a2sb("vres", [128, 976], F32)
                vresB = a2sb("vresB", [128, 976], F32)
                xresA2 = a2sb("xresA2", [128, 976], F32)
                vbf = a2sb("vbf", [128, 976], BF16)
                v2bf = a2sb("v2bf", [128, 976], BF16)
                xln1t = a2sb("xln1t", [128, 976], F32)
                tq = a2sb("tq", [128, 976], F32)
                tq32 = a2sb("tq32", [128, 976], I32)
                xqt = a2sb("xqt", [128, 976], BF16)
                Mx = a2sb("Mx", [128, 976], F32)

                stgA = [stage, stage2]
                tok_vd = [None] * 16
                tok_vc = [None] * 16
                for i, (dst, src_h, cbase, dc) in enumerate(
                        [(wv, wqkv_h, 2048, dc) for dc in range(8)] +
                        [(woutb, wout_h, 0, dc) for dc in range(8)]):
                    @dma_step(1)
                    def _(e, src_h=src_h, cbase=cbase, dc=dc, i=i):
                        return e.dma_start(out=stgA[i % 2][:, 0:1024],
                                           in_=src_h[dc * 128:(dc + 1) * 128, cbase:cbase + 1024])
                    tok_vd[i] = S.step("sync", _, dma=True, sem_key="sa%d" % (i % 2),
                                       after=[tok_vc[i - 2]] if i >= 2 else None)

                    def _(e, dst=dst, dc=dc, i=i):
                        return e.copy(out=dst[:, dc * 1024:(dc + 1) * 1024],
                                      in_=stgA[i % 2][:, 0:1024])
                    tok_vc[i] = S.step("scalar", _, after=[tok_vd[i]])

                n_tiles = (NQ + QT - 1) // QT
                t0_tok = S.last
                TK = {k: [None] * n_tiles for k in (
                    "xdma", "vmm", "vevac", "smm", "mask", "zmm", "recip", "amm",
                    "acp", "zmul", "oproj", "vres", "cast", "stats", "chain",
                    "sqrt", "mr", "bc1", "apply", "spill", "mxtr", "cols", "rowtr",
                    "rowcp", "bc2", "quant", "qsp")}
                vres2 = [vres, vresB]

                def front(it):
                    qs = QS0 + it * QT
                    qn = min(QT, QS0 + NQ - qs)
                    hs = min(qs - 3, TB - 128)
                    mask = bmask if it < n_tiles - 1 else bmask2
                    vr = vres2[it % 2]

                    @dma_step(1)
                    def _(e, qs=qs, qn=qn):
                        return e.dma_start(
                            out=xresA2[:].rearrange("p (c t) -> p c t", c=8)[:, :, 0:qn],
                            in_=xT_h.rearrange("(c p) t -> p c t", p=128)[:, :, qs:qs + qn])
                    TK["xdma"][it] = S.step("sync", _, dma=True, sem_key="ax",
                                            after=[TK["vres"][it - 1]] if it else [t0_tok])

                    def _(e, hs=hs):
                        mms = []
                        for dc in range(8):
                            for hb in range(2):
                                mms.append(e.matmul(
                                    out=pC[:, hb * 512:(hb + 1) * 512],
                                    lhsT=x_bf[:, dc * TB + hs: dc * TB + hs + 128],
                                    rhs=wv[:, dc * 1024 + hb * 512: dc * 1024 + (hb + 1) * 512],
                                    start=(dc == 0), stop=(dc == 7)))
                        return mms
                    TK["vmm"][it] = S.step("tensor", _,
                                           after=[TK["vres"][it - 1]] if it else [t0_tok])

                    def _(e):
                        return e.tensor_copy(out=v_sb[:], in_=pC[:, 0:1024])
                    TK["vevac"][it] = S.step("vector", _, after=[TK["vmm"][it]])

                    def _(e, hs=hs, qs=qs, qn=qn):
                        mms = []
                        for h in range(HEADS):
                            par = (h // 4) * 64
                            blk = (h % 4) * TB
                            mms.append(e.matmul(
                                out=pC[:, 1024 + h * 128: 1024 + h * 128 + qn],
                                lhsT=kphi[par:par + 64, blk + hs: blk + hs + 128],
                                rhs=qphi[par:par + 64, blk + qs: blk + qs + qn],
                                start=(h % 4 == 0), stop=(h % 4 == 3),
                                tile_position=(par, 0)))
                        return mms
                    TK["smm"][it] = S.step("tensor", _,
                                           after=[TK["recip"][it - 1]] if it else [t0_tok])

                    def _(e, mask=mask):
                        return e.tensor_tensor(out=Sm[:], in0=pC[:, 1024:2048],
                                               in1=mask[:], op=ALU.mult)
                    TK["mask"][it] = S.step("vector", _, after=[TK["smm"][it]])

                    def _(e, qn=qn):
                        mms = []
                        for hb in range(2):
                            mms.append(e.matmul(
                                out=pC[:, 1024 + hb * 512: 1536 + hb * 512],
                                lhsT=onesb[:, 0:128], rhs=Sm[:, hb * 512:(hb + 1) * 512],
                                start=True, stop=False))
                            for hl in range(4):
                                mms.append(e.matmul(
                                    out=pC[:, 1024 + hb * 512 + hl * 128:
                                            1024 + hb * 512 + hl * 128 + qn],
                                    lhsT=onesb[:, 0:128], rhs=eps128[:, 0:qn],
                                    start=False, stop=(hl == 3)))
                        return mms
                    TK["zmm"][it] = S.step("tensor", _, after=[TK["mask"][it]])

                    def _(e):
                        return e.reciprocal(out=zinv[:], in_=pC[:, 1024:2048])
                    TK["recip"][it] = S.step("vector", _, after=[TK["zmm"][it]])

                    def _(e, qn=qn):
                        mms = []
                        for h in range(HEADS):
                            dst = pA if h < 4 else pB
                            off = (h % 4) * 128
                            mms.append(e.matmul(
                                out=dst[:, off: off + qn],
                                lhsT=v_sb[:, h * 128:(h + 1) * 128],
                                rhs=Sm[:, h * 128: h * 128 + qn],
                                start=(h % 4 == 0), stop=(h % 4 == 3)))
                        return mms
                    TK["amm"][it] = S.step("tensor", _,
                                           after=[TK["mask"][it], TK["vevac"][it]] +
                                           ([TK["acp"][it - 1]] if it else []))

                    def _(e):
                        return [e.copy(out=attn_cp[:, 0:512], in_=pA[:]),
                                e.copy(out=attn_cp[:, 512:1024], in_=pB[:])]
                    TK["acp"][it] = S.step("scalar", _, after=[TK["amm"][it]])

                    def _(e):
                        return e.tensor_tensor(out=attn_sb[:], in0=attn_cp[:],
                                               in1=zinv[:], op=ALU.mult)
                    TK["zmul"][it] = S.step("vector", _, after=[TK["acp"][it]])

                    def _(e, qn=qn):
                        mms = []
                        for h in range(HEADS):
                            for oc in range(8):
                                mms.append(e.matmul(
                                    out=pC[:, oc * 128: oc * 128 + qn],
                                    lhsT=woutb[:, h * 1024 + oc * 128: h * 1024 + (oc + 1) * 128],
                                    rhs=attn_sb[:, h * 128: h * 128 + qn],
                                    start=(h == 0 and oc % 4 == 0), stop=False))
                        for oc in range(8):
                            mms.append(e.matmul(
                                out=pC[:, oc * 128: oc * 128 + qn],
                                lhsT=boutpad[0:32, oc * 128:(oc + 1) * 128],
                                rhs=onesb[0:32, 0:qn],
                                start=False, stop=(oc % 4 == 3)))
                        return mms
                    TK["oproj"][it] = S.step("tensor", _,
                                             after=[TK["zmul"][it], TK["vevac"][it]])

                    def _(e, qn=qn, vr=vr):
                        return e.tensor_tensor(
                            out=vr[:].rearrange("p (c t) -> p c t", c=8)[:, :, 0:qn],
                            in0=pC[:, 0:1024].rearrange("p (c t) -> p c t", t=128)[:, :, 0:qn],
                            in1=xresA2[:].rearrange("p (c t) -> p c t", c=8)[:, :, 0:qn],
                            op=ALU.add)
                    TK["vres"][it] = S.step("vector", _,
                                            after=[TK["oproj"][it], TK["xdma"][it]] +
                                            ([TK["apply"][it - 2]] if it >= 2 else []))

                def tail(it):
                    qs = QS0 + it * QT
                    qn = min(QT, QS0 + NQ - qs)
                    sc = qs - QS0
                    vr = vres2[it % 2]

                    def _(e, qn=qn, vr=vr):
                        return [
                            e.copy(out=vbf[:], in_=vr[:]),
                            e.activation(out=v2bf[:], in_=vr[:], func=AF.Square),
                        ]
                    TK["cast"][it] = S.step("scalar", _,
                                            after=[TK["vres"][it]] +
                                            ([TK["stats"][it - 1]] if it else []))

                    def _(e, qn=qn):
                        mms = []
                        for dc in range(8):
                            mms.append(e.matmul(out=ps3[0:1, 0:qn], lhsT=onesb[:, 0:1],
                                                rhs=vbf[:, dc * QT: dc * QT + qn],
                                                start=(dc == 0), stop=False))
                        for dc in range(8):
                            mms.append(e.matmul(out=ps3[0:1, 128:128 + qn], lhsT=onesb[:, 0:1],
                                                rhs=v2bf[:, dc * QT: dc * QT + qn],
                                                start=False, stop=(dc == 7)))
                        return mms
                    TK["stats"][it] = S.step("tensor", _,
                                             after=[TK["cast"][it]] +
                                             ([TK["rowcp"][it - 1]] if it else []))

                    def _(e, qn=qn):
                        return [
                            e.tensor_scalar(out=rows_t[0:1, 0:qn], in0=ps3[0:1, 0:qn],
                                            scalar1=1.0 / 1024.0, scalar2=None, op0=ALU.mult),
                            e.drain(),
                            e.tensor_tensor(out=rows_t[0:1, 512:512 + qn], in0=rows_t[0:1, 0:qn],
                                            in1=rows_t[0:1, 0:qn], op=ALU.mult),
                            e.drain(),
                            e.scalar_tensor_tensor(out=rows_t[0:1, 1024:1024 + qn],
                                                   in0=ps3[0:1, 128:128 + qn], scalar=1.0 / 1024.0,
                                                   in1=rows_t[0:1, 512:512 + qn],
                                                   op0=ALU.mult, op1=ALU.subtract),
                            e.drain(),
                            e.tensor_scalar(out=rows_t[0:1, 1024:1024 + qn],
                                            in0=rows_t[0:1, 1024:1024 + qn],
                                            scalar1=EPS_LN, scalar2=None, op0=ALU.add),
                            e.drain(),
                            e.reciprocal(out=rows_t[0:1, 1536:1536 + qn],
                                         in_=rows_t[0:1, 1024:1024 + qn]),
                        ]
                    TK["chain"][it] = S.step("vector", _,
                                             after=[TK["stats"][it]] +
                                             ([TK["mr"][it - 1]] if it else []))

                    def _(e, qn=qn):
                        return e.activation(out=srcrow[0:1, 0:qn], in_=rows_t[0:1, 1536:1536 + qn],
                                            func=AF.Sqrt)
                    TK["sqrt"][it] = S.step("scalar", _,
                                            after=[TK["chain"][it]] +
                                            ([TK["bc1"][it - 1]] if it else []))

                    def _(e, qn=qn):
                        return e.tensor_tensor(out=srcrow[0:1, 128:128 + qn],
                                               in0=srcrow[0:1, 0:qn], in1=rows_t[0:1, 0:qn],
                                               op=ALU.mult)
                    TK["mr"][it] = S.step("vector", _, after=[TK["sqrt"][it]])

                    def _(e, qn=qn):
                        return [
                            e.matmul(out=ps4[:, 0:qn], lhsT=onesf[:], rhs=srcrow[:, 0:qn],
                                     start=True, stop=False),
                            e.matmul(out=ps4[:, 128:128 + qn], lhsT=onesf[:], rhs=srcrow[:, 128:128 + qn],
                                     start=False, stop=True),
                        ]
                    TK["bc1"][it] = S.step("tensor", _,
                                           after=[TK["mr"][it]] +
                                           ([TK["quant"][it - 1]] if it else []))

                    def _(e, qn=qn, vr=vr):
                        v3 = vr[:].rearrange("p (c t) -> p c t", c=8)[:, :, 0:qn]
                        x3 = xln1t[:].rearrange("p (c t) -> p c t", c=8)[:, :, 0:qn]
                        xin = xln1t[:].rearrange("p (c t) -> p t c", c=8)[:, 0:qn, :]
                        return [
                            e.tensor_tensor(out=x3, in0=v3,
                                            in1=ps4[:, None, 0:qn].broadcast_to([128, 8, qn]),
                                            op=ALU.mult),
                            e.drain(),
                            e.tensor_tensor(out=x3, in0=x3,
                                            in1=ps4[:, None, 128:128 + qn].broadcast_to([128, 8, qn]),
                                            op=ALU.subtract),
                            e.drain(),
                            e.tensor_reduce(
                                out=Mx[:, 0:qn].rearrange("p (t o) -> p t o", o=1),
                                in_=xin, axis=AX.X, op=ALU.max, apply_absolute_value=True),
                        ]
                    TK["apply"][it] = S.step("vector", _,
                                             after=[TK["bc1"][it]] +
                                             ([TK["spill"][it - 1]] if it else []))

                    @dma_step(8)
                    def _(e, sc=sc, qn=qn):
                        return [e.dma_start(out=xln1_s[dc * 128:(dc + 1) * 128, sc:sc + qn],
                                            in_=xln1t[:, dc * QT: dc * QT + qn])
                                for dc in range(8)]
                    TK["spill"][it] = S.step("gpsimd", _, dma=True, after=[TK["apply"][it]])

                    def _(e, qn=qn):
                        return e.transpose(out=ps4[0:qn, 384:512], in_=Mx[:, 0:qn], identity=ident[:])
                    TK["mxtr"][it] = S.step("tensor", _, after=[TK["apply"][it]])

                    def _(e, qn=qn):
                        return [
                            e.reduce_max(out=cols_t[0:qn, 0:1], in_=ps4[0:qn, 384:512], axis=AX.X),
                            e.drain(),
                            e.tensor_scalar(out=cols_t[0:qn, 1:2], in0=cols_t[0:qn, 0:1],
                                            scalar1=1e-5, scalar2=None, op0=ALU.max),
                            e.drain(),
                            e.tensor_scalar(out=cols_t[0:qn, 1:2], in0=cols_t[0:qn, 1:2],
                                            scalar1=1.0 / 127.0, scalar2=None, op0=ALU.mult),
                            e.drain(),
                            e.reciprocal(out=cols_t[0:qn, 2:3], in_=cols_t[0:qn, 1:2]),
                            e.drain(),
                            e.tensor_tensor(out=cols_t[0:qn, 3:4], in0=cols_t[0:qn, 1:2],
                                            in1=mc1col[0:qn, :], op=ALU.mult),
                        ]
                    TK["cols"][it] = S.step("vector", _, after=[TK["mxtr"][it]])

                    def _(e, qn=qn):
                        return [
                            e.transpose(out=ps3[0:1, 256:256 + qn], in_=cols_t[0:qn, 2:3],
                                        identity=ident[0:qn, 0:qn]),
                            e.transpose(out=ps3[0:1, 384:384 + qn], in_=cols_t[0:qn, 3:4],
                                        identity=ident[0:qn, 0:qn]),
                        ]
                    TK["rowtr"][it] = S.step("tensor", _, after=[TK["cols"][it]])

                    def _(e, qn=qn):
                        return [
                            e.tensor_copy(out=srcrow[0:1, 256:256 + qn], in_=ps3[0:1, 256:256 + qn]),
                            e.tensor_copy(out=srcrow[0:1, 384:384 + qn], in_=ps3[0:1, 384:384 + qn]),
                        ]
                    TK["rowcp"][it] = S.step("vector", _, after=[TK["rowtr"][it]])

                    def _(e, qn=qn):
                        return [
                            e.matmul(out=ps4[:, 256:256 + qn], lhsT=onesf[:],
                                     rhs=srcrow[:, 256:256 + qn], start=True, stop=False),
                            e.matmul(out=ps4[:, 384:384 + qn], lhsT=onesf[:],
                                     rhs=srcrow[:, 384:384 + qn], start=False, stop=True),
                        ]
                    TK["bc2"][it] = S.step("tensor", _, after=[TK["rowcp"][it]])

                    def _(e, qn=qn, it=it):
                        x3 = xln1t[:].rearrange("p (c t) -> p c t", c=8)[:, :, 0:qn]
                        t3 = tq[:].rearrange("p (c t) -> p c t", c=8)[:, :, 0:qn]
                        q3 = xqt[:].rearrange("p (c t) -> p c t", c=8)[:, :, 0:qn]
                        ops = [
                            e.tensor_tensor(out=t3, in0=x3,
                                            in1=ps4[:, None, 256:256 + qn].broadcast_to([128, 8, qn]),
                                            op=ALU.mult),
                            e.drain(),
                            e.tensor_copy(out=tq32[:], in_=tq[:]),
                            e.drain(),
                            e.tensor_copy(out=tq[:], in_=tq32[:]),
                            e.drain(),
                            e.tensor_tensor(out=q3, in0=t3,
                                            in1=ps4[:, None, 384:384 + qn].broadcast_to([128, 8, qn]),
                                            op=ALU.mult),
                        ]
                        if it == 0 or it == n_tiles - 1:
                            col = 0 if it == 0 else qn - 1
                            midx = 0 if it == 0 else 1
                            qm = xqt[:].rearrange("p (c t) -> p c t", c=8)[:, :, col:col + 1]
                            ops.append(e.drain())
                            ops.append(e.tensor_tensor(
                                out=qm, in0=qm,
                                in1=qmask[:, None, midx:midx + 1].broadcast_to([128, 8, 1]),
                                op=ALU.mult))
                        return ops
                    TK["quant"][it] = S.step("vector", _,
                                             after=[TK["bc2"][it]] +
                                             ([TK["qsp"][it - 1]] if it else []))

                    @dma_step(8)
                    def _(e, sc=sc, qn=qn):
                        return [e.dma_start(out=xq_s[dc * 128:(dc + 1) * 128, sc:sc + qn],
                                            in_=xqt[:, dc * QT: dc * QT + qn])
                                for dc in range(8)]
                    TK["qsp"][it] = S.step("gpsimd", _, dma=True, after=[TK["quant"][it]])

                front(0)
                for it in range(1, n_tiles):
                    front(it)
                    tail(it - 1)
                tail(n_tiles - 1)
                S.last = TK["qsp"][n_tiles - 1]

            PHASE_MARKS.append(("A2", S.nsteps))

        S.auto_pump = None
        drain_wq()

        with contextlib.ExitStack() as fctx:
            def fsb(name, shape, dt):
                return fctx.enter_context(nc.sbuf_tensor(name, shape, dt))
            convd = fsb("convd", [128, 96 * 128], BF16)
            convb = fsb("convb", [128, 32], F32)
            xqb = fsb("xqb", [128, 8 * 520], BF16)
            w1buf = fsb("w1buf", [128, 3 * 8 * 512], BF16)
            w2buf = fsb("w2buf", [128, 2 * 4 * 512], BF16)
            h_bf = fsb("h_bf", [128, 32 * HBW], BF16)
            h_silu = fsb("h_silu", [128, 32 * FT], BF16)
            M2 = fsb("M2", [128, 512], F32)
            tbuf = fsb("tbuf", [128, 4 * FT], F32)
            t32 = fsb("t32", [128, 4 * FT], I32)
            d2sb = fsb("d2sb", [128, 512], F32)
            xl1b = fsb("xl1b", [128, 8 * FT], F32)
            outst = fsb("outst", [128, 8 * FT], F32)
            vbf2 = fsb("vbf2", [128, 8 * FT], BF16)

            @dma_step(1)
            def _(e):
                return e.dma_start(out=convb[:], in_=convb_h[:])
            S.step("sync", _, dma=True)

            for part in range(6):
                @dma_step(1)
                def _(e, part=part):
                    return e.dma_start(out=tbuf[:, 0:2048],
                                       in_=convd_h[:, part * 2048:(part + 1) * 2048])
                S.step("sync", _, dma=True)

                def _(e, part=part):
                    return e.copy(out=convd[:, part * 2048:(part + 1) * 2048], in_=tbuf[:, 0:2048])
                S.step("scalar", _)

            tok_ln2_prev = None
            tok_evh_prev = None
            tok_w1dma = {}
            tok_w2dma = {}
            for jt in range(4):
                ts0 = 1 + jt * FT
                fn = FT
                hn = fn + 2

                @dma_step(2)
                def _(e, ts0=ts0, hn=hn, fn=fn):
                    return [
                        e.dma_start(
                            out=xqb[:].rearrange("p (c t) -> p c t", c=8)[:, :, 0:hn],
                            in_=xq_s.rearrange("(c p) t -> p c t", p=128)[:, :, ts0 - 1: ts0 - 1 + hn]),
                        e.dma_start(
                            out=xl1b[:].rearrange("p (c t) -> p c t", c=8),
                            in_=xln1_s.rearrange("(c p) t -> p c t", p=128)[:, :, ts0:ts0 + fn]),
                    ]
                tok_xq = S.step("sync", _, dma=True, sem_key="fx",
                                after=([tok_evh_prev] if tok_evh_prev else None))

                mm1_tok = [None] * 32
                evac_tok = [None] * 32
                conv_tok = [None] * 16
                silu_tok = [None] * 16

                def emit_mm1(k):
                    g = k // 4
                    if (k % 4) == 0:
                        @dma_step(1)
                        def _(e, g=g):
                            return e.dma_start(
                                out=w1buf[:, (g % 3) * 4096:(g % 3) * 4096 + 4096]
                                .rearrange("p (c t) -> p c t", c=8),
                                in_=w1q_s.rearrange("(c p) t -> p c t", p=128)[:, :, g * 512:(g + 1) * 512])
                        dep = [tok_xq]
                        if g >= 3:
                            dep.append(mm1_tok[4 * (g - 3) + 3])
                        tok_w1dma[g] = S.step("sync", _, dma=True, after=dep,
                                              sem_key="fw%d" % (g % 3))

                    base = (k % 2) * 1024

                    def _(e, k=k, g=g, base=base):
                        hcl = k % 4
                        wb = (g % 3) * 4096
                        mms = []
                        for dc in range(8):
                            mms.append(e.matmul(
                                out=pC[:, base: base + 258],
                                lhsT=w1buf[:, wb + dc * 512 + hcl * 128: wb + dc * 512 + (hcl + 1) * 128],
                                rhs=xqb[:, dc * 520: dc * 520 + 258],
                                start=(dc == 0), stop=(dc == 7)))
                        for dc in range(8):
                            mms.append(e.matmul(
                                out=pC[:, base + 512: base + 512 + 256],
                                lhsT=w1buf[:, wb + dc * 512 + hcl * 128: wb + dc * 512 + (hcl + 1) * 128],
                                rhs=xqb[:, dc * 520 + 258: dc * 520 + 514],
                                start=(dc == 0), stop=(dc == 7)))
                        return mms
                    dep = [tok_w1dma[g], tok_xq]
                    if k >= 2:
                        dep.append(evac_tok[k - 2])
                    mm1_tok[k] = S.step("tensor", _, after=dep)

                    def _(e, k=k, base=base):
                        hc = k
                        return [
                            e.copy(out=h_bf[:, hc * HBW: hc * HBW + 258], in_=pC[:, base: base + 258]),
                            e.copy(out=h_bf[:, hc * HBW + 258: hc * HBW + 514],
                                   in_=pC[:, base + 512: base + 512 + 256]),
                        ]
                    evac_tok[k] = S.step("scalar", _, after=[mm1_tok[k]])

                def emit_conv(ck):
                    psa, psb = (pA, pB) if (ck % 2) == 0 else (ps3, ps4)

                    def _(e, ck=ck, fn=fn, psa=psa, psb=psb):
                        mms = []
                        for cl in range(2):
                            c = 2 * ck + cl
                            po = psa if cl == 0 else psb
                            for tap in range(3):
                                mms.append(e.matmul(
                                    out=po[:, 0:fn],
                                    lhsT=convd[:, (3 * c + tap) * 128:(3 * c + tap + 1) * 128],
                                    rhs=h_bf[:, c * HBW + tap: c * HBW + tap + fn],
                                    start=(tap == 0), stop=(tap == 2)))
                        return mms
                    dep = [evac_tok[2 * ck + 1]]
                    if ck >= 2:
                        dep.append(silu_tok[ck - 2])
                    if ck < 2 and tok_ln2_prev is not None:
                        dep.append(tok_ln2_prev)
                    conv_tok[ck] = S.step("tensor", _, after=dep)

                    def _(e, ck=ck, fn=fn, psa=psa, psb=psb):
                        return [e.activation(out=h_silu[:, (2 * ck + cl) * FT: (2 * ck + cl) * FT + fn],
                                             in_=(psa if cl == 0 else psb)[:, 0:fn],
                                             func=AF.Silu, bias=convb[:, 2 * ck + cl: 2 * ck + cl + 1],
                                             scale=1.0)
                                for cl in range(2)]
                    silu_tok[ck] = S.step("scalar", _, after=[conv_tok[ck]])

                for k in range(32):
                    emit_mm1(k)
                    if k >= 3 and k % 2 == 1:
                        emit_conv((k - 3) // 2)
                emit_conv(15)

                def _(e, fn=fn):
                    hin = h_silu[:].rearrange("p (c t) -> p t c", c=32)
                    return e.tensor_reduce(
                        out=M2[:].rearrange("p (t o) -> p t o", o=1),
                        in_=hin, axis=AX.X, op=ALU.max, apply_absolute_value=True)
                tok_red = S.step("vector", _, after=[silu_tok[15]])

                def _(e):
                    return [e.transpose(out=ps4[0:128, t * 128:(t + 1) * 128],
                                        in_=M2[:, t * 128:(t + 1) * 128],
                                        identity=ident[:])
                            for t in range(4)]
                tok_tr = S.step("tensor", _, after=[tok_red, silu_tok[15]])

                def _(e):
                    ops = []
                    for t in range(4):
                        ops.append(e.reduce_max(out=cols_t[0:128, 4 + t:5 + t],
                                                in_=ps4[0:128, t * 128:(t + 1) * 128], axis=AX.X))
                    ops += [
                        e.drain(),
                        e.tensor_scalar(out=cols_t[:, 8:12], in0=cols_t[:, 4:8],
                                        scalar1=1e-5, scalar2=None, op0=ALU.max),
                        e.drain(),
                        e.tensor_scalar(out=cols_t[:, 8:12], in0=cols_t[:, 8:12],
                                        scalar1=1.0 / 127.0, scalar2=None, op0=ALU.mult),
                        e.drain(),
                        e.reciprocal(out=cols_t[:, 12:16], in_=cols_t[:, 8:12]),
                        e.drain(),
                        e.tensor_tensor(out=cols_t[:, 4:8], in0=cols_t[:, 8:12],
                                        in1=mc2col[:, 0:1].broadcast_to([128, 4]),
                                        op=ALU.mult),
                    ]
                    return ops
                tok_cols = S.step("vector", _, after=[tok_tr])

                def _(e):
                    return [e.transpose(out=ps3[0:1, t * 128:(t + 1) * 128],
                                        in_=cols_t[:, 12 + t:13 + t], identity=ident[:])
                            for t in range(4)]
                tok_s2t = S.step("tensor", _, after=[tok_cols, silu_tok[15]])

                def _(e):
                    return e.tensor_copy(out=srcrow[0:1, 0:512], in_=ps3[0:1, 0:512])
                tok_s2c = S.step("vector", _, after=[tok_s2t])

                def _(e, fn=fn):
                    return e.matmul(out=ps3[:, 0:fn], lhsT=onesf[:], rhs=srcrow[:, 0:fn],
                                    start=True, stop=True)
                tok_s2b = S.step("tensor", _, after=[tok_s2c])

                def _(e):
                    return [e.transpose(out=ps4[0:1, t * 128:(t + 1) * 128],
                                        in_=cols_t[:, 4 + t:5 + t], identity=ident[:])
                            for t in range(4)]
                tok_d2t = S.step("tensor", _, after=[tok_cols])

                def _(e):
                    return e.tensor_copy(out=srcrow[0:1, 0:512], in_=ps4[0:1, 0:512])
                tok_d2c = S.step("vector", _, after=[tok_d2t, tok_s2b])

                def _(e, fn=fn):
                    return e.matmul(out=ps4[:, 0:fn], lhsT=onesf[:], rhs=srcrow[:, 0:fn],
                                    start=True, stop=True)
                tok_d2b = S.step("tensor", _, after=[tok_d2c])

                def _(e, fn=fn):
                    return e.tensor_copy(out=d2sb[:, 0:fn], in_=ps4[:, 0:fn])
                tok_d2s = S.step("vector", _, after=[tok_d2b])

                qv_tok = [None] * 8
                qs_tok = [None] * 8
                for grp in range(8):
                    def _(e, grp=grp, fn=fn):
                        h3 = h_silu[:].rearrange("p (c t) -> p c t", c=32)[:, 4 * grp:4 * grp + 4, :]
                        t3 = tbuf[:].rearrange("p (c t) -> p c t", c=4)
                        return [
                            e.tensor_tensor(out=t3, in0=h3,
                                            in1=ps3[:, None, 0:fn].broadcast_to([128, 4, fn]),
                                            op=ALU.mult),
                            e.drain(),
                            e.tensor_copy(out=t32[:], in_=tbuf[:]),
                        ]
                    dep = [tok_s2b]
                    if grp >= 1:
                        dep.append(qs_tok[grp - 1])
                    qv_tok[grp] = S.step("vector", _, after=dep)

                    def _(e, grp=grp, fn=fn):
                        h3 = h_bf[:].rearrange("p (c t) -> p c t", t=HBW)[:, 4 * grp:4 * grp + 4, 0:fn]
                        t3 = t32[:].rearrange("p (c t) -> p c t", c=4)
                        return e.copy(out=h3, in_=t3)
                    qs_tok[grp] = S.step("scalar", _,
                                         after=[qv_tok[grp], conv_tok[2 * grp + 1],
                                                mm2_prev_reader.get(grp, None)] if False else
                                         [qv_tok[grp], conv_tok[2 * grp + 1]])

                mm2_tok = {}
                evh_tok = {}
                for half in range(2):
                    for g in range(8):
                        @dma_step(1)
                        def _(e, g=g, half=half):
                            gi = half * 8 + g
                            return e.dma_start(
                                out=w2buf[:, (gi % 2) * 2048:(gi % 2) * 2048 + 2048]
                                .rearrange("p (c t) -> p c t", c=4),
                                in_=w2q_s.rearrange("(c p) t -> p c t", p=128)
                                [:, 4 * g: 4 * (g + 1), half * 512:(half + 1) * 512])
                        gi = half * 8 + g
                        dep = [tok_xq]
                        if gi >= 2:
                            dep.append(mm2_tok[gi - 2])
                        tok_w2dma[gi] = S.step("sync", _, dma=True, after=dep,
                                               sem_key="gw%d" % (gi % 2))

                        def _(e, g=g, half=half, fn=fn):
                            gi = half * 8 + g
                            wb = (gi % 2) * 2048
                            mms = []
                            for hcl in range(4):
                                hc = 4 * g + hcl
                                for ocl in range(4):
                                    mms.append(e.matmul(
                                        out=pC[:, ocl * 512: ocl * 512 + fn],
                                        lhsT=w2buf[:, wb + hcl * 512 + ocl * 128: wb + hcl * 512 + (ocl + 1) * 128],
                                        rhs=h_bf[:, hc * HBW: hc * HBW + fn],
                                        start=(hc == 0), stop=(hc == 31)))
                            return mms
                        dep = [tok_w2dma[gi], qs_tok[g]]
                        if half == 0 and g == 0:
                            dep.append(silu_tok[15])
                            dep.append(evac_tok[31])
                        if half == 1 and g == 0:
                            dep.append(evh_tok[0])
                        mm2_tok[gi] = S.step("tensor", _, after=dep)

                    def _(e, half=half, fn=fn):
                        p3 = pC[:].rearrange("p (c t) -> p c t", t=512)
                        o3 = outst[:].rearrange("p (c t) -> p c t", c=8)[:, half * 4:(half + 1) * 4, :]
                        x3 = xl1b[:].rearrange("p (c t) -> p c t", c=8)[:, half * 4:(half + 1) * 4, :]
                        return [
                            e.tensor_tensor(out=o3, in0=p3,
                                            in1=d2sb[:, None, 0:fn].broadcast_to([128, 4, fn]),
                                            op=ALU.mult),
                            e.drain(),
                            e.tensor_tensor(out=o3, in0=o3, in1=x3, op=ALU.add),
                        ]
                    evh_tok[half] = S.step("vector", _,
                                           after=[mm2_tok[half * 8 + 7], tok_d2s])

                tok_evh_prev = evh_tok[1]

                def _(e, fn=fn):
                    return e.tensor_copy(out=vbf2[:], in_=outst[:])
                S.step("vector", _, after=[evh_tok[1]])

                def _(e, fn=fn):
                    return [e.matmul(out=ps3[0:1, 0:fn], lhsT=onesb[:, 0:1],
                                     rhs=vbf2[:, dc * FT: dc * FT + fn],
                                     start=(dc == 0), stop=(dc == 7))
                            for dc in range(8)]
                S.step("tensor", _, extra_waits=[("vector", S.cnt["vector"])])

                def _(e, fn=fn):
                    return e.activation(out=vbf2[:], in_=outst[:], func=AF.Square)
                S.step("scalar", _, extra_waits=[("tensor", S.cnt["tensor"])])

                def _(e, fn=fn):
                    return [e.matmul(out=ps4[0:1, 0:fn], lhsT=onesb[:, 0:1],
                                     rhs=vbf2[:, dc * FT: dc * FT + fn],
                                     start=(dc == 0), stop=(dc == 7))
                            for dc in range(8)]
                S.step("tensor", _)

                def _(e, fn=fn):
                    return [
                        e.tensor_scalar(out=rows_t[0:1, 0:fn], in0=ps3[0:1, 0:fn],
                                        scalar1=1.0 / 1024.0, scalar2=None, op0=ALU.mult),
                        e.drain(),
                        e.tensor_tensor(out=rows_t[0:1, 512:512 + fn], in0=rows_t[0:1, 0:fn],
                                        in1=rows_t[0:1, 0:fn], op=ALU.mult),
                        e.drain(),
                        e.scalar_tensor_tensor(out=rows_t[0:1, 1024:1024 + fn],
                                               in0=ps4[0:1, 0:fn], scalar=1.0 / 1024.0,
                                               in1=rows_t[0:1, 512:512 + fn],
                                               op0=ALU.mult, op1=ALU.subtract),
                        e.drain(),
                        e.tensor_scalar(out=rows_t[0:1, 1024:1024 + fn],
                                        in0=rows_t[0:1, 1024:1024 + fn],
                                        scalar1=EPS_LN, scalar2=None, op0=ALU.add),
                        e.drain(),
                        e.reciprocal(out=rows_t[0:1, 1536:1536 + fn],
                                     in_=rows_t[0:1, 1024:1024 + fn]),
                    ]
                S.step("vector", _)

                def _(e, fn=fn):
                    return e.activation(out=srcrow[0:1, 0:fn], in_=rows_t[0:1, 1536:1536 + fn],
                                        func=AF.Sqrt)
                S.step("scalar", _)

                def _(e, fn=fn):
                    return e.matmul(out=ps4[:, 0:fn], lhsT=onesf[:], rhs=srcrow[:, 0:fn],
                                    start=True, stop=True)
                S.step("tensor", _)

                def _(e, fn=fn):
                    return e.tensor_tensor(out=srcrow[0:1, 0:fn], in0=srcrow[0:1, 0:fn],
                                           in1=rows_t[0:1, 0:fn], op=ALU.mult)
                S.step("vector", _)

                def _(e, fn=fn):
                    return e.matmul(out=ps3[:, 0:fn], lhsT=onesf[:], rhs=srcrow[:, 0:fn],
                                    start=True, stop=True)
                S.step("tensor", _)

                def _(e, fn=fn):
                    o3 = outst[:].rearrange("p (c t) -> p c t", c=8)
                    return [
                        e.tensor_tensor(out=o3, in0=o3,
                                        in1=ps4[:, None, 0:fn].broadcast_to([128, 8, fn]),
                                        op=ALU.mult),
                        e.drain(),
                        e.tensor_tensor(out=o3, in0=o3,
                                        in1=ps3[:, None, 0:fn].broadcast_to([128, 8, fn]),
                                        op=ALU.subtract),
                    ]
                tok_ln2_prev = S.step("vector", _)

                @dma_step(1)
                def _(e, ts0=ts0, fn=fn):
                    return e.dma_start(
                        out=outT_h.rearrange("(c p) t -> p c t", p=128)[:, :, ts0 - 1: ts0 - 1 + fn],
                        in_=outst[:].rearrange("p (c t) -> p c t", c=8))
                S.step("gpsimd", _, dma=True, after=[tok_ln2_prev])
                S.last = tok_ln2_prev

        PHASE_MARKS.append(("FFN", S.nsteps))

        final_g = S.cnt["gpsimd"]

        def _(e):
            e.wait_ge(sems["gpsimd"], final_g)
            return e.engine_nop()
        S.step("gpsimd", _)

        with nc.Block() as block:
            @block.sync
            def _(eng):
                for op in S.ops["sync"]:
                    op(eng)

            @block.vector
            def _(eng):
                for op in S.ops["vector"]:
                    op(eng)

            @block.scalar
            def _(eng):
                for op in S.ops["scalar"]:
                    op(eng)

            @block.tensor
            def _(eng):
                for op in S.ops["tensor"]:
                    op(eng)

            @block.gpsimd
            def _(eng):
                for op in S.ops["gpsimd"]:
                    op(eng)

    return nc


def _band_mask(off, qn):
    m = np.zeros((128, 1024), np.float32)
    for h in range(8):
        mm = np.zeros((128, 128), np.float32)
        i = np.arange(128)[:, None]
        j = np.arange(128)[None, :]
        band = (np.abs(i - j - off) <= 3) & (j < qn)
        mm[band] = 1.0
        m[:, h * 128:(h + 1) * 128] = mm
    return m


def kernel(L0_attn_x, L0_attn_w_qkv, L0_attn_proj, L0_attn_w_out, L0_attn_b_out,
           L0_ffn_w1, L0_ffn_conv_w, L0_ffn_conv_b, L0_ffn_w2):
    x = np.ascontiguousarray(np.asarray(L0_attn_x, dtype=np.float32))
    wqkv = np.ascontiguousarray(np.asarray(L0_attn_w_qkv, dtype=np.float32))
    proj = np.ascontiguousarray(np.asarray(L0_attn_proj, dtype=np.float32))
    wout = np.ascontiguousarray(np.asarray(L0_attn_w_out, dtype=np.float32))
    bout = np.ascontiguousarray(np.asarray(L0_attn_b_out, dtype=np.float32))[None, :]
    w1 = np.asarray(L0_ffn_w1, dtype=np.float32)
    convw = np.asarray(L0_ffn_conv_w, dtype=np.float32)[:, 0, :]
    convb = np.asarray(L0_ffn_conv_b, dtype=np.float32)
    w2 = np.asarray(L0_ffn_w2, dtype=np.float32)

    w1T = np.ascontiguousarray(w1.T)
    w2T = np.ascontiguousarray(w2.T)

    convdiag = np.zeros((128, 96 * 128), np.float32)
    idx = np.arange(128)
    for c in range(32):
        for tap in range(3):
            blk = (3 * c + tap) * 128
            convdiag[idx, blk + idx] = convw[c * 128:(c + 1) * 128, tap]
    convb_m = np.ascontiguousarray(convb.reshape(32, 128).T)

    ident = np.eye(128, dtype=np.float32)
    n_tiles = (NQ + QT - 1) // QT
    qs_last = QS0 + (n_tiles - 1) * QT
    qn_last = QS0 + NQ - qs_last
    off_last = qs_last - min(qs_last - 3, TB - 128)
    bandmask = _band_mask(3, QT)
    bandmask2 = _band_mask(off_last, qn_last)

    in_maps = []
    for core in range(NCORES):
        b, r = core // 4, core % 4
        t0 = r * TOK
        lo, hi = t0 - HB, t0 + TOK + HB
        xT = np.zeros((D, TB), np.float32)
        s, e = max(lo, 0), min(hi, N)
        xT[:, s - lo: s - lo + (e - s)] = x[b, s:e, :].T
        kmask = np.zeros((128, 16), np.float32)
        kmask[:, 0:8] = 1.0 if lo >= 0 else 0.0
        kmask[:, 8:16] = 1.0 if hi <= N else 0.0
        qmask = np.zeros((128, 2), np.float32)
        qmask[:, 0] = 1.0 if t0 - 1 >= 0 else 0.0
        qmask[:, 1] = 1.0 if t0 + TOK + 1 <= N else 0.0
        in_maps.append({
            "xT": xT, "wqkv": wqkv, "proj": proj, "wout": wout, "boutrow": bout,
            "w1T": w1T, "w2T": w2T, "convdiag": convdiag, "convbH": convb_m,
            "identH": ident, "bandmask": bandmask, "bandmask2": bandmask2,
            "kmaskH": kmask, "qmaskH": qmask,
        })

    nc = build_program()
    res = run_bass_kernel_spmd(nc, in_maps, core_ids=list(range(NCORES)))

    out = np.empty((B, N, D), np.float32)
    for core in range(NCORES):
        b, r = core // 4, core % 4
        t0 = r * TOK
        out[b, t0:t0 + TOK, :] = res.results[core]["outT"].T
    return out
